# version 02
# baseline (speedup 1.0000x reference)
import numpy as np
import ml_dtypes
import concourse.bass as bass
import concourse.mybir as mybir
from concourse.bass_utils import run_bass_kernel_spmd
from concourse.alu_op_type import AluOpType as OP

F32 = mybir.dt.float32
BF16 = mybir.dt.bfloat16
AF = mybir.ActivationFunctionType
AX = mybir.AxisListType
BF = ml_dtypes.bfloat16

S = 2048
Dm = 2048
H = 16
DH = 128
HFD = 5632
NT = S // 128
ND = Dm // 128
NHF = HFD // 128
NWELEM = float(HFD * Dm)
INV_SCALE = float(1.0 / np.sqrt(np.float32(128.0)))
NEG_CAUSAL = -100000.0
NEG_BAND = -1.0e30

STAGE = 4
USE_ACCUM_SUMS = True
PARANOID = False
PARANOID_ENGS = ("te", "dve", "act")
PARANOID_MODE = "full"
FFN_PART = 30

ENGS = ("te", "dve", "act", "sp", "pl")


class Plan:

    def __init__(self, cnt, known):
        self.ops = {k: [] for k in ENGS}
        self.cnt = cnt
        self.known = known

    def emit(self, eng, fn, waits=(), inc=None, incn=1):
        k = self.known[eng]
        if PARANOID and eng in PARANOID_ENGS:
            if PARANOID_MODE in ("full", "waits"):
                waits = list(waits) + [(s, self.cnt.get(s, 0))
                                       for s in ("s_te", "s_dve", "s_act")]
            if inc is None and PARANOID_MODE in ("full", "incs"):
                inc = {"te": "s_te", "dve": "s_dve", "act": "s_act"}[eng]
        real_waits = []
        for s, v in waits:
            if v is None or v <= 0:
                continue
            if k.get(s, 0) >= v:
                continue
            real_waits.append((s, v))
            k[s] = v
        self.ops[eng].append((fn, real_waits, inc, incn))
        if inc is not None:
            self.cnt[inc] = self.cnt.get(inc, 0) + incn
            if eng not in ("sp", "pl"):
                k[inc] = self.cnt[inc]
        return self.cnt.get(inc)

    def v(self, sem):
        return self.cnt.get(sem, 0)


def _replay(plan, sems, eng):
    drain_all = eng == "dve"
    drain_inc = eng in ("dve", "act")

    def run(e):
        for fn, waits, inc, incn in plan.ops[eng]:
            for s, v in waits:
                e.wait_ge(sems[s], v)
            ins = fn(e)
            if drain_all:
                d = e.drain()
                if inc is not None:
                    d.then_inc(sems[inc], incn)
            elif inc is not None:
                if drain_inc:
                    e.drain().then_inc(sems[inc], incn)
                else:
                    ins.then_inc(sems[inc], incn)
    return run


def _run_blocks(nc, plans, sems):
    for plan in plans:
        with nc.Block() as b:
            if plan.ops["sp"]:
                b.sync(_replay(plan, sems, "sp"))
            if plan.ops["pl"]:
                b.gpsimd(_replay(plan, sems, "pl"))
            if plan.ops["te"]:
                b.tensor(_replay(plan, sems, "te"))
            if plan.ops["dve"]:
                b.vector(_replay(plan, sems, "dve"))
            if plan.ops["act"]:
                b.scalar(_replay(plan, sems, "act"))


def build(stage, debug=False):
    nc = bass.Bass("TRN2", target_bir_lowering=False)
    dk = "ExternalOutput" if debug else "Internal"

    d_x = nc.dram_tensor("x", [S, Dm], F32, kind="ExternalInput")
    d_wqA = nc.dram_tensor("wqA", [H, Dm, DH], F32, kind="ExternalInput")
    d_wkA = nc.dram_tensor("wkA", [H, Dm, DH], F32, kind="ExternalInput")
    d_wvA = nc.dram_tensor("wvA", [H, Dm, DH], F32, kind="ExternalInput")
    d_woA = nc.dram_tensor("woA", [H, DH, Dm], F32, kind="ExternalInput")
    d_bq = nc.dram_tensor("bq", [H, DH], F32, kind="ExternalInput")
    d_bk = nc.dram_tensor("bk", [H, DH], F32, kind="ExternalInput")
    d_bv = nc.dram_tensor("bv", [H, DH], F32, kind="ExternalInput")
    d_bo = nc.dram_tensor("bo", [1, Dm], F32, kind="ExternalInput")
    d_wqB = nc.dram_tensor("wqB", [Dm, Dm], F32, kind="ExternalInput")
    d_wkB = nc.dram_tensor("wkB", [Dm, Dm], F32, kind="ExternalInput")
    d_wvB = nc.dram_tensor("wvB", [Dm, Dm], F32, kind="ExternalInput")
    d_woB = nc.dram_tensor("woB", [Dm, Dm], F32, kind="ExternalInput")
    d_w1 = nc.dram_tensor("w1", [HFD, Dm], F32, kind="ExternalInput")
    d_w2 = nc.dram_tensor("w2", [HFD, Dm], F32, kind="ExternalInput")
    d_w3 = nc.dram_tensor("w3", [Dm, HFD], F32, kind="ExternalInput")
    d_cI = nc.dram_tensor("cI", [128, 128], BF16, kind="ExternalInput")
    d_cShift = nc.dram_tensor("cShift", [128, 128], BF16, kind="ExternalInput")
    d_cTri = nc.dram_tensor("cTri", [128, 128], BF16, kind="ExternalInput")
    d_cuL = nc.dram_tensor("cuL", [1, 128], BF16, kind="ExternalInput")
    d_cvL = nc.dram_tensor("cvL", [1, 128], BF16, kind="ExternalInput")
    d_cuR = nc.dram_tensor("cuR", [1, 128], BF16, kind="ExternalInput")
    d_cvR = nc.dram_tensor("cvR", [1, 128], BF16, kind="ExternalInput")
    d_cOnes1 = nc.dram_tensor("cOnes1", [1, 128], BF16, kind="ExternalInput")
    d_cOnesC = nc.dram_tensor("cOnesC", [128, 1], F32, kind="ExternalInput")
    d_cOnesRF = nc.dram_tensor("cOnesRF", [1, 128], F32, kind="ExternalInput")

    d_y = nc.dram_tensor("y", [S, Dm], F32, kind="ExternalOutput")
    d_xres = nc.dram_tensor("xres", [S, Dm], F32, kind=dk)
    d_xln = nc.dram_tensor("xln", [S, Dm], BF16, kind=dk)
    d_z = nc.dram_tensor("zb", [NT, H, DH, 128], BF16, kind=dk)
    d_hid = nc.dram_tensor("hid", [NHF, 4, 128, 512], BF16, kind=dk)
    d_w1qc = nc.dram_tensor("w1qc", [HFD, Dm], BF16, kind=dk)
    d_w2qc = nc.dram_tensor("w2qc", [HFD, Dm], BF16, kind=dk)
    d_w3qc = nc.dram_tensor("w3qc", [Dm, HFD], BF16, kind=dk)
    d_dbg = None
    if debug:
        d_dbg = nc.dram_tensor("dbg", [NT, 128, 2048], BF16, kind="ExternalOutput")
        d_dbg2 = nc.dram_tensor("dbg2", [NT, 128, 8], F32, kind="ExternalOutput")

    A = nc.alloc_sbuf_tensor
    big1 = A("big1", [128, ND * S], BF16)
    big2 = A("big2", [128, H * Dm], BF16)
    wbuf = A("wbuf", [128, 12288], BF16)
    qkv = A("qkv", [128, 6144], BF16)
    xz = A("xz", [128, 8192], BF16)
    fscr = A("fscr", [128, 4096], BF16)
    anx = A("anx", [128, 3072], BF16)
    xlnst = A("xlnst", [128, 4096], BF16)
    I_sb = A("I_sb", [128, 128], BF16)
    shA = A("shA", [128, 128], BF16)
    triB = A("triB", [128, 128], BF16)
    uL = A("uL", [1, 128], BF16)
    vL = A("vL", [1, 128], BF16)
    uR = A("uR", [1, 128], BF16)
    vR = A("vR", [1, 128], BF16)
    ones1 = A("ones1", [1, 128], BF16)
    onesC = A("onesC", [128, 1], F32)
    onesRF = A("onesRF", [1, 128], F32)
    bnat = wbuf[0:16, 0:384]
    bosb = qkv[0:1, 4096:6144]
    bqkv = A("bqkv", [128, 48], F32)
    cvals = A("cvals", [128, 2], F32)
    stats = A("stats", [128, 64], F32)
    gacc = A("gacc", [128, 4], F32)
    thrbc = A("thrbc", [128, 8], F32)

    xrow = xz.ap().bitcast(F32)
    wf32 = wbuf.ap().bitcast(F32)
    fscr32 = fscr.ap().bitcast(F32)
    accv = big2.ap().bitcast(F32)
    gstage = xlnst.ap().bitcast(F32)

    PS_A = nc.alloc_psum_tensor("psA", [128, 2048], F32)
    PS_B = nc.alloc_psum_tensor("psB", [128, 2048], F32)

    sems = {}
    import contextlib
    _stack = contextlib.ExitStack()
    for name in ("s_te", "s_dve", "s_act", "s_sp", "s_pl", "s_st"):
        sems[name] = _stack.enter_context(nc.semaphore(name))

    cnt = {}
    known = {e: {} for e in ENGS}
    plans = []

    def NP():
        p = Plan(cnt, known)
        plans.append(p)
        return p

    def st(i):
        return stats[:, i:i + 1]

    NEGMAX, SUMS, RECIP, GTMP = 0, 1, 2, 3
    SUMX, SSQ, NEGM, MSQ, VR = 0, 1, 2, 3, 4

    def ln_chain(P, xr_ap, out_bf_ap, par, wait_first=(), war_apply=()):
        b = 8 + 8 * par

        def s(i):
            return stats[:, b + i:b + i + 1]
        P.emit("dve", lambda e, a=xr_ap, s0=s(SUMX): e.reduce_sum(s0, a, axis=AX.X),
               waits=wait_first)
        P.emit("act", lambda e, a=xr_ap, s1=s(SSQ): e.activation(
            fscr32[:, 0:2048], a, AF.Square, accum_out=s1),
            waits=list(wait_first) + list(ln_chain.act_war[par]), inc="s_act")
        a_sq = P.v("s_act")
        P.emit("dve", lambda e, s0=s(SUMX), s2=s(NEGM): e.tensor_scalar(
            s2, s0, -1.0 / Dm, None, OP.mult))
        P.emit("dve", lambda e, s2=s(NEGM), s3=s(MSQ): e.tensor_mul(s3, s2, s2),
               waits=[("s_act", a_sq)])
        P.emit("dve", lambda e, s1=s(SSQ), s3=s(MSQ), s4=s(VR): e.scalar_tensor_tensor(
            s4, s1, 1.0 / Dm, s3, OP.mult, OP.subtract),
            waits=list(ln_chain.var_war[par]), inc="s_dve")
        a_var = P.v("s_dve")
        P.emit("act", lambda e, s4=s(VR): e.activation(
            s4, s4, AF.Ln, bias=cvals[:, 0:1]), waits=[("s_dve", a_var)])
        P.emit("act", lambda e, s4=s(VR): e.activation(
            s4, s4, AF.Exp, scale=cvals[:, 1:2]), inc="s_act")
        a_rstd = P.v("s_act")
        P.emit("dve", lambda e, a=xr_ap, o=out_bf_ap, s2=s(NEGM), s4=s(VR): e.tensor_scalar(
            o, a, s2, s4, OP.add, OP.mult),
            waits=[("s_act", a_rstd)] + list(war_apply), inc="s_dve")
        dv = P.v("s_dve")
        ln_chain.act_war[par] = [("s_dve", dv)]
        ln_chain.var_war[par] = [("s_act", a_rstd)]
        return dv
    ln_chain.act_war = {0: [], 1: []}
    ln_chain.var_war = {0: [], 1: []}

    bst = {"xln_st": 0, "big1_te": 0, "wbuf_te": 0, "sp_xlnt": 0}

    def emit_xlnt_treads():
        P = plans[-1]
        w = [("s_st", bst["xln_st"]), ("s_te", bst["big1_te"])]
        for dc in range(ND):
            P.emit("sp", lambda e, dc=dc: e.dma_start_transpose(
                big1[:, dc * S:(dc + 1) * S], d_xln.ap()[:, dc * 128:(dc + 1) * 128]),
                waits=w if dc == 0 else (), inc="s_sp", incn=16)
        bst["sp_xlnt"] = P.v("s_sp")

    def gamma_chunk_list():
        out = []
        for mi, dt_ in ((0, d_w1), (1, d_w2)):
            for hf in range(NHF):
                out.append((mi, dt_.ap()[hf * 128:(hf + 1) * 128, :], 2048))
        for dc in range(ND):
            for c0, w in ((0, 2048), (2048, 2048), (4096, 1536)):
                out.append((2, d_w3.ap()[dc * 128:(dc + 1) * 128, c0:c0 + w], w))
        return out

    P = NP()
    for dst, src in ((I_sb, d_cI), (shA, d_cShift), (triB, d_cTri),
                     (uL, d_cuL), (vL, d_cvL), (uR, d_cuR), (vR, d_cvR),
                     (ones1, d_cOnes1), (onesC, d_cOnesC), (onesRF, d_cOnesRF)):
        P.emit("sp", lambda e, d=dst, s=src: e.dma_start(d[:], s.ap()),
               inc="s_sp", incn=16)
    for j, src in enumerate((d_bq, d_bk, d_bv)):
        P.emit("pl", lambda e, j=j, s=src: e.dma_start(
            bnat[:, j * 128:(j + 1) * 128], s.ap()), inc="s_pl", incn=16)
    pl_b = P.v("s_pl")
    sp_c = P.v("s_sp")
    P.emit("dve", lambda e: e.memset(gacc[:], 0.0))
    P.emit("dve", lambda e: e.memset(cvals[:, 0:1], 1e-5))
    P.emit("dve", lambda e: e.memset(cvals[:, 1:2], -0.5), inc="s_dve")
    for j in range(3):
        P.emit("te", lambda e, j=j: e.matmul(
            PS_B[:, j * 16:(j + 1) * 16], bnat[:, j * 128:(j + 1) * 128],
            I_sb[0:16, 0:16], start=True, stop=True),
            waits=[("s_pl", pl_b), ("s_sp", sp_c)],
            inc="s_te" if j == 2 else None)
    te_b = P.v("s_te")
    P.emit("dve", lambda e: e.tensor_copy(bqkv[:], PS_B[:, 0:48]),
           waits=[("s_te", te_b)], inc="s_dve")

    dve_apply = [0] * NT
    st_xln = [0] * NT
    for t in range(NT):
        par = t % 2
        xa = xrow[:, par * 2048:(par + 1) * 2048]
        wload = [("s_dve", dve_apply[t - 2])] if t >= 2 else []
        P.emit("sp", lambda e, t=t, a=xa: e.dma_start(
            a, d_x.ap()[t * 128:(t + 1) * 128, :]), waits=wload,
            inc="s_sp", incn=16)
        sp_t = P.v("s_sp")
        xo = xlnst[:, par * 2048:(par + 1) * 2048]
        war = [("s_st", st_xln[t - 2])] if t >= 2 else []
        dv = ln_chain(P, xa, xo, par, wait_first=[("s_sp", sp_t)], war_apply=war)
        dve_apply[t] = dv
        P.emit("pl", lambda e, t=t, o=xo: e.dma_start(
            d_xln.ap()[t * 128:(t + 1) * 128, :], o), waits=[("s_dve", dv)],
            inc="s_st", incn=16)
        st_xln[t] = P.v("s_st")
    bst["xln_st"] = P.v("s_st")
    bst["wbuf_te"] = te_b

    def attn_sublayer(variant, resid_src, out_dst, do_ln_next, do_gamma):
        emit_xlnt_treads()
        prevP = plans[-1]
        P = NP()
        sp_xlnt = bst["sp_xlnt"]

        gam = gamma_chunk_list() if do_gamma else []
        gam_i = [0]
        gam_dve = {0: 0, 1: 0}

        def emit_gamma(P):
            if not do_gamma or gam_i[0] >= len(gam):
                return
            i = gam_i[0]
            mi, gap, gw_ = gam[i]
            gdst = gstage[:, 0:gw_]
            P.emit("sp", lambda e, d=gdst, a=gap: e.dma_start(d, a),
                   waits=[("s_dve", gam_dve[0])], inc="s_sp", incn=16)
            sp_g = P.v("s_sp")
            P.emit("dve", lambda e, d=gdst: e.reduce_sum(
                st(GTMP), d, axis=AX.X, apply_absolute_value=True),
                waits=[("s_sp", sp_g)])
            P.emit("dve", lambda e, mi=mi: e.tensor_add(
                gacc[:, mi:mi + 1], gacc[:, mi:mi + 1], st(GTMP)), inc="s_dve")
            gam_dve[0] = P.v("s_dve")
            gam_i[0] += 1

        wo_pl = []
        wq_pl = []
        te_head_end = []
        dve_z_end = []
        st_z_end = []
        act_head_end = 0

        def w_cast(e, h, j):
            par = h % 2
            dst = wbuf[:, (par * 3 + j) * 2048:(par * 3 + j) * 2048 + 2048]
            if variant == "a":
                t = (d_wqA, d_wkA, d_wvA)[j]
                ap = t.ap()[h].rearrange("(dc p) e -> p dc e", p=128)
            else:
                t = (d_wqB, d_wkB, d_wvB)[j]
                ap = t.ap()[:, h * 128:(h + 1) * 128].rearrange(
                    "(dc p) e -> p dc e", p=128)
            return e.dma_start(dst.rearrange("p (dc e) -> p dc e", dc=ND), ap)

        def emit_wq(h, war, plan):
            for j in range(3):
                plan.emit("pl", lambda e, h=h, j=j: w_cast(e, h, j),
                          waits=war if j == 0 else (), inc="s_pl", incn=16)
            wq_pl.append(plan.v("s_pl"))

        def emit_wo(h, plan):
            if variant == "a":
                woap = d_woA.ap()[h]
            else:
                woap = d_woB.ap()[h * 128:(h + 1) * 128, :]
            plan.emit("pl", lambda e, h=h, a=woap: e.dma_start(
                big2[:, h * Dm:(h + 1) * Dm], a), inc="s_pl", incn=16)
            wo_pl.append(plan.v("s_pl"))

        def emit_wcasts(h, war):
            emit_wq(h, war, P)
            emit_wo(h, P)

        emit_wq(0, [("s_te", bst["wbuf_te"])], prevP)
        emit_wq(1, [], prevP)
        emit_wo(0, P)
        emit_wo(1, P)


        Ksb = qkv[:, 0:2048]
        Qsb = qkv[:, 2048:4096]
        Vsb = qkv[:, 4096:6144]
        vfm = fscr[:, 0:2048]

        for h in range(H):
            par = h % 2
            wq_s = wbuf[:, (par * 3 + 0) * 2048:(par * 3 + 0) * 2048 + 2048]
            wk_s = wbuf[:, (par * 3 + 1) * 2048:(par * 3 + 1) * 2048 + 2048]
            wv_s = wbuf[:, (par * 3 + 2) * 2048:(par * 3 + 2) * 2048 + 2048]

            wts = [("s_pl", wq_pl[h]), ("s_sp", sp_xlnt)]
            if h > 0:
                wts += [("s_act", act_head_end), ("s_dve", dve_z_end[h - 1])]
            for sti in range(4):
                for dc in range(ND):
                    last = (sti == 3 and dc == ND - 1)
                    P.emit("te", lambda e, sti=sti, dc=dc, w=wk_s: e.matmul(
                        PS_A[:, sti * 512:(sti + 1) * 512],
                        w[:, dc * 128:(dc + 1) * 128],
                        big1[:, dc * S + sti * 512: dc * S + (sti + 1) * 512],
                        start=(dc == 0), stop=(dc == ND - 1)),
                        waits=wts if (sti == 0 and dc == 0) else (),
                        inc="s_te" if last else None)
            te_k = P.v("s_te")
            if variant == "a":
                P.emit("dve", lambda e, h=h: e.tensor_scalar(
                    Ksb, PS_A[:, :], bqkv[:, 16 + h:16 + h + 1], None, OP.add),
                    waits=[("s_te", te_k)], inc="s_dve")
            else:
                P.emit("dve", lambda e: e.tensor_copy(Ksb, PS_A[:, :]),
                       waits=[("s_te", te_k)], inc="s_dve")
            dve_k = P.v("s_dve")
            emit_gamma(P)
            qwts = [("s_dve", dve_z_end[h - 1])] if h > 0 else []
            for sti in range(4):
                for dc in range(ND):
                    last = (sti == 3 and dc == ND - 1)
                    P.emit("te", lambda e, sti=sti, dc=dc, w=wq_s: e.matmul(
                        PS_B[:, sti * 512:(sti + 1) * 512],
                        w[:, dc * 128:(dc + 1) * 128],
                        big1[:, dc * S + sti * 512: dc * S + (sti + 1) * 512],
                        start=(dc == 0), stop=(dc == ND - 1)),
                        waits=qwts if (sti == 0 and dc == 0) else (),
                        inc="s_te" if last else None)
            te_q = P.v("s_te")
            if variant == "a":
                P.emit("dve", lambda e, h=h: e.tensor_scalar(
                    Qsb, PS_B[:, :], bqkv[:, h:h + 1], INV_SCALE, OP.add, OP.mult),
                    waits=[("s_te", te_q)], inc="s_dve")
            else:
                P.emit("dve", lambda e: e.tensor_scalar(
                    Qsb, PS_B[:, :], INV_SCALE, None, OP.mult),
                    waits=[("s_te", te_q)], inc="s_dve")
            dve_q = P.v("s_dve")
            emit_gamma(P)
            for sti in range(4):
                for dc in range(ND):
                    last = (sti == 3 and dc == ND - 1)
                    P.emit("te", lambda e, sti=sti, dc=dc, w=wv_s: e.matmul(
                        PS_A[:, sti * 512:(sti + 1) * 512],
                        w[:, dc * 128:(dc + 1) * 128],
                        big1[:, dc * S + sti * 512: dc * S + (sti + 1) * 512],
                        start=(dc == 0), stop=(dc == ND - 1)),
                        waits=[("s_dve", dve_k)] if (sti == 0 and dc == 0) else (),
                        inc="s_te" if last else None)
            te_v = P.v("s_te")
            P.emit("dve", lambda e: e.tensor_copy(vfm, PS_A[:, :]),
                   waits=[("s_te", te_v)], inc="s_dve")
            dve_vf = P.v("s_dve")
            for tt in range(NT):
                P.emit("te", lambda e, tt=tt: e.matmul(
                    PS_B[:, tt * 128:(tt + 1) * 128],
                    vfm[:, tt * 128:(tt + 1) * 128], I_sb[:],
                    start=True, stop=True),
                    waits=[("s_dve", dve_q), ("s_dve", dve_vf)] if tt == 0 else (),
                    inc="s_te" if tt == NT - 1 else None)
            te_vt = P.v("s_te")
            P.emit("dve", lambda e: e.tensor_copy(Vsb, PS_B[:, :]),
                   waits=[("s_te", te_vt)], inc="s_dve")
            dve_vt = P.v("s_dve")

            te_attnT_done = 0
            act_exp_prev = 0
            for qt in range(NT):
                if variant == "a":
                    kts = list(range(qt + 1))
                else:
                    kts = [k for k in (qt - 1, qt, qt + 1) if 0 <= k < NT]
                kw = len(kts) * 128
                scw = [("s_act", act_exp_prev)]
                if qt == 0:
                    scw.append(("s_dve", dve_vt))
                if variant == "a":
                    nfull = (qt * 128) // 512
                    for j in range(nfull):
                        P.emit("te", lambda e, qt=qt, j=j: e.matmul(
                            PS_A[:, j * 512:(j + 1) * 512],
                            Qsb[:, qt * 128:(qt + 1) * 128],
                            Ksb[:, j * 512:(j + 1) * 512],
                            start=True, stop=True),
                            waits=scw if j == 0 else ())
                    rem0 = nfull * 512
                    remw = kw - rem0
                    P.emit("te", lambda e, qt=qt, rem0=rem0, remw=remw: e.matmul(
                        PS_A[:, rem0:rem0 + remw],
                        Qsb[:, qt * 128:(qt + 1) * 128],
                        Ksb[:, rem0:rem0 + remw],
                        start=True, stop=True),
                        waits=scw if nfull == 0 else ())
                    P.emit("te", lambda e, qt=qt: e.matmul(
                        PS_A[:, qt * 128:(qt + 1) * 128], shA[:], triB[:],
                        start=False, stop=True, skip_group_check=True),
                        inc="s_te")
                else:
                    k0 = kts[0] * 128
                    P.emit("te", lambda e, qt=qt, k0=k0, kw=kw: e.matmul(
                        PS_A[:, 0:kw],
                        Qsb[:, qt * 128:(qt + 1) * 128], Ksb[:, k0:k0 + kw],
                        start=True, stop=True), waits=scw)
                    nmask = sum(1 for kt in kts if kt in (qt - 1, qt + 1))
                    mdone = 0
                    for i, kt in enumerate(kts):
                        if kt == qt - 1:
                            uu, vv = uL, vL
                        elif kt == qt + 1:
                            uu, vv = uR, vR
                        else:
                            continue
                        mdone += 1
                        P.emit("te", lambda e, i=i, uu=uu, vv=vv: e.matmul(
                            PS_A[:, i * 128:(i + 1) * 128], uu[:], vv[:],
                            start=False, stop=True, skip_group_check=True),
                            inc="s_te" if mdone == nmask else None)
                te_sc = P.v("s_te")
                P.emit("dve", lambda e, kw=kw: e.reduce_max(
                    st(NEGMAX), PS_A[:, 0:kw], axis=AX.X, negate=True),
                    waits=[("s_te", te_sc)], inc="s_dve")
                dve_nm = P.v("s_dve")
                P.emit("act", lambda e, kw=kw: e.activation(
                    anx[:, 0:kw], PS_A[:, 0:kw], AF.Exp,
                    bias=st(NEGMAX), accum_out=st(SUMS)),
                    waits=[("s_dve", dve_nm), ("s_te", te_attnT_done)],
                    inc="s_act")
                act_exp = P.v("s_act")
                act_exp_prev = act_exp
                if USE_ACCUM_SUMS:
                    P.emit("dve", lambda e: e.reciprocal(st(RECIP), st(SUMS)),
                           waits=[("s_act", act_exp)])
                else:
                    P.emit("dve", lambda e, kw=kw: e.reduce_sum(
                        st(SUMS), anx[:, 0:kw], axis=AX.X),
                        waits=[("s_act", act_exp)])
                    P.emit("dve", lambda e: e.reciprocal(st(RECIP), st(SUMS)))
                P.emit("dve", lambda e: e.tensor_scalar(
                    anx[:, 2560:2688], I_sb[:], st(RECIP), None, OP.mult),
                    inc="s_dve")
                dve_diag = P.v("s_dve")
                zslot = 1792 + (qt % 2) * 128
                n_k = len(kts)
                dve_attnT = 0
                for g0 in range(0, n_k, 4):
                    gkts = kts[g0:g0 + 4]
                    ng = len(gkts)
                    for i in range(ng):
                        P.emit("te", lambda e, i=i, g0=g0: e.matmul(
                            PS_B[:, 1024 + i * 128:1024 + (i + 1) * 128],
                            anx[:, (g0 + i) * 128:(g0 + i + 1) * 128],
                            anx[:, 2560:2688], start=True, stop=True),
                            waits=([("s_dve", dve_diag), ("s_dve", dve_attnT)]
                                   if i == 0 else ()),
                            inc="s_te" if i == ng - 1 else None)
                    te_at = P.v("s_te")
                    P.emit("dve", lambda e, gw=ng * 128: e.tensor_copy(
                        anx[:, 2048:2048 + gw], PS_B[:, 1024:1024 + gw]),
                        waits=[("s_te", te_at)], inc="s_dve")
                    dve_attnT = P.v("s_dve")
                    for i, kt in enumerate(gkts):
                        glast = (g0 + i == n_k - 1)
                        gfirst = (g0 == 0 and i == 0)
                        P.emit("te", lambda e, i=i, kt=kt, zs=zslot, st_=gfirst, sp_=glast: e.matmul(
                            PS_B[:, zs:zs + 128],
                            Vsb[:, kt * 128:(kt + 1) * 128],
                            anx[:, 2048 + i * 128:2048 + (i + 1) * 128],
                            start=st_, stop=sp_),
                            waits=([("s_dve", dve_attnT)] if i == 0 else ()),
                            inc="s_te" if glast else None)
                te_z = P.v("s_te")
                te_attnT_done = te_z
                if debug and h == 0:
                    P.emit("pl", lambda e, qt=qt, kw=kw: e.dma_start(
                        d_dbg.ap()[qt][:, 0:kw], anx[:, 0:kw]),
                        waits=[("s_dve", P.v("s_dve"))], inc="s_st", incn=16)
                    P.emit("pl", lambda e, qt=qt: e.dma_start(
                        d_dbg2.ap()[qt][:, 0:4], stats[:, 0:4]),
                        inc="s_st", incn=16)
                zdst = xz[:, (h % 2) * 2048 + qt * 128:(h % 2) * 2048 + (qt + 1) * 128]
                zwar = [("s_st", st_z_end[h - 2])] if h >= 2 else []
                if variant == "a":
                    P.emit("dve", lambda e, h=h, zd=zdst, zs=zslot: e.tensor_scalar(
                        zd, PS_B[:, zs:zs + 128], bqkv[:, 32 + h:32 + h + 1],
                        None, OP.add),
                        waits=[("s_te", te_z)] + zwar, inc="s_dve")
                else:
                    P.emit("dve", lambda e, zd=zdst, zs=zslot: e.tensor_copy(
                        zd, PS_B[:, zs:zs + 128]),
                        waits=[("s_te", te_z)] + zwar, inc="s_dve")
                dve_zev = P.v("s_dve")
                P.emit("pl", lambda e, qt=qt, h=h, zd=zdst: e.dma_start(
                    d_z.ap()[qt, h], zd), waits=[("s_dve", dve_zev)],
                    inc="s_st", incn=16)
                if (h * NT + qt) % 2 == 0:
                    emit_gamma(P)
            act_head_end = P.v("s_act")
            dve_z_end.append(P.v("s_dve"))
            st_z_end.append(P.v("s_st"))
            te_head_end.append(P.v("s_te"))
            if h + 2 < H:
                emit_wcasts(h + 2, [("s_te", te_head_end[h])])

        while do_gamma and gam_i[0] < len(gam):
            emit_gamma(P)
        if do_gamma:
            dve_g = P.v("s_dve")
            P.emit("te", lambda e: e.matmul(
                PS_B[0:1, 0:3], onesC[:], gacc[:, 0:3], start=True, stop=True),
                waits=[("s_dve", dve_g), ("s_dve", dve_z_end[-1])], inc="s_te")
            te_g = P.v("s_te")
            P.emit("dve", lambda e: e.tensor_scalar(
                stats[0:1, 32:35], PS_B[0:1, 0:3], 0.5 / NWELEM, 0.5e-8,
                OP.mult, OP.add), waits=[("s_te", te_g)], inc="s_dve")
            dve_t = P.v("s_dve")
            P.emit("te", lambda e: e.matmul(
                PS_B[:, 16:19], onesRF[:], stats[0:1, 32:35],
                start=True, stop=True), waits=[("s_dve", dve_t)], inc="s_te")
            te_t = P.v("s_te")
            P.emit("dve", lambda e: e.tensor_copy(thrbc[:, 0:3], PS_B[:, 16:19]),
                   waits=[("s_te", te_t)])
            P.emit("dve", lambda e: e.tensor_scalar(
                thrbc[:, 4:7], thrbc[:, 0:3], -1.0, None, OP.mult), inc="s_dve")
        bst["big1_te"] = te_head_end[-1]
        bst["wbuf_te"] = te_head_end[-1]

        P = NP()
        pl_bo = 0
        if variant == "a":
            P.emit("pl", lambda e: e.dma_start(bosb, d_bo.ap()),
                   inc="s_pl", incn=16)
            pl_bo = P.v("s_pl")
        te_end = [0] * NT
        dve_add_l = [0] * NT
        dve_ln_l = [0] * NT
        st_q = [0] * NT
        for qt in range(NT):
            par = qt % 2
            zt = qkv[:, par * 2048:(par + 1) * 2048]
            xa = xrow[:, par * 2048:(par + 1) * 2048]
            wz = [("s_te", te_end[qt - 2])] if qt >= 2 else []
            P.emit("sp", lambda e, qt=qt, d=zt: e.dma_start(
                d.rearrange("p (hh e) -> p hh e", hh=H),
                d_z.ap()[qt].rearrange("hh p e -> p hh e")),
                waits=wz, inc="s_sp", incn=16)
            wx = []
            if qt >= 2:
                wx = [("s_st", st_q[qt - 2]), ("s_dve", dve_ln_l[qt - 2])]
            P.emit("sp", lambda e, qt=qt, d=xa: e.dma_start(
                d, resid_src.ap()[qt * 128:(qt + 1) * 128, :]),
                waits=wx, inc="s_sp", incn=16)
            sp_q = P.v("s_sp")
            PS = PS_A if par == 0 else PS_B
            wts0 = [("s_sp", sp_q), ("s_pl", wo_pl[-1])]
            if qt >= 2:
                wts0.append(("s_dve", dve_add_l[qt - 2]))
            lastb = (variant != "a")
            for hh in range(H):
                for j in range(4):
                    last = (hh == H - 1 and j == 3)
                    P.emit("te", lambda e, hh=hh, j=j, z=zt, PS=PS: e.matmul(
                        PS[:, j * 512:(j + 1) * 512],
                        z[:, hh * 128:(hh + 1) * 128],
                        big2[:, hh * Dm + j * 512: hh * Dm + (j + 1) * 512],
                        start=(hh == 0), stop=(hh == H - 1 and lastb)),
                        waits=wts0 if (hh == 0 and j == 0) else (),
                        inc="s_te" if (last and lastb) else None)
            if variant == "a":
                for j in range(4):
                    P.emit("te", lambda e, j=j, PS=PS: e.matmul(
                        PS[:, j * 512:(j + 1) * 512], ones1[:],
                        bosb[0:1, j * 512:(j + 1) * 512],
                        start=False, stop=True, skip_group_check=True),
                        waits=[("s_pl", pl_bo)] if j == 0 else (),
                        inc="s_te" if j == 3 else None)
            te_end[qt] = P.v("s_te")
            P.emit("dve", lambda e, xa=xa, PS=PS: e.tensor_add(xa, xa, PS[:, :]),
                   waits=[("s_te", te_end[qt])], inc="s_dve")
            dve_add_l[qt] = P.v("s_dve")
            P.emit("pl", lambda e, qt=qt, xa=xa, dst=out_dst: e.dma_start(
                dst.ap()[qt * 128:(qt + 1) * 128, :], xa),
                waits=[("s_dve", dve_add_l[qt])], inc="s_st", incn=16)
            if do_ln_next:
                xo = xlnst[:, par * 2048:(par + 1) * 2048]
                war = [("s_st", st_q[qt - 2])] if qt >= 2 else []
                dv = ln_chain(P, xa, xo, par,
                              wait_first=[("s_dve", dve_add_l[qt])],
                              war_apply=war)
                dve_ln_l[qt] = dv
                P.emit("pl", lambda e, qt=qt, o=xo: e.dma_start(
                    d_xln.ap()[qt * 128:(qt + 1) * 128, :], o),
                    waits=[("s_dve", dv)], inc="s_st", incn=16)
            else:
                dve_ln_l[qt] = dve_add_l[qt]
            st_q[qt] = P.v("s_st")
        bst["xln_st"] = P.v("s_st")

    def ffn_sublayer(first, resid_src, out_dst, do_ln_next):
        SUP = 4
        nsup = NHF // SUP
        emit_xlnt_treads()
        P = NP()
        sp_xlnt = bst["sp_xlnt"]

        qstate = {"par": 0, "dq": {0: 0, 1: 0}, "last_st": 0}

        def quantize_chunk(P, src_ap, dst_ap, mi, width):
            par_f = qstate["par"]
            qstate["par"] ^= 1
            fa = wf32[:, par_f * 2048:par_f * 2048 + width]
            P.emit("sp", lambda e, d=fa, a=src_ap: e.dma_start(d, a),
                   waits=[("s_dve", qstate["dq"][par_f])], inc="s_sp", incn=16)
            sp_w = P.v("s_sp")
            pbuf = fscr[:, 2048:2048 + width]
            nbuf = xlnst[:, 0:width]
            qbuf = qkv[:, 4096:4096 + width]
            P.emit("dve", lambda e, f=fa, p=pbuf, mi=mi: e.tensor_single_scalar(
                p, f, thrbc[:, mi:mi + 1], OP.is_gt),
                waits=[("s_sp", sp_w)])
            P.emit("dve", lambda e, f=fa, n=nbuf, mi=mi: e.tensor_single_scalar(
                n, f, thrbc[:, 4 + mi:4 + mi + 1], OP.is_lt))
            P.emit("dve", lambda e, p=pbuf, n=nbuf, q=qbuf: e.tensor_sub(q, p, n),
                   waits=[("s_st", qstate["last_st"])], inc="s_dve")
            dq = P.v("s_dve")
            qstate["dq"][par_f] = dq
            P.emit("pl", lambda e, d=dst_ap, q=qbuf: e.dma_start(d, q),
                   waits=[("s_dve", dq)], inc="s_st", incn=16)
            qstate["last_st"] = P.v("s_st")
            return qstate["last_st"]

        w3chunks = []
        for dc in range(ND):
            for c0, w in ((0, 2048), (2048, 2048), (4096, 1536)):
                w3chunks.append((dc, c0, w))
        w3_i = [0]

        def emit_w3_quant(P):
            if not first or w3_i[0] >= len(w3chunks):
                return
            dc, c0, w = w3chunks[w3_i[0]]
            quantize_chunk(P, d_w3.ap()[dc * 128:(dc + 1) * 128, c0:c0 + w],
                           d_w3qc.ap()[dc * 128:(dc + 1) * 128, c0:c0 + w], 2, w)
            w3_i[0] += 1

        st_after_q = [0] * nsup

        def quantize_super(s_):
            for hf in range(s_ * SUP, (s_ + 1) * SUP):
                quantize_chunk(P, d_w1.ap()[hf * 128:(hf + 1) * 128, :],
                               d_w1qc.ap()[hf * 128:(hf + 1) * 128, :], 0, 2048)
                quantize_chunk(P, d_w2.ap()[hf * 128:(hf + 1) * 128, :],
                               d_w2qc.ap()[hf * 128:(hf + 1) * 128, :], 1, 2048)
            st_after_q[s_] = P.v("s_st")

        if first and FFN_PART >= 1:
            quantize_super(0)

        sp_tread = [0] * nsup
        te_super_done = [0] * nsup
        act_sig = 0
        dve_hid = 0
        st_hid = [0] * NHF

        for s_ in range(nsup if FFN_PART >= 2 else 0):
            war = [("s_te", te_super_done[s_ - 2])] if s_ >= 2 else []
            if first:
                war.append(("s_st", st_after_q[s_]))
            par = s_ % 2
            base = par * 16384
            for mi, qc in ((0, d_w1qc), (1, d_w2qc)):
                for dc in range(ND):
                    P.emit("sp", lambda e, qc=qc, s_=s_, dc=dc, mi=mi, base=base: e.dma_start_transpose(
                        big2[:, base + mi * 8192 + dc * 512: base + mi * 8192 + dc * 512 + 512],
                        qc.ap()[s_ * 512:(s_ + 1) * 512, dc * 128:(dc + 1) * 128]),
                        waits=war if (mi == 0 and dc == 0) else (),
                        inc="s_sp", incn=16)
            sp_tread[s_] = P.v("s_sp")
            if first and FFN_PART >= 1 and s_ + 1 < nsup:
                quantize_super(s_ + 1)
            for ii in range(SUP):
                hf = s_ * SUP + ii
                for mi, PS in ((0, PS_A), (1, PS_B)):
                    wts = [("s_sp", sp_tread[s_]), ("s_sp", sp_xlnt)]
                    if hf > 0:
                        wts.append(("s_act", act_sig) if mi == 0 else
                                   ("s_dve", dve_hid))
                    for sti in range(4):
                        for dc in range(ND):
                            last = (sti == 3 and dc == ND - 1)
                            P.emit("te", lambda e, mi=mi, PS=PS, sti=sti, dc=dc, base=base, ii=ii: e.matmul(
                                PS[:, sti * 512:(sti + 1) * 512],
                                big2[:, base + mi * 8192 + dc * 512 + ii * 128: base + mi * 8192 + dc * 512 + ii * 128 + 128],
                                big1[:, dc * S + sti * 512: dc * S + (sti + 1) * 512],
                                start=(dc == 0), stop=(dc == ND - 1)),
                                waits=wts if (sti == 0 and dc == 0) else (),
                                inc="s_te" if last else None)
                te_hf = P.v("s_te")
                s1 = qkv[:, 0:2048]
                hidb = qkv[:, 2048:4096]
                P.emit("act", lambda e: e.activation(s1, PS_A[:, :], AF.Sigmoid),
                       waits=[("s_te", te_hf - 1), ("s_dve", dve_hid)],
                       inc="s_act")
                act_sig = P.v("s_act")
                hwar = [("s_st", st_hid[hf - 1])] if hf >= 1 else []
                P.emit("dve", lambda e: e.tensor_mul(hidb, s1, PS_B[:, :]),
                       waits=[("s_act", act_sig), ("s_te", te_hf)] + hwar,
                       inc="s_dve")
                dve_hid = P.v("s_dve")
                P.emit("pl", lambda e, hf=hf: e.dma_start(
                    d_hid.ap()[hf].rearrange("st p f -> p st f"),
                    hidb.rearrange("p (st f) -> p st f", st=4)),
                    waits=[("s_dve", dve_hid)], inc="s_st", incn=16)
                st_hid[hf] = P.v("s_st")
                emit_w3_quant(P)
            te_super_done[s_] = P.v("s_te")
        while first and FFN_PART >= 1 and w3_i[0] < len(w3chunks):
            emit_w3_quant(P)
        st_w3_done = P.v("s_st")
        st_hid_done = st_hid[-1]

        GRP = 6
        ngrp = (NHF + GRP - 1) // GRP
        for half in range(2 if FFN_PART >= 21 else 0):
            P = NP()
            te_grp_done = [0] * ngrp
            dve_acc = [0] * 8
            last_ps = {0: 0, 1: 0}
            for g in range(ngrp):
                lo = g * GRP
                hi = min(NHF, lo + GRP)
                n = hi - lo
                gpar = g % 2
                war = [("s_te", te_grp_done[g - 2])] if g >= 2 else []
                if first and half == 0 and g == 0:
                    war.append(("s_st", st_w3_done))
                for i in range(n):
                    hf = lo + i
                    P.emit("sp", lambda e, hf=hf, i=i, gpar=gpar: e.dma_start_transpose(
                        big1[:, gpar * 12288 + i * 2048: gpar * 12288 + (i + 1) * 2048],
                        d_w3qc.ap()[:, hf * 128:(hf + 1) * 128]),
                        waits=war if i == 0 else (), inc="s_sp", incn=16)
                for i in range(n):
                    hf = lo + i
                    P.emit("sp", lambda e, hf=hf, i=i, gpar=gpar, half=half: e.dma_start(
                        wbuf[:, gpar * 6144 + i * 1024: gpar * 6144 + (i + 1) * 1024].rearrange("p (st f) -> p st f", st=2),
                        d_hid.ap()[hf][half * 2:half * 2 + 2].rearrange("st p f -> p st f")),
                        waits=([("s_st", st_hid_done)]
                               if (g == 0 and half == 0 and i == 0) else ()),
                        inc="s_sp", incn=16)
                sp_g = P.v("s_sp")
                for sc in range(8 if FFN_PART >= 25 else 0):
                    PS = PS_A if sc % 2 == 0 else PS_B
                    for i in range(n):
                        for j in range(4):
                            last = (i == n - 1 and j == 3)
                            P.emit("te", lambda e, i=i, j=j, sc=sc, gpar=gpar, PS=PS, n=n: e.matmul(
                                PS[:, j * 512:(j + 1) * 512],
                                wbuf[:, gpar * 6144 + i * 1024 + sc * 128: gpar * 6144 + i * 1024 + sc * 128 + 128],
                                big1[:, gpar * 12288 + i * 2048 + j * 512: gpar * 12288 + i * 2048 + (j + 1) * 512],
                                start=(i == 0), stop=(i == n - 1)),
                                waits=([("s_sp", sp_g), ("s_dve", last_ps[sc % 2])]
                                       if (i == 0 and j == 0) else ()),
                                inc="s_te" if last else None)
                    te_sc = P.v("s_te")
                    acc = accv[:, sc * 2048:(sc + 1) * 2048]
                    if g == 0:
                        P.emit("dve", lambda e, acc=acc, PS=PS: e.tensor_copy(
                            acc, PS[:, :]), waits=[("s_te", te_sc)], inc="s_dve")
                    else:
                        P.emit("dve", lambda e, acc=acc, PS=PS: e.tensor_add(
                            acc, acc, PS[:, :]), waits=[("s_te", te_sc)], inc="s_dve")
                    dve_acc[sc] = P.v("s_dve")
                    last_ps[sc % 2] = dve_acc[sc]
                te_grp_done[g] = P.v("s_te")
            st_sc = [0] * 8
            dve_ln_sc = [0] * 8
            for sc in range(8 if FFN_PART >= 30 else 0):
                t = half * 8 + sc
                par = sc % 2
                xa = xrow[:, par * 2048:(par + 1) * 2048]
                wx = []
                if sc >= 2:
                    wx = [("s_st", st_sc[sc - 2]), ("s_dve", dve_ln_sc[sc - 2])]
                P.emit("sp", lambda e, t=t, d=xa: e.dma_start(
                    d, resid_src.ap()[t * 128:(t + 1) * 128, :]),
                    waits=wx, inc="s_sp", incn=16)
                sp_x = P.v("s_sp")
                acc = accv[:, sc * 2048:(sc + 1) * 2048]
                P.emit("dve", lambda e, xa=xa, acc=acc: e.tensor_add(xa, xa, acc),
                       waits=[("s_sp", sp_x), ("s_dve", dve_acc[sc])], inc="s_dve")
                dve_add = P.v("s_dve")
                P.emit("pl", lambda e, t=t, xa=xa, dst=out_dst: e.dma_start(
                    dst.ap()[t * 128:(t + 1) * 128, :], xa),
                    waits=[("s_dve", dve_add)], inc="s_st", incn=16)
                if do_ln_next:
                    xo = xlnst[:, par * 2048:(par + 1) * 2048]
                    war = [("s_st", st_sc[sc - 2])] if sc >= 2 else []
                    dv = ln_chain(P, xa, xo, par,
                                  wait_first=[("s_dve", dve_add)],
                                  war_apply=war)
                    dve_ln_sc[sc] = dv
                    P.emit("pl", lambda e, t=t, o=xo: e.dma_start(
                        d_xln.ap()[t * 128:(t + 1) * 128, :], o),
                        waits=[("s_dve", dv)], inc="s_st", incn=16)
                else:
                    dve_ln_sc[sc] = dve_add
                st_sc[sc] = P.v("s_st")
            if half == 1:
                bst["xln_st"] = P.v("s_st")
                bst["big1_te"] = te_grp_done[ngrp - 1]
                bst["wbuf_te"] = te_grp_done[ngrp - 1]

    attn_sublayer("a", d_x, d_xres if stage > 1 else d_y,
                  do_ln_next=(stage > 1), do_gamma=(stage > 1))
    if stage > 1:
        ffn_sublayer(True, d_xres, d_xres if stage > 2 else d_y,
                     do_ln_next=(stage > 2))
    if stage > 2:
        attn_sublayer("b", d_xres, d_xres if stage > 3 else d_y,
                      do_ln_next=(stage > 3), do_gamma=False)
    if stage > 3:
        ffn_sublayer(False, d_xres, d_y, do_ln_next=False)

    P = NP()
    total_st = cnt.get("s_st", 0)
    P.emit("pl", lambda e: e.engine_nop(), waits=[("s_st", total_st)])

    _run_blocks(nc, plans, sems)
    return nc


def make_consts():
    cI = np.eye(128, dtype=np.float32)
    cShift = np.eye(128, k=-1, dtype=np.float32)
    cTri = np.triu(np.ones((128, 128), np.float32)) * NEG_CAUSAL
    uLr = np.zeros((1, 128), np.float32); uLr[0, 64:] = 1.0
    vLr = np.zeros((1, 128), np.float32); vLr[0, :64] = NEG_BAND
    uRr = np.zeros((1, 128), np.float32); uRr[0, :64] = 1.0
    vRr = np.zeros((1, 128), np.float32); vRr[0, 64:] = NEG_BAND
    b = lambda a: np.ascontiguousarray(a.astype(BF))
    return {
        "cI": b(cI), "cShift": b(cShift), "cTri": b(cTri),
        "cuL": b(uLr), "cvL": b(vLr), "cuR": b(uRr), "cvR": b(vRr),
        "cOnes1": b(np.ones((1, 128), np.float32)),
        "cOnesC": np.ones((128, 1), np.float32),
        "cOnesRF": np.ones((1, 128), np.float32),
    }


def kernel(L0_attn_a_x, L0_attn_a_W_Q, L0_attn_a_W_K, L0_attn_a_W_V,
           L0_attn_a_W_O, L0_attn_a_b_Q, L0_attn_a_b_K, L0_attn_a_b_V,
           L0_attn_a_b_O, L0_attn_b_Wq, L0_attn_b_Wk, L0_attn_b_Wv,
           L0_attn_b_Wo, L0_ffn_w1, L0_ffn_w2, L0_ffn_w3):
    consts = make_consts()
    shared = {
        "wqA": np.ascontiguousarray(L0_attn_a_W_Q, np.float32),
        "wkA": np.ascontiguousarray(L0_attn_a_W_K, np.float32),
        "wvA": np.ascontiguousarray(L0_attn_a_W_V, np.float32),
        "woA": np.ascontiguousarray(L0_attn_a_W_O, np.float32),
        "bq": np.ascontiguousarray(L0_attn_a_b_Q, np.float32),
        "bk": np.ascontiguousarray(L0_attn_a_b_K, np.float32),
        "bv": np.ascontiguousarray(L0_attn_a_b_V, np.float32),
        "bo": np.ascontiguousarray(L0_attn_a_b_O, np.float32).reshape(1, Dm),
        "wqB": np.ascontiguousarray(L0_attn_b_Wq, np.float32),
        "wkB": np.ascontiguousarray(L0_attn_b_Wk, np.float32),
        "wvB": np.ascontiguousarray(L0_attn_b_Wv, np.float32),
        "woB": np.ascontiguousarray(L0_attn_b_Wo, np.float32),
        "w1": np.ascontiguousarray(L0_ffn_w1, np.float32),
        "w2": np.ascontiguousarray(L0_ffn_w2, np.float32),
        "w3": np.ascontiguousarray(L0_ffn_w3, np.float32),
        **consts,
    }
    x = np.ascontiguousarray(L0_attn_a_x, np.float32)
    in_maps = [dict(shared, x=x[c]) for c in range(8)]
    nc = build(STAGE)
    res = run_bass_kernel_spmd(nc, in_maps, core_ids=list(range(8)))
    return np.stack([np.asarray(res.results[c]["y"], np.float32)
                     for c in range(8)], axis=0)


# version 04
# speedup vs baseline: 1.1402x; 1.1402x over previous
import numpy as np
import ml_dtypes
import concourse.bass as bass
import concourse.mybir as mybir
from concourse.bass_utils import run_bass_kernel_spmd
from concourse.alu_op_type import AluOpType as OP

F32 = mybir.dt.float32
BF16 = mybir.dt.bfloat16
AF = mybir.ActivationFunctionType
AX = mybir.AxisListType
BF = ml_dtypes.bfloat16

S = 2048
Dm = 2048
H = 16
DH = 128
HFD = 5632
NT = S // 128
ND = Dm // 128
NHF = HFD // 128
NWELEM = float(HFD * Dm)
INV_SCALE = float(1.0 / np.sqrt(np.float32(128.0)))
NEG_CAUSAL = -100000.0
NEG_BAND = -1.0e30

STAGE = 4
USE_ACCUM_SUMS = True
PARANOID = False
PARANOID_ENGS = ("te", "dve", "act")
PARANOID_MODE = "full"
FFN_PART = 30

ENGS = ("te", "dve", "act", "sp", "pl")


class Plan:

    def __init__(self, cnt, known):
        self.ops = {k: [] for k in ENGS}
        self.cnt = cnt
        self.known = known

    def emit(self, eng, fn, waits=(), inc=None, incn=1):
        k = self.known[eng]
        if PARANOID and eng in PARANOID_ENGS:
            if PARANOID_MODE in ("full", "waits"):
                waits = list(waits) + [(s, self.cnt.get(s, 0))
                                       for s in ("s_te", "s_dve", "s_act")]
            if inc is None and PARANOID_MODE in ("full", "incs"):
                inc = {"te": "s_te", "dve": "s_dve", "act": "s_act"}[eng]
        real_waits = []
        for s, v in waits:
            if v is None or v <= 0:
                continue
            if k.get(s, 0) >= v:
                continue
            real_waits.append((s, v))
            k[s] = v
        self.ops[eng].append((fn, real_waits, inc, incn))
        if inc is not None:
            self.cnt[inc] = self.cnt.get(inc, 0) + incn
            if eng not in ("sp", "pl"):
                k[inc] = self.cnt[inc]
        return self.cnt.get(inc)

    def v(self, sem):
        return self.cnt.get(sem, 0)


def _replay(plan, sems, eng):
    drain_all = eng == "dve"
    drain_inc = eng in ("dve", "act")

    def run(e):
        for fn, waits, inc, incn in plan.ops[eng]:
            for s, v in waits:
                e.wait_ge(sems[s], v)
            ins = fn(e)
            if drain_all:
                d = e.drain()
                if inc is not None:
                    d.then_inc(sems[inc], incn)
            elif inc is not None:
                if drain_inc:
                    e.drain().then_inc(sems[inc], incn)
                else:
                    ins.then_inc(sems[inc], incn)
    return run


def _run_blocks(nc, plans, sems):
    for plan in plans:
        with nc.Block() as b:
            if plan.ops["sp"]:
                b.sync(_replay(plan, sems, "sp"))
            if plan.ops["pl"]:
                b.gpsimd(_replay(plan, sems, "pl"))
            if plan.ops["te"]:
                b.tensor(_replay(plan, sems, "te"))
            if plan.ops["dve"]:
                b.vector(_replay(plan, sems, "dve"))
            if plan.ops["act"]:
                b.scalar(_replay(plan, sems, "act"))


def build(stage, debug=False):
    nc = bass.Bass("TRN2", target_bir_lowering=False)
    dk = "ExternalOutput" if debug else "Internal"

    d_x = nc.dram_tensor("x", [S, Dm], F32, kind="ExternalInput")
    d_wqA = nc.dram_tensor("wqA", [H, Dm, DH], F32, kind="ExternalInput")
    d_wkA = nc.dram_tensor("wkA", [H, Dm, DH], F32, kind="ExternalInput")
    d_wvA = nc.dram_tensor("wvA", [H, Dm, DH], F32, kind="ExternalInput")
    d_woA = nc.dram_tensor("woA", [H, DH, Dm], F32, kind="ExternalInput")
    d_bq = nc.dram_tensor("bq", [H, DH], F32, kind="ExternalInput")
    d_bk = nc.dram_tensor("bk", [H, DH], F32, kind="ExternalInput")
    d_bv = nc.dram_tensor("bv", [H, DH], F32, kind="ExternalInput")
    d_bo = nc.dram_tensor("bo", [1, Dm], F32, kind="ExternalInput")
    d_wqB = nc.dram_tensor("wqB", [Dm, Dm], F32, kind="ExternalInput")
    d_wkB = nc.dram_tensor("wkB", [Dm, Dm], F32, kind="ExternalInput")
    d_wvB = nc.dram_tensor("wvB", [Dm, Dm], F32, kind="ExternalInput")
    d_woB = nc.dram_tensor("woB", [Dm, Dm], F32, kind="ExternalInput")
    d_w1 = nc.dram_tensor("w1", [HFD, Dm], F32, kind="ExternalInput")
    d_w2 = nc.dram_tensor("w2", [HFD, Dm], F32, kind="ExternalInput")
    d_w3 = nc.dram_tensor("w3", [Dm, HFD], F32, kind="ExternalInput")
    d_cI = nc.dram_tensor("cI", [128, 128], BF16, kind="ExternalInput")
    d_cShift = nc.dram_tensor("cShift", [128, 128], BF16, kind="ExternalInput")
    d_cTri = nc.dram_tensor("cTri", [128, 128], BF16, kind="ExternalInput")
    d_cuL = nc.dram_tensor("cuL", [1, 128], BF16, kind="ExternalInput")
    d_cvL = nc.dram_tensor("cvL", [1, 128], BF16, kind="ExternalInput")
    d_cuR = nc.dram_tensor("cuR", [1, 128], BF16, kind="ExternalInput")
    d_cvR = nc.dram_tensor("cvR", [1, 128], BF16, kind="ExternalInput")
    d_cOnes1 = nc.dram_tensor("cOnes1", [1, 128], BF16, kind="ExternalInput")
    d_cOnesC = nc.dram_tensor("cOnesC", [128, 1], F32, kind="ExternalInput")
    d_cOnesRF = nc.dram_tensor("cOnesRF", [1, 128], F32, kind="ExternalInput")

    d_y = nc.dram_tensor("y", [S, Dm], F32, kind="ExternalOutput")
    d_xres = nc.dram_tensor("xres", [S, Dm], F32, kind=dk)
    d_xln = nc.dram_tensor("xln", [S, Dm], BF16, kind=dk)
    d_z = nc.dram_tensor("zb", [NT, H, DH, 128], BF16, kind=dk)
    d_hid = nc.dram_tensor("hid", [NHF, 4, 128, 512], BF16, kind=dk)
    d_w1qc = nc.dram_tensor("w1qc", [HFD, Dm], BF16, kind=dk)
    d_w2qc = nc.dram_tensor("w2qc", [HFD, Dm], BF16, kind=dk)
    d_w3qc = nc.dram_tensor("w3qc", [Dm, HFD], BF16, kind=dk)
    d_dbg = None
    if debug:
        d_dbg = nc.dram_tensor("dbg", [NT, 128, 2048], BF16, kind="ExternalOutput")
        d_dbg2 = nc.dram_tensor("dbg2", [NT, 128, 8], F32, kind="ExternalOutput")

    A = nc.alloc_sbuf_tensor
    big1 = A("big1", [128, ND * S], BF16)
    big2 = A("big2", [128, H * Dm], BF16)
    wbuf = A("wbuf", [128, 12288], BF16)
    qkv = A("qkv", [128, 6144], BF16)
    xz = A("xz", [128, 8192], BF16)
    fscr = A("fscr", [128, 4096], BF16)
    anx = A("anx", [128, 3072], BF16)
    xlnst = A("xlnst", [128, 4096], BF16)
    I_sb = A("I_sb", [128, 128], BF16)
    shA = A("shA", [128, 128], BF16)
    triB = A("triB", [128, 128], BF16)
    uL = A("uL", [1, 128], BF16)
    vL = A("vL", [1, 128], BF16)
    uR = A("uR", [1, 128], BF16)
    vR = A("vR", [1, 128], BF16)
    ones1 = A("ones1", [1, 128], BF16)
    onesC = A("onesC", [128, 1], F32)
    onesRF = A("onesRF", [1, 128], F32)
    bnat = wbuf[0:16, 0:384]
    bosb = qkv[0:1, 4096:6144]
    bqkv = A("bqkv", [128, 48], F32)
    cvals = A("cvals", [128, 2], F32)
    stats = A("stats", [128, 64], F32)
    gacc = A("gacc", [128, 4], F32)
    thrbc = A("thrbc", [128, 8], F32)

    xrow = xz.ap().bitcast(F32)
    wf32 = wbuf.ap().bitcast(F32)
    fscr32 = fscr.ap().bitcast(F32)
    accv = big2.ap().bitcast(F32)
    gstage = xlnst.ap().bitcast(F32)

    PS_A = nc.alloc_psum_tensor("psA", [128, 2048], F32)
    PS_B = nc.alloc_psum_tensor("psB", [128, 2048], F32)

    sems = {}
    import contextlib
    _stack = contextlib.ExitStack()
    for name in ("s_te", "s_dve", "s_act", "s_sp", "s_pl", "s_st", "s_q"):
        sems[name] = _stack.enter_context(nc.semaphore(name))

    cnt = {}
    known = {e: {} for e in ENGS}
    plans = []

    def NP():
        p = Plan(cnt, known)
        plans.append(p)
        return p

    def st(i):
        return stats[:, i:i + 1]

    NEGMAX, SUMS, RECIP, GTMP = 0, 1, 2, 3
    SUMX, SSQ, NEGM, MSQ, VR = 0, 1, 2, 3, 4

    def ln_chain(P, xr_ap, out_bf_ap, par, wait_first=(), war_apply=()):
        b = 8 + 8 * par

        def s(i):
            return stats[:, b + i:b + i + 1]
        P.emit("dve", lambda e, a=xr_ap, s0=s(SUMX): e.reduce_sum(s0, a, axis=AX.X),
               waits=wait_first)
        P.emit("act", lambda e, a=xr_ap, s1=s(SSQ): e.activation(
            fscr32[:, 0:2048], a, AF.Square, accum_out=s1),
            waits=list(wait_first) + list(ln_chain.act_war[par]), inc="s_act")
        a_sq = P.v("s_act")
        P.emit("dve", lambda e, s0=s(SUMX), s2=s(NEGM): e.tensor_scalar(
            s2, s0, -1.0 / Dm, None, OP.mult))
        P.emit("dve", lambda e, s2=s(NEGM), s3=s(MSQ): e.tensor_mul(s3, s2, s2),
               waits=[("s_act", a_sq)])
        P.emit("dve", lambda e, s1=s(SSQ), s3=s(MSQ), s4=s(VR): e.scalar_tensor_tensor(
            s4, s1, 1.0 / Dm, s3, OP.mult, OP.subtract),
            waits=list(ln_chain.var_war[par]), inc="s_dve")
        a_var = P.v("s_dve")
        P.emit("act", lambda e, s4=s(VR): e.activation(
            s4, s4, AF.Ln, bias=cvals[:, 0:1]), waits=[("s_dve", a_var)])
        P.emit("act", lambda e, s4=s(VR): e.activation(
            s4, s4, AF.Exp, scale=cvals[:, 1:2]), inc="s_act")
        a_rstd = P.v("s_act")
        P.emit("dve", lambda e, a=xr_ap, o=out_bf_ap, s2=s(NEGM), s4=s(VR): e.tensor_scalar(
            o, a, s2, s4, OP.add, OP.mult),
            waits=[("s_act", a_rstd)] + list(war_apply), inc="s_dve")
        dv = P.v("s_dve")
        ln_chain.act_war[par] = [("s_dve", dv)]
        ln_chain.var_war[par] = [("s_act", a_rstd)]
        return dv
    ln_chain.act_war = {0: [], 1: []}
    ln_chain.var_war = {0: [], 1: []}

    bst = {"xln_st": 0, "big1_te": 0, "wbuf_te": 0, "sp_xlnt": 0}

    def emit_xlnt_treads():
        P = plans[-1]
        w = [("s_st", bst["xln_st"]), ("s_te", bst["big1_te"])]
        for dc in range(ND):
            P.emit("sp", lambda e, dc=dc: e.dma_start_transpose(
                big1[:, dc * S:(dc + 1) * S], d_xln.ap()[:, dc * 128:(dc + 1) * 128]),
                waits=w if dc == 0 else (), inc="s_sp", incn=16)
        bst["sp_xlnt"] = P.v("s_sp")

    def gamma_chunk_list():
        out = []
        for mi, dt_ in ((0, d_w1), (1, d_w2)):
            for hf in range(NHF):
                out.append((mi, dt_.ap()[hf * 128:(hf + 1) * 128, :], 2048))
        for dc in range(ND):
            for c0, w in ((0, 2048), (2048, 2048), (4096, 1536)):
                out.append((2, d_w3.ap()[dc * 128:(dc + 1) * 128, c0:c0 + w], w))
        return out

    P = NP()
    for dst, src in ((I_sb, d_cI), (shA, d_cShift), (triB, d_cTri),
                     (uL, d_cuL), (vL, d_cvL), (uR, d_cuR), (vR, d_cvR),
                     (ones1, d_cOnes1), (onesC, d_cOnesC), (onesRF, d_cOnesRF)):
        P.emit("sp", lambda e, d=dst, s=src: e.dma_start(d[:], s.ap()),
               inc="s_sp", incn=16)
    for j, src in enumerate((d_bq, d_bk, d_bv)):
        P.emit("pl", lambda e, j=j, s=src: e.dma_start(
            bnat[:, j * 128:(j + 1) * 128], s.ap()), inc="s_pl", incn=16)
    pl_b = P.v("s_pl")
    sp_c = P.v("s_sp")
    P.emit("dve", lambda e: e.memset(gacc[:], 0.0))
    P.emit("dve", lambda e: e.memset(cvals[:, 0:1], 1e-5))
    P.emit("dve", lambda e: e.memset(cvals[:, 1:2], -0.5), inc="s_dve")
    for j in range(3):
        P.emit("te", lambda e, j=j: e.matmul(
            PS_B[:, j * 16:(j + 1) * 16], bnat[:, j * 128:(j + 1) * 128],
            I_sb[0:16, 0:16], start=True, stop=True),
            waits=[("s_pl", pl_b), ("s_sp", sp_c)],
            inc="s_te" if j == 2 else None)
    te_b = P.v("s_te")
    P.emit("dve", lambda e: e.tensor_copy(bqkv[:], PS_B[:, 0:48]),
           waits=[("s_te", te_b)], inc="s_dve")

    dve_apply = [0] * NT
    st_xln = [0] * NT
    for t in range(NT):
        par = t % 2
        xa = xrow[:, par * 2048:(par + 1) * 2048]
        wload = [("s_dve", dve_apply[t - 2])] if t >= 2 else []
        P.emit("sp", lambda e, t=t, a=xa: e.dma_start(
            a, d_x.ap()[t * 128:(t + 1) * 128, :]), waits=wload,
            inc="s_sp", incn=16)
        sp_t = P.v("s_sp")
        xo = xlnst[:, par * 2048:(par + 1) * 2048]
        war = [("s_st", st_xln[t - 2])] if t >= 2 else []
        dv = ln_chain(P, xa, xo, par, wait_first=[("s_sp", sp_t)], war_apply=war)
        dve_apply[t] = dv
        P.emit("pl", lambda e, t=t, o=xo: e.dma_start(
            d_xln.ap()[t * 128:(t + 1) * 128, :], o), waits=[("s_dve", dv)],
            inc="s_st", incn=16)
        st_xln[t] = P.v("s_st")
    bst["xln_st"] = P.v("s_st")
    bst["wbuf_te"] = te_b

    def attn_sublayer(variant, resid_src, out_dst, do_ln_next, do_gamma):
        emit_xlnt_treads()
        prevP = plans[-1]
        P = NP()
        sp_xlnt = bst["sp_xlnt"]

        gam = gamma_chunk_list() if do_gamma else []
        gam_i = [0]
        gam_dve = {0: 0, 1: 0}

        def emit_gamma(P):
            if not do_gamma or gam_i[0] >= len(gam):
                return
            i = gam_i[0]
            mi, gap, gw_ = gam[i]
            gdst = gstage[:, 0:gw_]
            P.emit("sp", lambda e, d=gdst, a=gap: e.dma_start(d, a),
                   waits=[("s_dve", gam_dve[0])], inc="s_sp", incn=16)
            sp_g = P.v("s_sp")
            P.emit("dve", lambda e, d=gdst: e.reduce_sum(
                st(GTMP), d, axis=AX.X, apply_absolute_value=True),
                waits=[("s_sp", sp_g)])
            P.emit("dve", lambda e, mi=mi: e.tensor_add(
                gacc[:, mi:mi + 1], gacc[:, mi:mi + 1], st(GTMP)), inc="s_dve")
            gam_dve[0] = P.v("s_dve")
            gam_i[0] += 1

        wo_pl = []
        wq_pl = []
        te_head_end = []
        dve_z_end = []
        st_z_end = []
        act_head_end = 0

        def w_cast(e, h, j):
            par = h % 2
            dst = wbuf[:, (par * 3 + j) * 2048:(par * 3 + j) * 2048 + 2048]
            if variant == "a":
                t = (d_wqA, d_wkA, d_wvA)[j]
                ap = t.ap()[h].rearrange("(dc p) e -> p dc e", p=128)
            else:
                t = (d_wqB, d_wkB, d_wvB)[j]
                ap = t.ap()[:, h * 128:(h + 1) * 128].rearrange(
                    "(dc p) e -> p dc e", p=128)
            return e.dma_start(dst.rearrange("p (dc e) -> p dc e", dc=ND), ap)

        def emit_wq(h, war, plan):
            for j in range(3):
                plan.emit("pl", lambda e, h=h, j=j: w_cast(e, h, j),
                          waits=war if j == 0 else (), inc="s_pl", incn=16)
            wq_pl.append(plan.v("s_pl"))

        def emit_wo(h, plan):
            if variant == "a":
                woap = d_woA.ap()[h]
            else:
                woap = d_woB.ap()[h * 128:(h + 1) * 128, :]
            plan.emit("pl", lambda e, h=h, a=woap: e.dma_start(
                big2[:, h * Dm:(h + 1) * Dm], a), inc="s_pl", incn=16)
            wo_pl.append(plan.v("s_pl"))

        def emit_wcasts(h, war):
            emit_wq(h, war, P)
            emit_wo(h, P)

        emit_wq(0, [("s_te", bst["wbuf_te"])], prevP)
        emit_wq(1, [], prevP)
        emit_wo(0, P)
        emit_wo(1, P)


        Ksb = qkv[:, 0:2048]
        Qsb = qkv[:, 2048:4096]
        Vsb = qkv[:, 4096:6144]
        vfm = fscr[:, 0:2048]

        for h in range(H):
            par = h % 2
            wq_s = wbuf[:, (par * 3 + 0) * 2048:(par * 3 + 0) * 2048 + 2048]
            wk_s = wbuf[:, (par * 3 + 1) * 2048:(par * 3 + 1) * 2048 + 2048]
            wv_s = wbuf[:, (par * 3 + 2) * 2048:(par * 3 + 2) * 2048 + 2048]

            wts = [("s_pl", wq_pl[h]), ("s_sp", sp_xlnt)]
            if h > 0:
                wts += [("s_act", act_head_end), ("s_dve", dve_z_end[h - 1])]
            for sti in range(4):
                for dc in range(ND):
                    last = (sti == 3 and dc == ND - 1)
                    P.emit("te", lambda e, sti=sti, dc=dc, w=wk_s: e.matmul(
                        PS_A[:, sti * 512:(sti + 1) * 512],
                        w[:, dc * 128:(dc + 1) * 128],
                        big1[:, dc * S + sti * 512: dc * S + (sti + 1) * 512],
                        start=(dc == 0), stop=(dc == ND - 1)),
                        waits=wts if (sti == 0 and dc == 0) else (),
                        inc="s_te" if last else None)
            te_k = P.v("s_te")
            if variant == "a":
                P.emit("dve", lambda e, h=h: e.tensor_scalar(
                    Ksb, PS_A[:, :], bqkv[:, 16 + h:16 + h + 1], None, OP.add),
                    waits=[("s_te", te_k)], inc="s_dve")
            else:
                P.emit("dve", lambda e: e.tensor_copy(Ksb, PS_A[:, :]),
                       waits=[("s_te", te_k)], inc="s_dve")
            dve_k = P.v("s_dve")
            emit_gamma(P)
            qwts = [("s_dve", dve_z_end[h - 1])] if h > 0 else []
            for sti in range(4):
                for dc in range(ND):
                    last = (sti == 3 and dc == ND - 1)
                    P.emit("te", lambda e, sti=sti, dc=dc, w=wq_s: e.matmul(
                        PS_B[:, sti * 512:(sti + 1) * 512],
                        w[:, dc * 128:(dc + 1) * 128],
                        big1[:, dc * S + sti * 512: dc * S + (sti + 1) * 512],
                        start=(dc == 0), stop=(dc == ND - 1)),
                        waits=qwts if (sti == 0 and dc == 0) else (),
                        inc="s_te" if last else None)
            te_q = P.v("s_te")
            if variant == "a":
                P.emit("dve", lambda e, h=h: e.tensor_scalar(
                    Qsb, PS_B[:, :], bqkv[:, h:h + 1], INV_SCALE, OP.add, OP.mult),
                    waits=[("s_te", te_q)], inc="s_dve")
            else:
                P.emit("dve", lambda e: e.tensor_scalar(
                    Qsb, PS_B[:, :], INV_SCALE, None, OP.mult),
                    waits=[("s_te", te_q)], inc="s_dve")
            dve_q = P.v("s_dve")
            emit_gamma(P)
            for sti in range(4):
                for dc in range(ND):
                    last = (sti == 3 and dc == ND - 1)
                    P.emit("te", lambda e, sti=sti, dc=dc, w=wv_s: e.matmul(
                        PS_A[:, sti * 512:(sti + 1) * 512],
                        w[:, dc * 128:(dc + 1) * 128],
                        big1[:, dc * S + sti * 512: dc * S + (sti + 1) * 512],
                        start=(dc == 0), stop=(dc == ND - 1)),
                        waits=[("s_dve", dve_k)] if (sti == 0 and dc == 0) else (),
                        inc="s_te" if last else None)
            te_v = P.v("s_te")
            P.emit("dve", lambda e: e.tensor_copy(vfm, PS_A[:, :]),
                   waits=[("s_te", te_v)], inc="s_dve")
            dve_vf = P.v("s_dve")
            for tt in range(NT):
                P.emit("te", lambda e, tt=tt: e.matmul(
                    PS_B[:, tt * 128:(tt + 1) * 128],
                    vfm[:, tt * 128:(tt + 1) * 128], I_sb[:],
                    start=True, stop=True),
                    waits=[("s_dve", dve_q), ("s_dve", dve_vf)] if tt == 0 else (),
                    inc="s_te" if tt == NT - 1 else None)
            te_vt = P.v("s_te")
            P.emit("dve", lambda e: e.tensor_copy(Vsb, PS_B[:, :]),
                   waits=[("s_te", te_vt)], inc="s_dve")
            dve_vt = P.v("s_dve")

            te_attnT_done = 0
            act_exp_prev = 0
            for qt in range(NT):
                if variant == "a":
                    kts = list(range(qt + 1))
                else:
                    kts = [k for k in (qt - 1, qt, qt + 1) if 0 <= k < NT]
                kw = len(kts) * 128
                scw = [("s_act", act_exp_prev)]
                if qt == 0:
                    scw.append(("s_dve", dve_vt))
                if variant == "a":
                    nfull = (qt * 128) // 512
                    for j in range(nfull):
                        P.emit("te", lambda e, qt=qt, j=j: e.matmul(
                            PS_A[:, j * 512:(j + 1) * 512],
                            Qsb[:, qt * 128:(qt + 1) * 128],
                            Ksb[:, j * 512:(j + 1) * 512],
                            start=True, stop=True),
                            waits=scw if j == 0 else ())
                    rem0 = nfull * 512
                    remw = kw - rem0
                    P.emit("te", lambda e, qt=qt, rem0=rem0, remw=remw: e.matmul(
                        PS_A[:, rem0:rem0 + remw],
                        Qsb[:, qt * 128:(qt + 1) * 128],
                        Ksb[:, rem0:rem0 + remw],
                        start=True, stop=True),
                        waits=scw if nfull == 0 else ())
                    P.emit("te", lambda e, qt=qt: e.matmul(
                        PS_A[:, qt * 128:(qt + 1) * 128], shA[:], triB[:],
                        start=False, stop=True, skip_group_check=True),
                        inc="s_te")
                else:
                    k0 = kts[0] * 128
                    P.emit("te", lambda e, qt=qt, k0=k0, kw=kw: e.matmul(
                        PS_A[:, 0:kw],
                        Qsb[:, qt * 128:(qt + 1) * 128], Ksb[:, k0:k0 + kw],
                        start=True, stop=True), waits=scw)
                    nmask = sum(1 for kt in kts if kt in (qt - 1, qt + 1))
                    mdone = 0
                    for i, kt in enumerate(kts):
                        if kt == qt - 1:
                            uu, vv = uL, vL
                        elif kt == qt + 1:
                            uu, vv = uR, vR
                        else:
                            continue
                        mdone += 1
                        P.emit("te", lambda e, i=i, uu=uu, vv=vv: e.matmul(
                            PS_A[:, i * 128:(i + 1) * 128], uu[:], vv[:],
                            start=False, stop=True, skip_group_check=True),
                            inc="s_te" if mdone == nmask else None)
                te_sc = P.v("s_te")
                P.emit("dve", lambda e, kw=kw: e.reduce_max(
                    st(NEGMAX), PS_A[:, 0:kw], axis=AX.X, negate=True),
                    waits=[("s_te", te_sc)], inc="s_dve")
                dve_nm = P.v("s_dve")
                P.emit("act", lambda e, kw=kw: e.activation(
                    anx[:, 0:kw], PS_A[:, 0:kw], AF.Exp,
                    bias=st(NEGMAX), accum_out=st(SUMS)),
                    waits=[("s_dve", dve_nm), ("s_te", te_attnT_done)],
                    inc="s_act")
                act_exp = P.v("s_act")
                act_exp_prev = act_exp
                if USE_ACCUM_SUMS:
                    P.emit("dve", lambda e: e.reciprocal(st(RECIP), st(SUMS)),
                           waits=[("s_act", act_exp)])
                else:
                    P.emit("dve", lambda e, kw=kw: e.reduce_sum(
                        st(SUMS), anx[:, 0:kw], axis=AX.X),
                        waits=[("s_act", act_exp)])
                    P.emit("dve", lambda e: e.reciprocal(st(RECIP), st(SUMS)))
                P.emit("dve", lambda e: e.tensor_scalar(
                    anx[:, 2560:2688], I_sb[:], st(RECIP), None, OP.mult),
                    inc="s_dve")
                dve_diag = P.v("s_dve")
                zslot = 1792 + (qt % 2) * 128
                n_k = len(kts)
                dve_attnT = 0
                for g0 in range(0, n_k, 4):
                    gkts = kts[g0:g0 + 4]
                    ng = len(gkts)
                    for i in range(ng):
                        P.emit("te", lambda e, i=i, g0=g0: e.matmul(
                            PS_B[:, 1024 + i * 128:1024 + (i + 1) * 128],
                            anx[:, (g0 + i) * 128:(g0 + i + 1) * 128],
                            anx[:, 2560:2688], start=True, stop=True),
                            waits=([("s_dve", dve_diag), ("s_dve", dve_attnT)]
                                   if i == 0 else ()),
                            inc="s_te" if i == ng - 1 else None)
                    te_at = P.v("s_te")
                    P.emit("dve", lambda e, gw=ng * 128: e.tensor_copy(
                        anx[:, 2048:2048 + gw], PS_B[:, 1024:1024 + gw]),
                        waits=[("s_te", te_at)], inc="s_dve")
                    dve_attnT = P.v("s_dve")
                    for i, kt in enumerate(gkts):
                        glast = (g0 + i == n_k - 1)
                        gfirst = (g0 == 0 and i == 0)
                        P.emit("te", lambda e, i=i, kt=kt, zs=zslot, st_=gfirst, sp_=glast: e.matmul(
                            PS_B[:, zs:zs + 128],
                            Vsb[:, kt * 128:(kt + 1) * 128],
                            anx[:, 2048 + i * 128:2048 + (i + 1) * 128],
                            start=st_, stop=sp_),
                            waits=([("s_dve", dve_attnT)] if i == 0 else ()),
                            inc="s_te" if glast else None)
                te_z = P.v("s_te")
                te_attnT_done = te_z
                if debug and h == 0:
                    P.emit("pl", lambda e, qt=qt, kw=kw: e.dma_start(
                        d_dbg.ap()[qt][:, 0:kw], anx[:, 0:kw]),
                        waits=[("s_dve", P.v("s_dve"))], inc="s_st", incn=16)
                    P.emit("pl", lambda e, qt=qt: e.dma_start(
                        d_dbg2.ap()[qt][:, 0:4], stats[:, 0:4]),
                        inc="s_st", incn=16)
                zdst = xz[:, (h % 2) * 2048 + qt * 128:(h % 2) * 2048 + (qt + 1) * 128]
                zwar = [("s_st", st_z_end[h - 2])] if h >= 2 else []
                if variant == "a":
                    P.emit("dve", lambda e, h=h, zd=zdst, zs=zslot: e.tensor_scalar(
                        zd, PS_B[:, zs:zs + 128], bqkv[:, 32 + h:32 + h + 1],
                        None, OP.add),
                        waits=[("s_te", te_z)] + zwar, inc="s_dve")
                else:
                    P.emit("dve", lambda e, zd=zdst, zs=zslot: e.tensor_copy(
                        zd, PS_B[:, zs:zs + 128]),
                        waits=[("s_te", te_z)] + zwar, inc="s_dve")
                dve_zev = P.v("s_dve")
                P.emit("pl", lambda e, qt=qt, h=h, zd=zdst: e.dma_start(
                    d_z.ap()[qt, h], zd), waits=[("s_dve", dve_zev)],
                    inc="s_st", incn=16)
                if (h * NT + qt) % 2 == 0:
                    emit_gamma(P)
            act_head_end = P.v("s_act")
            dve_z_end.append(P.v("s_dve"))
            st_z_end.append(P.v("s_st"))
            te_head_end.append(P.v("s_te"))
            if h + 2 < H:
                emit_wcasts(h + 2, [("s_te", te_head_end[h])])

        while do_gamma and gam_i[0] < len(gam):
            emit_gamma(P)
        if do_gamma:
            dve_g = P.v("s_dve")
            P.emit("te", lambda e: e.matmul(
                PS_B[0:1, 0:3], onesC[:], gacc[:, 0:3], start=True, stop=True),
                waits=[("s_dve", dve_g), ("s_dve", dve_z_end[-1])], inc="s_te")
            te_g = P.v("s_te")
            P.emit("dve", lambda e: e.tensor_scalar(
                stats[0:1, 32:35], PS_B[0:1, 0:3], 0.5 / NWELEM, 0.5e-8,
                OP.mult, OP.add), waits=[("s_te", te_g)], inc="s_dve")
            dve_t = P.v("s_dve")
            P.emit("te", lambda e: e.matmul(
                PS_B[:, 16:19], onesRF[:], stats[0:1, 32:35],
                start=True, stop=True), waits=[("s_dve", dve_t)], inc="s_te")
            te_t = P.v("s_te")
            P.emit("dve", lambda e: e.tensor_copy(thrbc[:, 0:3], PS_B[:, 16:19]),
                   waits=[("s_te", te_t)])
            P.emit("dve", lambda e: e.tensor_scalar(
                thrbc[:, 4:7], thrbc[:, 0:3], -1.0, None, OP.mult), inc="s_dve")
        bst["big1_te"] = te_head_end[-1]
        bst["wbuf_te"] = te_head_end[-1]

        P = NP()
        pl_bo = 0
        if variant == "a":
            P.emit("pl", lambda e: e.dma_start(bosb, d_bo.ap()),
                   inc="s_pl", incn=16)
            pl_bo = P.v("s_pl")
        te_end = [0] * NT
        dve_add_l = [0] * NT
        dve_ln_l = [0] * NT
        st_q = [0] * NT
        for qt in range(NT):
            par = qt % 2
            zt = qkv[:, par * 2048:(par + 1) * 2048]
            xa = xrow[:, par * 2048:(par + 1) * 2048]
            wz = [("s_te", te_end[qt - 2])] if qt >= 2 else []
            P.emit("sp", lambda e, qt=qt, d=zt: e.dma_start(
                d.rearrange("p (hh e) -> p hh e", hh=H),
                d_z.ap()[qt].rearrange("hh p e -> p hh e")),
                waits=wz, inc="s_sp", incn=16)
            wx = []
            if qt >= 2:
                wx = [("s_st", st_q[qt - 2]), ("s_dve", dve_ln_l[qt - 2])]
            P.emit("sp", lambda e, qt=qt, d=xa: e.dma_start(
                d, resid_src.ap()[qt * 128:(qt + 1) * 128, :]),
                waits=wx, inc="s_sp", incn=16)
            sp_q = P.v("s_sp")
            PS = PS_A if par == 0 else PS_B
            wts0 = [("s_sp", sp_q), ("s_pl", wo_pl[-1])]
            if qt >= 2:
                wts0.append(("s_dve", dve_add_l[qt - 2]))
            lastb = (variant != "a")
            for hh in range(H):
                for j in range(4):
                    last = (hh == H - 1 and j == 3)
                    P.emit("te", lambda e, hh=hh, j=j, z=zt, PS=PS: e.matmul(
                        PS[:, j * 512:(j + 1) * 512],
                        z[:, hh * 128:(hh + 1) * 128],
                        big2[:, hh * Dm + j * 512: hh * Dm + (j + 1) * 512],
                        start=(hh == 0), stop=(hh == H - 1 and lastb)),
                        waits=wts0 if (hh == 0 and j == 0) else (),
                        inc="s_te" if (last and lastb) else None)
            if variant == "a":
                for j in range(4):
                    P.emit("te", lambda e, j=j, PS=PS: e.matmul(
                        PS[:, j * 512:(j + 1) * 512], ones1[:],
                        bosb[0:1, j * 512:(j + 1) * 512],
                        start=False, stop=True, skip_group_check=True),
                        waits=[("s_pl", pl_bo)] if j == 0 else (),
                        inc="s_te" if j == 3 else None)
            te_end[qt] = P.v("s_te")
            P.emit("dve", lambda e, xa=xa, PS=PS: e.tensor_add(xa, xa, PS[:, :]),
                   waits=[("s_te", te_end[qt])], inc="s_dve")
            dve_add_l[qt] = P.v("s_dve")
            P.emit("pl", lambda e, qt=qt, xa=xa, dst=out_dst: e.dma_start(
                dst.ap()[qt * 128:(qt + 1) * 128, :], xa),
                waits=[("s_dve", dve_add_l[qt])], inc="s_st", incn=16)
            if do_ln_next:
                xo = xlnst[:, par * 2048:(par + 1) * 2048]
                war = [("s_st", st_q[qt - 2])] if qt >= 2 else []
                dv = ln_chain(P, xa, xo, par,
                              wait_first=[("s_dve", dve_add_l[qt])],
                              war_apply=war)
                dve_ln_l[qt] = dv
                P.emit("pl", lambda e, qt=qt, o=xo: e.dma_start(
                    d_xln.ap()[qt * 128:(qt + 1) * 128, :], o),
                    waits=[("s_dve", dv)], inc="s_st", incn=16)
            else:
                dve_ln_l[qt] = dve_add_l[qt]
            st_q[qt] = P.v("s_st")
        bst["xln_st"] = P.v("s_st")

    def ffn_sublayer(first, resid_src, out_dst, do_ln_next):
        SUP = 4
        nsup = NHF // SUP
        emit_xlnt_treads()
        P = NP()
        sp_xlnt = bst["sp_xlnt"]

        qstate = {"par": 0, "dq": {0: 0, 1: 0}, "last_st": 0}

        def quantize_chunk(P, src_ap, dst_ap, mi, width, q="pl"):
            par_f = qstate["par"]
            qstate["par"] ^= 1
            fa = wf32[:, par_f * 2048:par_f * 2048 + width]
            P.emit("pl", lambda e, d=fa, a=src_ap: e.dma_start(d, a),
                   waits=[("s_dve", qstate["dq"][par_f])], inc="s_q", incn=16)
            sp_w = P.v("s_q")
            pbuf = fscr[:, 2048:2048 + width]
            nbuf = xlnst[:, 0:width]
            qbuf = qkv[:, 4096:4096 + width]
            P.emit("dve", lambda e, f=fa, p=pbuf, mi=mi: e.tensor_single_scalar(
                p, f, thrbc[:, mi:mi + 1], OP.is_gt),
                waits=[("s_q", sp_w)])
            P.emit("dve", lambda e, f=fa, n=nbuf, mi=mi: e.tensor_single_scalar(
                n, f, thrbc[:, 4 + mi:4 + mi + 1], OP.is_lt))
            P.emit("dve", lambda e, p=pbuf, n=nbuf, q=qbuf: e.tensor_sub(q, p, n),
                   waits=[("s_st", qstate["last_st"])], inc="s_dve")
            dq = P.v("s_dve")
            qstate["dq"][par_f] = dq
            P.emit("pl", lambda e, d=dst_ap, q=qbuf: e.dma_start(d, q),
                   waits=[("s_dve", dq)], inc="s_st", incn=16)
            qstate["last_st"] = P.v("s_st")
            return qstate["last_st"]

        w3chunks = []
        for dc in range(ND):
            for c0, w in ((0, 2048), (2048, 2048), (4096, 1536)):
                w3chunks.append((dc, c0, w))
        w3_i = [0]

        def emit_w3_quant(P):
            if not first or w3_i[0] >= len(w3chunks):
                return
            dc, c0, w = w3chunks[w3_i[0]]
            quantize_chunk(P, d_w3.ap()[dc * 128:(dc + 1) * 128, c0:c0 + w],
                           d_w3qc.ap()[dc * 128:(dc + 1) * 128, c0:c0 + w], 2, w,
                           q="sp")
            w3_i[0] += 1

        st_after_q = [0] * nsup

        def quantize_super(s_):
            for hf in range(s_ * SUP, (s_ + 1) * SUP):
                quantize_chunk(P, d_w1.ap()[hf * 128:(hf + 1) * 128, :],
                               d_w1qc.ap()[hf * 128:(hf + 1) * 128, :], 0, 2048)
                quantize_chunk(P, d_w2.ap()[hf * 128:(hf + 1) * 128, :],
                               d_w2qc.ap()[hf * 128:(hf + 1) * 128, :], 1, 2048)
            st_after_q[s_] = P.v("s_st")

        if first and FFN_PART >= 1:
            quantize_super(0)

        sp_tread = [0] * nsup
        te_super_done = [0] * nsup
        act_sig = 0
        dve_hid = 0
        st_hid = [0] * NHF

        for s_ in range(nsup if FFN_PART >= 2 else 0):
            war = [("s_te", te_super_done[s_ - 2])] if s_ >= 2 else []
            if first:
                war.append(("s_st", st_after_q[s_]))
            par = s_ % 2
            base = par * 16384
            for mi, qc in ((0, d_w1qc), (1, d_w2qc)):
                for dc in range(ND):
                    P.emit("sp", lambda e, qc=qc, s_=s_, dc=dc, mi=mi, base=base: e.dma_start_transpose(
                        big2[:, base + mi * 8192 + dc * 512: base + mi * 8192 + dc * 512 + 512],
                        qc.ap()[s_ * 512:(s_ + 1) * 512, dc * 128:(dc + 1) * 128]),
                        waits=war if (mi == 0 and dc == 0) else (),
                        inc="s_sp", incn=16)
            sp_tread[s_] = P.v("s_sp")
            if first and FFN_PART >= 1 and s_ + 1 < nsup:
                quantize_super(s_ + 1)
            for ii in range(SUP):
                hf = s_ * SUP + ii
                for mi, PS in ((0, PS_A), (1, PS_B)):
                    wts = [("s_sp", sp_tread[s_]), ("s_sp", sp_xlnt)]
                    if hf > 0:
                        wts.append(("s_act", act_sig) if mi == 0 else
                                   ("s_dve", dve_hid))
                    for sti in range(4):
                        for dc in range(ND):
                            last = (sti == 3 and dc == ND - 1)
                            P.emit("te", lambda e, mi=mi, PS=PS, sti=sti, dc=dc, base=base, ii=ii: e.matmul(
                                PS[:, sti * 512:(sti + 1) * 512],
                                big2[:, base + mi * 8192 + dc * 512 + ii * 128: base + mi * 8192 + dc * 512 + ii * 128 + 128],
                                big1[:, dc * S + sti * 512: dc * S + (sti + 1) * 512],
                                start=(dc == 0), stop=(dc == ND - 1)),
                                waits=wts if (sti == 0 and dc == 0) else (),
                                inc="s_te" if last else None)
                te_hf = P.v("s_te")
                s1 = qkv[:, 0:2048]
                hidb = qkv[:, 2048:4096]
                P.emit("act", lambda e: e.activation(s1, PS_A[:, :], AF.Sigmoid),
                       waits=[("s_te", te_hf - 1), ("s_dve", dve_hid)],
                       inc="s_act")
                act_sig = P.v("s_act")
                hwar = [("s_st", st_hid[hf - 1])] if hf >= 1 else []
                P.emit("dve", lambda e: e.tensor_mul(hidb, s1, PS_B[:, :]),
                       waits=[("s_act", act_sig), ("s_te", te_hf)] + hwar,
                       inc="s_dve")
                dve_hid = P.v("s_dve")
                P.emit("pl", lambda e, hf=hf: e.dma_start(
                    d_hid.ap()[hf].rearrange("st p f -> p st f"),
                    hidb.rearrange("p (st f) -> p st f", st=4)),
                    waits=[("s_dve", dve_hid)], inc="s_st", incn=16)
                st_hid[hf] = P.v("s_st")
                emit_w3_quant(P)
            te_super_done[s_] = P.v("s_te")
        while first and FFN_PART >= 1 and w3_i[0] < len(w3chunks):
            emit_w3_quant(P)
        st_w3_done = P.v("s_st")
        st_hid_done = st_hid[-1]

        GRP = 6
        ngrp = (NHF + GRP - 1) // GRP

        def emit_group_loads(Pl, half, g, war):
            lo = g * GRP
            hi = min(NHF, lo + GRP)
            n = hi - lo
            gpar = g % 2
            for i in range(n):
                hf = lo + i
                Pl.emit("sp", lambda e, hf=hf, i=i, gpar=gpar: e.dma_start_transpose(
                    big1[:, gpar * 12288 + i * 2048: gpar * 12288 + (i + 1) * 2048],
                    d_w3qc.ap()[:, hf * 128:(hf + 1) * 128]),
                    waits=war if i == 0 else (), inc="s_sp", incn=16)
            for i in range(n):
                hf = lo + i
                Pl.emit("sp", lambda e, hf=hf, i=i, gpar=gpar, half=half: e.dma_start(
                    wbuf[:, gpar * 6144 + i * 1024: gpar * 6144 + (i + 1) * 1024].rearrange("p (st f) -> p st f", st=2),
                    d_hid.ap()[hf][half * 2:half * 2 + 2].rearrange("st p f -> p st f")),
                    inc="s_sp", incn=16)
            return Pl.v("s_sp")

        pref = {}
        if FFN_PART >= 21:
            pref[(0, 0)] = emit_group_loads(
                P, 0, 0, [("s_te", te_super_done[-1] if FFN_PART >= 2 else 0),
                          ("s_st", P.v("s_st")), ("s_dve", P.v("s_dve"))])
        for half in range(2 if FFN_PART >= 21 else 0):
            prev_plan = plans[-1]
            P = NP()
            te_grp_done = [0] * ngrp
            dve_acc = [0] * 8
            last_ps = {0: 0, 1: 0}
            for g in range(ngrp):
                lo = g * GRP
                hi = min(NHF, lo + GRP)
                n = hi - lo
                gpar = g % 2
                if (half, g) in pref:
                    sp_g = pref[(half, g)]
                else:
                    war = [("s_te", te_grp_done[g - 2])] if g >= 2 else []
                    sp_g = emit_group_loads(P, half, g, war)
                for sc in range(8 if FFN_PART >= 25 else 0):
                    PS = PS_A if sc % 2 == 0 else PS_B
                    for i in range(n):
                        for j in range(4):
                            last = (i == n - 1 and j == 3)
                            P.emit("te", lambda e, i=i, j=j, sc=sc, gpar=gpar, PS=PS, n=n: e.matmul(
                                PS[:, j * 512:(j + 1) * 512],
                                wbuf[:, gpar * 6144 + i * 1024 + sc * 128: gpar * 6144 + i * 1024 + sc * 128 + 128],
                                big1[:, gpar * 12288 + i * 2048 + j * 512: gpar * 12288 + i * 2048 + (j + 1) * 512],
                                start=(i == 0), stop=(i == n - 1)),
                                waits=([("s_sp", sp_g), ("s_dve", last_ps[sc % 2])]
                                       if (i == 0 and j == 0) else ()),
                                inc="s_te" if last else None)
                    te_sc = P.v("s_te")
                    acc = accv[:, sc * 2048:(sc + 1) * 2048]
                    if g == 0:
                        P.emit("dve", lambda e, acc=acc, PS=PS: e.tensor_copy(
                            acc, PS[:, :]), waits=[("s_te", te_sc)], inc="s_dve")
                    else:
                        P.emit("dve", lambda e, acc=acc, PS=PS: e.tensor_add(
                            acc, acc, PS[:, :]), waits=[("s_te", te_sc)], inc="s_dve")
                    dve_acc[sc] = P.v("s_dve")
                    last_ps[sc % 2] = dve_acc[sc]
                te_grp_done[g] = P.v("s_te")
            st_sc = [0] * 8
            dve_ln_sc = [0] * 8
            for sc in range(8 if FFN_PART >= 30 else 0):
                t = half * 8 + sc
                par = sc % 2
                xa = xrow[:, par * 2048:(par + 1) * 2048]
                wx = []
                if sc >= 2:
                    wx = [("s_st", st_sc[sc - 2]), ("s_dve", dve_ln_sc[sc - 2])]
                P.emit("sp", lambda e, t=t, d=xa: e.dma_start(
                    d, resid_src.ap()[t * 128:(t + 1) * 128, :]),
                    waits=wx, inc="s_sp", incn=16)
                sp_x = P.v("s_sp")
                acc = accv[:, sc * 2048:(sc + 1) * 2048]
                P.emit("dve", lambda e, xa=xa, acc=acc: e.tensor_add(xa, xa, acc),
                       waits=[("s_sp", sp_x), ("s_dve", dve_acc[sc])], inc="s_dve")
                dve_add = P.v("s_dve")
                P.emit("pl", lambda e, t=t, xa=xa, dst=out_dst: e.dma_start(
                    dst.ap()[t * 128:(t + 1) * 128, :], xa),
                    waits=[("s_dve", dve_add)], inc="s_st", incn=16)
                if do_ln_next:
                    xo = xlnst[:, par * 2048:(par + 1) * 2048]
                    war = [("s_st", st_sc[sc - 2])] if sc >= 2 else []
                    dv = ln_chain(P, xa, xo, par,
                                  wait_first=[("s_dve", dve_add)],
                                  war_apply=war)
                    dve_ln_sc[sc] = dv
                    P.emit("pl", lambda e, t=t, o=xo: e.dma_start(
                        d_xln.ap()[t * 128:(t + 1) * 128, :], o),
                        waits=[("s_dve", dv)], inc="s_st", incn=16)
                else:
                    dve_ln_sc[sc] = dve_add
                st_sc[sc] = P.v("s_st")
            if half == 0 and FFN_PART >= 21:
                pref[(1, 0)] = emit_group_loads(
                    P, 1, 0, [("s_te", te_grp_done[ngrp - 2])])
            if half == 1:
                bst["xln_st"] = P.v("s_st")
                bst["big1_te"] = te_grp_done[ngrp - 1]
                bst["wbuf_te"] = te_grp_done[ngrp - 1]

    attn_sublayer("a", d_x, d_xres if stage > 1 else d_y,
                  do_ln_next=(stage > 1), do_gamma=(stage > 1))
    if stage > 1:
        ffn_sublayer(True, d_xres, d_xres if stage > 2 else d_y,
                     do_ln_next=(stage > 2))
    if stage > 2:
        attn_sublayer("b", d_xres, d_xres if stage > 3 else d_y,
                      do_ln_next=(stage > 3), do_gamma=False)
    if stage > 3:
        ffn_sublayer(False, d_xres, d_y, do_ln_next=False)

    P = NP()
    total_st = cnt.get("s_st", 0)
    P.emit("pl", lambda e: e.engine_nop(), waits=[("s_st", total_st)])

    _run_blocks(nc, plans, sems)
    return nc


def make_consts():
    cI = np.eye(128, dtype=np.float32)
    cShift = np.eye(128, k=-1, dtype=np.float32)
    cTri = np.triu(np.ones((128, 128), np.float32)) * NEG_CAUSAL
    uLr = np.zeros((1, 128), np.float32); uLr[0, 64:] = 1.0
    vLr = np.zeros((1, 128), np.float32); vLr[0, :64] = NEG_BAND
    uRr = np.zeros((1, 128), np.float32); uRr[0, :64] = 1.0
    vRr = np.zeros((1, 128), np.float32); vRr[0, 64:] = NEG_BAND
    b = lambda a: np.ascontiguousarray(a.astype(BF))
    return {
        "cI": b(cI), "cShift": b(cShift), "cTri": b(cTri),
        "cuL": b(uLr), "cvL": b(vLr), "cuR": b(uRr), "cvR": b(vRr),
        "cOnes1": b(np.ones((1, 128), np.float32)),
        "cOnesC": np.ones((128, 1), np.float32),
        "cOnesRF": np.ones((1, 128), np.float32),
    }


def kernel(L0_attn_a_x, L0_attn_a_W_Q, L0_attn_a_W_K, L0_attn_a_W_V,
           L0_attn_a_W_O, L0_attn_a_b_Q, L0_attn_a_b_K, L0_attn_a_b_V,
           L0_attn_a_b_O, L0_attn_b_Wq, L0_attn_b_Wk, L0_attn_b_Wv,
           L0_attn_b_Wo, L0_ffn_w1, L0_ffn_w2, L0_ffn_w3):
    consts = make_consts()
    shared = {
        "wqA": np.ascontiguousarray(L0_attn_a_W_Q, np.float32),
        "wkA": np.ascontiguousarray(L0_attn_a_W_K, np.float32),
        "wvA": np.ascontiguousarray(L0_attn_a_W_V, np.float32),
        "woA": np.ascontiguousarray(L0_attn_a_W_O, np.float32),
        "bq": np.ascontiguousarray(L0_attn_a_b_Q, np.float32),
        "bk": np.ascontiguousarray(L0_attn_a_b_K, np.float32),
        "bv": np.ascontiguousarray(L0_attn_a_b_V, np.float32),
        "bo": np.ascontiguousarray(L0_attn_a_b_O, np.float32).reshape(1, Dm),
        "wqB": np.ascontiguousarray(L0_attn_b_Wq, np.float32),
        "wkB": np.ascontiguousarray(L0_attn_b_Wk, np.float32),
        "wvB": np.ascontiguousarray(L0_attn_b_Wv, np.float32),
        "woB": np.ascontiguousarray(L0_attn_b_Wo, np.float32),
        "w1": np.ascontiguousarray(L0_ffn_w1, np.float32),
        "w2": np.ascontiguousarray(L0_ffn_w2, np.float32),
        "w3": np.ascontiguousarray(L0_ffn_w3, np.float32),
        **consts,
    }
    x = np.ascontiguousarray(L0_attn_a_x, np.float32)
    in_maps = [dict(shared, x=x[c]) for c in range(8)]
    nc = build(STAGE)
    res = run_bass_kernel_spmd(nc, in_maps, core_ids=list(range(8)))
    return np.stack([np.asarray(res.results[c]["y"], np.float32)
                     for c in range(8)], axis=0)


# version 05
# speedup vs baseline: 1.1507x; 1.0093x over previous
import numpy as np
import ml_dtypes
import concourse.bass as bass
import concourse.mybir as mybir
from concourse.bass_utils import run_bass_kernel_spmd
from concourse.alu_op_type import AluOpType as OP

F32 = mybir.dt.float32
BF16 = mybir.dt.bfloat16
AF = mybir.ActivationFunctionType
AX = mybir.AxisListType
BF = ml_dtypes.bfloat16

S = 2048
Dm = 2048
H = 16
DH = 128
HFD = 5632
NT = S // 128
ND = Dm // 128
NHF = HFD // 128
NWELEM = float(HFD * Dm)
INV_SCALE = float(1.0 / np.sqrt(np.float32(128.0)))
NEG_CAUSAL = -100000.0
NEG_BAND = -1.0e30

STAGE = 4
USE_ACCUM_SUMS = True
PARANOID = False
PARANOID_ENGS = ("te", "dve", "act")
PARANOID_MODE = "full"
FFN_PART = 30

ENGS = ("te", "dve", "act", "sp", "pl")


class Plan:

    def __init__(self, cnt, known):
        self.ops = {k: [] for k in ENGS}
        self.cnt = cnt
        self.known = known

    def emit(self, eng, fn, waits=(), inc=None, incn=1):
        k = self.known[eng]
        if PARANOID and eng in PARANOID_ENGS:
            if PARANOID_MODE in ("full", "waits"):
                waits = list(waits) + [(s, self.cnt.get(s, 0))
                                       for s in ("s_te", "s_dve", "s_act")]
            if inc is None and PARANOID_MODE in ("full", "incs"):
                inc = {"te": "s_te", "dve": "s_dve", "act": "s_act"}[eng]
        real_waits = []
        for s, v in waits:
            if v is None or v <= 0:
                continue
            if k.get(s, 0) >= v:
                continue
            real_waits.append((s, v))
            k[s] = v
        self.ops[eng].append((fn, real_waits, inc, incn))
        if inc is not None:
            self.cnt[inc] = self.cnt.get(inc, 0) + incn
            if eng not in ("sp", "pl"):
                k[inc] = self.cnt[inc]
        return self.cnt.get(inc)

    def v(self, sem):
        return self.cnt.get(sem, 0)


def _replay(plan, sems, eng):
    drain_all = eng == "dve"
    drain_inc = eng in ("dve", "act")

    def run(e):
        for fn, waits, inc, incn in plan.ops[eng]:
            for s, v in waits:
                e.wait_ge(sems[s], v)
            ins = fn(e)
            if drain_all:
                d = e.drain()
                if inc is not None:
                    d.then_inc(sems[inc], incn)
            elif inc is not None:
                if drain_inc:
                    e.drain().then_inc(sems[inc], incn)
                else:
                    ins.then_inc(sems[inc], incn)
    return run


def _run_blocks(nc, plans, sems):
    for plan in plans:
        with nc.Block() as b:
            if plan.ops["sp"]:
                b.sync(_replay(plan, sems, "sp"))
            if plan.ops["pl"]:
                b.gpsimd(_replay(plan, sems, "pl"))
            if plan.ops["te"]:
                b.tensor(_replay(plan, sems, "te"))
            if plan.ops["dve"]:
                b.vector(_replay(plan, sems, "dve"))
            if plan.ops["act"]:
                b.scalar(_replay(plan, sems, "act"))


def build(stage, debug=False):
    nc = bass.Bass("TRN2", target_bir_lowering=False)
    dk = "ExternalOutput" if debug else "Internal"

    d_x = nc.dram_tensor("x", [S, Dm], F32, kind="ExternalInput")
    d_wqA = nc.dram_tensor("wqA", [H, Dm, DH], F32, kind="ExternalInput")
    d_wkA = nc.dram_tensor("wkA", [H, Dm, DH], F32, kind="ExternalInput")
    d_wvA = nc.dram_tensor("wvA", [H, Dm, DH], F32, kind="ExternalInput")
    d_woA = nc.dram_tensor("woA", [H, DH, Dm], F32, kind="ExternalInput")
    d_bq = nc.dram_tensor("bq", [H, DH], F32, kind="ExternalInput")
    d_bk = nc.dram_tensor("bk", [H, DH], F32, kind="ExternalInput")
    d_bv = nc.dram_tensor("bv", [H, DH], F32, kind="ExternalInput")
    d_bo = nc.dram_tensor("bo", [1, Dm], F32, kind="ExternalInput")
    d_wqB = nc.dram_tensor("wqB", [Dm, Dm], F32, kind="ExternalInput")
    d_wkB = nc.dram_tensor("wkB", [Dm, Dm], F32, kind="ExternalInput")
    d_wvB = nc.dram_tensor("wvB", [Dm, Dm], F32, kind="ExternalInput")
    d_woB = nc.dram_tensor("woB", [Dm, Dm], F32, kind="ExternalInput")
    d_w1 = nc.dram_tensor("w1", [HFD, Dm], F32, kind="ExternalInput")
    d_w2 = nc.dram_tensor("w2", [HFD, Dm], F32, kind="ExternalInput")
    d_w3 = nc.dram_tensor("w3", [Dm, HFD], F32, kind="ExternalInput")
    d_cI = nc.dram_tensor("cI", [128, 128], BF16, kind="ExternalInput")
    d_cShift = nc.dram_tensor("cShift", [128, 128], BF16, kind="ExternalInput")
    d_cTri = nc.dram_tensor("cTri", [128, 128], BF16, kind="ExternalInput")
    d_cuL = nc.dram_tensor("cuL", [1, 128], BF16, kind="ExternalInput")
    d_cvL = nc.dram_tensor("cvL", [1, 128], BF16, kind="ExternalInput")
    d_cuR = nc.dram_tensor("cuR", [1, 128], BF16, kind="ExternalInput")
    d_cvR = nc.dram_tensor("cvR", [1, 128], BF16, kind="ExternalInput")
    d_cOnes1 = nc.dram_tensor("cOnes1", [1, 128], BF16, kind="ExternalInput")
    d_cOnesC = nc.dram_tensor("cOnesC", [128, 1], F32, kind="ExternalInput")
    d_cOnesRF = nc.dram_tensor("cOnesRF", [1, 128], F32, kind="ExternalInput")

    d_y = nc.dram_tensor("y", [S, Dm], F32, kind="ExternalOutput")
    d_xres = nc.dram_tensor("xres", [S, Dm], F32, kind=dk)
    d_xln = nc.dram_tensor("xln", [S, Dm], BF16, kind=dk)
    d_z = nc.dram_tensor("zb", [NT, H, DH, 128], BF16, kind=dk)
    d_hid = nc.dram_tensor("hid", [NHF, 4, 128, 512], BF16, kind=dk)
    d_w1qc = nc.dram_tensor("w1qc", [HFD, Dm], BF16, kind=dk)
    d_w2qc = nc.dram_tensor("w2qc", [HFD, Dm], BF16, kind=dk)
    d_w3qc = nc.dram_tensor("w3qc", [Dm, HFD], BF16, kind=dk)
    d_dbg = None
    if debug:
        d_dbg = nc.dram_tensor("dbg", [NT, 128, 2048], BF16, kind="ExternalOutput")
        d_dbg2 = nc.dram_tensor("dbg2", [NT, 128, 8], F32, kind="ExternalOutput")

    A = nc.alloc_sbuf_tensor
    big1 = A("big1", [128, ND * S], BF16)
    big2 = A("big2", [128, H * Dm], BF16)
    wbuf = A("wbuf", [128, 12288], BF16)
    qkv = A("qkv", [128, 6144], BF16)
    xz = A("xz", [128, 8192], BF16)
    fscr = A("fscr", [128, 4096], BF16)
    anx = A("anx", [128, 3072], BF16)
    xlnst = A("xlnst", [128, 4096], BF16)
    I_sb = A("I_sb", [128, 128], BF16)
    shA = A("shA", [128, 128], BF16)
    triB = A("triB", [128, 128], BF16)
    uL = A("uL", [1, 128], BF16)
    vL = A("vL", [1, 128], BF16)
    uR = A("uR", [1, 128], BF16)
    vR = A("vR", [1, 128], BF16)
    ones1 = A("ones1", [1, 128], BF16)
    onesC = A("onesC", [128, 1], F32)
    onesRF = A("onesRF", [1, 128], F32)
    bnat = wbuf[0:16, 0:384]
    bosb = qkv[0:1, 4096:6144]
    bqkv = A("bqkv", [128, 48], F32)
    cvals = A("cvals", [128, 2], F32)
    stats = A("stats", [128, 64], F32)
    gacc = A("gacc", [128, 4], F32)
    thrbc = A("thrbc", [128, 8], F32)

    xrow = xz.ap().bitcast(F32)
    wf32 = wbuf.ap().bitcast(F32)
    fscr32 = fscr.ap().bitcast(F32)
    accv = big2.ap().bitcast(F32)
    gstage = xlnst.ap().bitcast(F32)

    PS_A = nc.alloc_psum_tensor("psA", [128, 2048], F32)
    PS_B = nc.alloc_psum_tensor("psB", [128, 2048], F32)

    sems = {}
    import contextlib
    _stack = contextlib.ExitStack()
    for name in ("s_te", "s_dve", "s_act", "s_sp", "s_pl", "s_st", "s_q"):
        sems[name] = _stack.enter_context(nc.semaphore(name))

    cnt = {}
    known = {e: {} for e in ENGS}
    plans = []

    def NP():
        p = Plan(cnt, known)
        plans.append(p)
        return p

    def st(i):
        return stats[:, i:i + 1]

    NEGMAX, SUMS, RECIP, GTMP = 0, 1, 2, 3
    SUMX, SSQ, NEGM, MSQ, VR = 0, 1, 2, 3, 4

    def ln_chain(P, xr_ap, out_bf_ap, par, wait_first=(), war_apply=()):
        b = 8 + 8 * par

        def s(i):
            return stats[:, b + i:b + i + 1]
        P.emit("dve", lambda e, a=xr_ap, s0=s(SUMX): e.reduce_sum(s0, a, axis=AX.X),
               waits=wait_first)
        P.emit("act", lambda e, a=xr_ap, s1=s(SSQ): e.activation(
            fscr32[:, 0:2048], a, AF.Square, accum_out=s1),
            waits=list(wait_first) + list(ln_chain.act_war[par]), inc="s_act")
        a_sq = P.v("s_act")
        P.emit("dve", lambda e, s0=s(SUMX), s2=s(NEGM): e.tensor_scalar(
            s2, s0, -1.0 / Dm, None, OP.mult))
        P.emit("dve", lambda e, s2=s(NEGM), s3=s(MSQ): e.tensor_mul(s3, s2, s2),
               waits=[("s_act", a_sq)])
        P.emit("dve", lambda e, s1=s(SSQ), s3=s(MSQ), s4=s(VR): e.scalar_tensor_tensor(
            s4, s1, 1.0 / Dm, s3, OP.mult, OP.subtract),
            waits=list(ln_chain.var_war[par]), inc="s_dve")
        a_var = P.v("s_dve")
        P.emit("act", lambda e, s4=s(VR): e.activation(
            s4, s4, AF.Ln, bias=cvals[:, 0:1]), waits=[("s_dve", a_var)])
        P.emit("act", lambda e, s4=s(VR): e.activation(
            s4, s4, AF.Exp, scale=cvals[:, 1:2]), inc="s_act")
        a_rstd = P.v("s_act")
        P.emit("dve", lambda e, a=xr_ap, o=out_bf_ap, s2=s(NEGM), s4=s(VR): e.tensor_scalar(
            o, a, s2, s4, OP.add, OP.mult),
            waits=[("s_act", a_rstd)] + list(war_apply), inc="s_dve")
        dv = P.v("s_dve")
        ln_chain.act_war[par] = [("s_dve", dv)]
        ln_chain.var_war[par] = [("s_act", a_rstd)]
        return dv
    ln_chain.act_war = {0: [], 1: []}
    ln_chain.var_war = {0: [], 1: []}

    bst = {"xln_st": 0, "big1_te": 0, "wbuf_te": 0, "sp_xlnt": 0}

    def emit_xlnt_treads():
        P = plans[-1]
        w = [("s_st", bst["xln_st"]), ("s_te", bst["big1_te"])]
        for dc in range(ND):
            P.emit("sp", lambda e, dc=dc: e.dma_start_transpose(
                big1[:, dc * S:(dc + 1) * S], d_xln.ap()[:, dc * 128:(dc + 1) * 128]),
                waits=w if dc == 0 else (), inc="s_sp", incn=16)
        bst["sp_xlnt"] = P.v("s_sp")

    def gamma_chunk_list():
        out = []
        for mi, dt_ in ((0, d_w1), (1, d_w2)):
            for hf in range(NHF):
                out.append((mi, dt_.ap()[hf * 128:(hf + 1) * 128, :], 2048))
        for dc in range(ND):
            for c0, w in ((0, 2048), (2048, 2048), (4096, 1536)):
                out.append((2, d_w3.ap()[dc * 128:(dc + 1) * 128, c0:c0 + w], w))
        return out

    P = NP()
    for dst, src in ((I_sb, d_cI), (shA, d_cShift), (triB, d_cTri),
                     (uL, d_cuL), (vL, d_cvL), (uR, d_cuR), (vR, d_cvR),
                     (ones1, d_cOnes1), (onesC, d_cOnesC), (onesRF, d_cOnesRF)):
        P.emit("sp", lambda e, d=dst, s=src: e.dma_start(d[:], s.ap()),
               inc="s_sp", incn=16)
    for j, src in enumerate((d_bq, d_bk, d_bv)):
        P.emit("pl", lambda e, j=j, s=src: e.dma_start(
            bnat[:, j * 128:(j + 1) * 128], s.ap()), inc="s_pl", incn=16)
    pl_b = P.v("s_pl")
    sp_c = P.v("s_sp")
    P.emit("dve", lambda e: e.memset(gacc[:], 0.0))
    P.emit("dve", lambda e: e.memset(cvals[:, 0:1], 1e-5))
    P.emit("dve", lambda e: e.memset(cvals[:, 1:2], -0.5), inc="s_dve")
    for j in range(3):
        P.emit("te", lambda e, j=j: e.matmul(
            PS_B[:, j * 16:(j + 1) * 16], bnat[:, j * 128:(j + 1) * 128],
            I_sb[0:16, 0:16], start=True, stop=True),
            waits=[("s_pl", pl_b), ("s_sp", sp_c)],
            inc="s_te" if j == 2 else None)
    te_b = P.v("s_te")
    P.emit("dve", lambda e: e.tensor_copy(bqkv[:], PS_B[:, 0:48]),
           waits=[("s_te", te_b)], inc="s_dve")

    dve_apply = [0] * NT
    st_xln = [0] * NT
    for t in range(NT):
        par = t % 2
        xa = xrow[:, par * 2048:(par + 1) * 2048]
        wload = [("s_dve", dve_apply[t - 2])] if t >= 2 else []
        P.emit("sp", lambda e, t=t, a=xa: e.dma_start(
            a, d_x.ap()[t * 128:(t + 1) * 128, :]), waits=wload,
            inc="s_sp", incn=16)
        sp_t = P.v("s_sp")
        xo = xlnst[:, par * 2048:(par + 1) * 2048]
        war = [("s_st", st_xln[t - 2])] if t >= 2 else []
        dv = ln_chain(P, xa, xo, par, wait_first=[("s_sp", sp_t)], war_apply=war)
        dve_apply[t] = dv
        P.emit("pl", lambda e, t=t, o=xo: e.dma_start(
            d_xln.ap()[t * 128:(t + 1) * 128, :], o), waits=[("s_dve", dv)],
            inc="s_st", incn=16)
        st_xln[t] = P.v("s_st")
    bst["xln_st"] = P.v("s_st")
    bst["wbuf_te"] = te_b

    def attn_sublayer(variant, resid_src, out_dst, do_ln_next, do_gamma):
        emit_xlnt_treads()
        prevP = plans[-1]
        P = NP()
        sp_xlnt = bst["sp_xlnt"]

        gam = gamma_chunk_list() if do_gamma else []
        gam_i = [0]
        gam_dve = {0: 0, 1: 0}

        def emit_gamma(P):
            if not do_gamma or gam_i[0] >= len(gam):
                return
            i = gam_i[0]
            mi, gap, gw_ = gam[i]
            gdst = gstage[:, 0:gw_]
            P.emit("sp", lambda e, d=gdst, a=gap: e.dma_start(d, a),
                   waits=[("s_dve", gam_dve[0])], inc="s_sp", incn=16)
            sp_g = P.v("s_sp")
            P.emit("dve", lambda e, d=gdst: e.reduce_sum(
                st(GTMP), d, axis=AX.X, apply_absolute_value=True),
                waits=[("s_sp", sp_g)])
            P.emit("dve", lambda e, mi=mi: e.tensor_add(
                gacc[:, mi:mi + 1], gacc[:, mi:mi + 1], st(GTMP)), inc="s_dve")
            gam_dve[0] = P.v("s_dve")
            gam_i[0] += 1

        wo_pl = []
        wq_pl = []
        te_head_end = []
        dve_z_end = []
        st_z_end = []
        act_head_end = 0

        def w_cast(e, h, j):
            par = h % 2
            dst = wbuf[:, (par * 3 + j) * 2048:(par * 3 + j) * 2048 + 2048]
            if variant == "a":
                t = (d_wqA, d_wkA, d_wvA)[j]
                ap = t.ap()[h].rearrange("(dc p) e -> p dc e", p=128)
            else:
                t = (d_wqB, d_wkB, d_wvB)[j]
                ap = t.ap()[:, h * 128:(h + 1) * 128].rearrange(
                    "(dc p) e -> p dc e", p=128)
            return e.dma_start(dst.rearrange("p (dc e) -> p dc e", dc=ND), ap)

        def emit_wq(h, war, plan):
            for j in range(3):
                plan.emit("pl", lambda e, h=h, j=j: w_cast(e, h, j),
                          waits=war if j == 0 else (), inc="s_pl", incn=16)
            wq_pl.append(plan.v("s_pl"))

        def emit_wo(h, plan):
            if variant == "a":
                woap = d_woA.ap()[h]
            else:
                woap = d_woB.ap()[h * 128:(h + 1) * 128, :]
            plan.emit("pl", lambda e, h=h, a=woap: e.dma_start(
                big2[:, h * Dm:(h + 1) * Dm], a), inc="s_pl", incn=16)
            wo_pl.append(plan.v("s_pl"))

        def emit_wcasts(h, war):
            emit_wq(h, war, P)
            emit_wo(h, P)

        emit_wq(0, [("s_te", bst["wbuf_te"])], prevP)
        emit_wq(1, [], prevP)
        emit_wo(0, P)
        emit_wo(1, P)


        Ksb = qkv[:, 0:2048]
        Qsb = qkv[:, 2048:4096]
        Vsb = qkv[:, 4096:6144]
        vfm = fscr[:, 0:2048]

        for h in range(H):
            par = h % 2
            wq_s = wbuf[:, (par * 3 + 0) * 2048:(par * 3 + 0) * 2048 + 2048]
            wk_s = wbuf[:, (par * 3 + 1) * 2048:(par * 3 + 1) * 2048 + 2048]
            wv_s = wbuf[:, (par * 3 + 2) * 2048:(par * 3 + 2) * 2048 + 2048]

            wts = [("s_pl", wq_pl[h]), ("s_sp", sp_xlnt)]
            if h > 0:
                wts += [("s_act", act_head_end), ("s_dve", dve_z_end[h - 1])]
            for sti in range(4):
                for dc in range(ND):
                    last = (sti == 3 and dc == ND - 1)
                    P.emit("te", lambda e, sti=sti, dc=dc, w=wk_s: e.matmul(
                        PS_A[:, sti * 512:(sti + 1) * 512],
                        w[:, dc * 128:(dc + 1) * 128],
                        big1[:, dc * S + sti * 512: dc * S + (sti + 1) * 512],
                        start=(dc == 0), stop=(dc == ND - 1)),
                        waits=wts if (sti == 0 and dc == 0) else (),
                        inc="s_te" if last else None)
            te_k = P.v("s_te")
            if variant == "a":
                P.emit("dve", lambda e, h=h: e.tensor_scalar(
                    Ksb, PS_A[:, :], bqkv[:, 16 + h:16 + h + 1], None, OP.add),
                    waits=[("s_te", te_k)], inc="s_dve")
            else:
                P.emit("dve", lambda e: e.tensor_copy(Ksb, PS_A[:, :]),
                       waits=[("s_te", te_k)], inc="s_dve")
            dve_k = P.v("s_dve")
            emit_gamma(P)
            qwts = [("s_dve", dve_z_end[h - 1])] if h > 0 else []
            for sti in range(4):
                for dc in range(ND):
                    last = (sti == 3 and dc == ND - 1)
                    P.emit("te", lambda e, sti=sti, dc=dc, w=wq_s: e.matmul(
                        PS_B[:, sti * 512:(sti + 1) * 512],
                        w[:, dc * 128:(dc + 1) * 128],
                        big1[:, dc * S + sti * 512: dc * S + (sti + 1) * 512],
                        start=(dc == 0), stop=(dc == ND - 1)),
                        waits=qwts if (sti == 0 and dc == 0) else (),
                        inc="s_te" if last else None)
            te_q = P.v("s_te")
            if variant == "a":
                P.emit("dve", lambda e, h=h: e.tensor_scalar(
                    Qsb, PS_B[:, :], bqkv[:, h:h + 1], INV_SCALE, OP.add, OP.mult),
                    waits=[("s_te", te_q)], inc="s_dve")
            else:
                P.emit("dve", lambda e: e.tensor_scalar(
                    Qsb, PS_B[:, :], INV_SCALE, None, OP.mult),
                    waits=[("s_te", te_q)], inc="s_dve")
            dve_q = P.v("s_dve")
            emit_gamma(P)
            for sti in range(4):
                for dc in range(ND):
                    last = (sti == 3 and dc == ND - 1)
                    P.emit("te", lambda e, sti=sti, dc=dc, w=wv_s: e.matmul(
                        PS_A[:, sti * 512:(sti + 1) * 512],
                        w[:, dc * 128:(dc + 1) * 128],
                        big1[:, dc * S + sti * 512: dc * S + (sti + 1) * 512],
                        start=(dc == 0), stop=(dc == ND - 1)),
                        waits=[("s_dve", dve_k)] if (sti == 0 and dc == 0) else (),
                        inc="s_te" if last else None)
            te_v = P.v("s_te")
            P.emit("dve", lambda e: e.tensor_copy(vfm, PS_A[:, :]),
                   waits=[("s_te", te_v)], inc="s_dve")
            dve_vf = P.v("s_dve")
            for tt in range(NT):
                P.emit("te", lambda e, tt=tt: e.matmul(
                    PS_B[:, tt * 128:(tt + 1) * 128],
                    vfm[:, tt * 128:(tt + 1) * 128], I_sb[:],
                    start=True, stop=True),
                    waits=[("s_dve", dve_q), ("s_dve", dve_vf)] if tt == 0 else (),
                    inc="s_te" if tt == NT - 1 else None)
            te_vt = P.v("s_te")
            P.emit("dve", lambda e: e.tensor_copy(Vsb, PS_B[:, :]),
                   waits=[("s_te", te_vt)], inc="s_dve")
            dve_vt = P.v("s_dve")

            act_exp_of = [0] * NT
            te_at_of = [0] * NT
            te_z_of = [0] * NT
            dve_diag_of = [0] * NT

            def kts_of(qt):
                if variant == "a":
                    return list(range(qt + 1))
                return [k for k in (qt - 1, qt, qt + 1) if 0 <= k < NT]

            def base_of(qt):
                par = qt % 2
                if variant == "a":
                    return (par * 1024) if qt < 8 else 0
                return par * 512

            def stage1(qt):
                par = qt % 2
                kts = kts_of(qt)
                kw = len(kts) * 128
                base = base_of(qt)
                dslot = 2048 + par * 128
                if variant == "a" and qt >= 8:
                    scw = [("s_act", act_exp_of[qt - 1])]
                elif qt >= 2:
                    scw = [("s_act", act_exp_of[qt - 2])]
                else:
                    scw = []
                if qt <= 1:
                    scw = scw + [("s_dve", dve_vt)]
                if variant == "a":
                    nfull = (qt * 128) // 512
                    for j in range(nfull):
                        P.emit("te", lambda e, qt=qt, j=j, base=base: e.matmul(
                            PS_A[:, base + j * 512:base + (j + 1) * 512],
                            Qsb[:, qt * 128:(qt + 1) * 128],
                            Ksb[:, j * 512:(j + 1) * 512],
                            start=True, stop=True),
                            waits=scw if j == 0 else ())
                    rem0 = nfull * 512
                    remw = kw - rem0
                    P.emit("te", lambda e, qt=qt, rem0=rem0, remw=remw, base=base: e.matmul(
                        PS_A[:, base + rem0:base + rem0 + remw],
                        Qsb[:, qt * 128:(qt + 1) * 128],
                        Ksb[:, rem0:rem0 + remw],
                        start=True, stop=True),
                        waits=scw if nfull == 0 else ())
                    P.emit("te", lambda e, qt=qt, base=base: e.matmul(
                        PS_A[:, base + qt * 128:base + (qt + 1) * 128],
                        shA[:], triB[:],
                        start=False, stop=True, skip_group_check=True),
                        inc="s_te")
                else:
                    k0 = kts[0] * 128
                    P.emit("te", lambda e, qt=qt, k0=k0, kw=kw, base=base: e.matmul(
                        PS_A[:, base:base + kw],
                        Qsb[:, qt * 128:(qt + 1) * 128], Ksb[:, k0:k0 + kw],
                        start=True, stop=True), waits=scw)
                    nmask = sum(1 for kt in kts if kt in (qt - 1, qt + 1))
                    mdone = 0
                    for i, kt in enumerate(kts):
                        if kt == qt - 1:
                            uu, vv = uL, vL
                        elif kt == qt + 1:
                            uu, vv = uR, vR
                        else:
                            continue
                        mdone += 1
                        P.emit("te", lambda e, i=i, uu=uu, vv=vv, base=base: e.matmul(
                            PS_A[:, base + i * 128:base + (i + 1) * 128],
                            uu[:], vv[:],
                            start=False, stop=True, skip_group_check=True),
                            inc="s_te" if mdone == nmask else None)
                te_sc = P.v("s_te")
                P.emit("dve", lambda e, kw=kw, base=base: e.reduce_max(
                    st(NEGMAX), PS_A[:, base:base + kw], axis=AX.X, negate=True),
                    waits=[("s_te", te_sc)], inc="s_dve")
                dve_nm = P.v("s_dve")
                if variant == "a" and qt >= 8:
                    anx_war = te_at_of[qt - 1]
                else:
                    anx_war = te_at_of[qt - 2] if qt >= 2 else 0
                P.emit("act", lambda e, kw=kw, base=base: e.activation(
                    anx[:, base:base + kw], PS_A[:, base:base + kw], AF.Exp,
                    bias=st(NEGMAX), accum_out=st(SUMS)),
                    waits=[("s_dve", dve_nm), ("s_te", anx_war)],
                    inc="s_act")
                act_exp_of[qt] = P.v("s_act")
                if USE_ACCUM_SUMS:
                    P.emit("dve", lambda e: e.reciprocal(st(RECIP), st(SUMS)),
                           waits=[("s_act", act_exp_of[qt])])
                else:
                    P.emit("dve", lambda e, kw=kw, base=base: e.reduce_sum(
                        st(SUMS), anx[:, base:base + kw], axis=AX.X),
                        waits=[("s_act", act_exp_of[qt])])
                    P.emit("dve", lambda e: e.reciprocal(st(RECIP), st(SUMS)))
                P.emit("dve", lambda e, dslot=dslot: e.tensor_scalar(
                    fscr[:, dslot:dslot + 128], I_sb[:], st(RECIP), None, OP.mult),
                    waits=[("s_te", anx_war)], inc="s_dve")
                dve_diag_of[qt] = P.v("s_dve")

            def stage2(qt):
                par = qt % 2
                kts = kts_of(qt)
                kw = len(kts) * 128
                base = base_of(qt)
                base_t = par * 512
                base_e = 2048 + par * 512
                dslot = 2048 + par * 128
                zslot = 1024 + par * 128
                n_k = len(kts)
                dve_attnT = 0
                for g0 in range(0, n_k, 4):
                    gkts = kts[g0:g0 + 4]
                    ng = len(gkts)
                    for i in range(ng):
                        P.emit("te", lambda e, i=i, g0=g0, base=base, base_t=base_t, dslot=dslot: e.matmul(
                            PS_B[:, base_t + i * 128:base_t + (i + 1) * 128],
                            anx[:, base + (g0 + i) * 128:base + (g0 + i + 1) * 128],
                            fscr[:, dslot:dslot + 128], start=True, stop=True),
                            waits=([("s_dve", dve_diag_of[qt]), ("s_dve", dve_attnT)]
                                   if i == 0 else ()),
                            inc="s_te" if i == ng - 1 else None)
                    te_at = P.v("s_te")
                    P.emit("dve", lambda e, gw=ng * 128, base_t=base_t, base_e=base_e: e.tensor_copy(
                        anx[:, base_e:base_e + gw], PS_B[:, base_t:base_t + gw]),
                        waits=[("s_te", te_at)], inc="s_dve")
                    dve_attnT = P.v("s_dve")
                    for i, kt in enumerate(gkts):
                        glast = (g0 + i == n_k - 1)
                        gfirst = (g0 == 0 and i == 0)
                        P.emit("te", lambda e, i=i, kt=kt, zs=zslot, be=base_e, st_=gfirst, sp_=glast: e.matmul(
                            PS_B[:, zs:zs + 128],
                            Vsb[:, kt * 128:(kt + 1) * 128],
                            anx[:, be + i * 128:be + (i + 1) * 128],
                            start=st_, stop=sp_),
                            waits=([("s_dve", dve_attnT)] if i == 0 else ()),
                            inc="s_te" if glast else None)
                te_at_of[qt] = te_at
                te_z = P.v("s_te")
                te_z_of[qt] = te_z
                zdst = xz[:, (h % 2) * 2048 + qt * 128:(h % 2) * 2048 + (qt + 1) * 128]
                zwar = [("s_st", st_z_end[h - 2])] if h >= 2 else []
                if variant == "a":
                    P.emit("dve", lambda e, h=h, zd=zdst, zs=zslot: e.tensor_scalar(
                        zd, PS_B[:, zs:zs + 128], bqkv[:, 32 + h:32 + h + 1],
                        None, OP.add),
                        waits=[("s_te", te_z)] + zwar, inc="s_dve")
                else:
                    P.emit("dve", lambda e, zd=zdst, zs=zslot: e.tensor_copy(
                        zd, PS_B[:, zs:zs + 128]),
                        waits=[("s_te", te_z)] + zwar, inc="s_dve")
                dve_zev = P.v("s_dve")
                P.emit("pl", lambda e, qt=qt, h=h, zd=zdst: e.dma_start(
                    d_z.ap()[qt, h], zd), waits=[("s_dve", dve_zev)],
                    inc="s_st", incn=16)
                if (h * NT + qt) % 2 == 0:
                    emit_gamma(P)

            npipe = 8 if variant == "a" else NT
            stage1(0)
            for qt in range(1, npipe):
                stage1(qt)
                stage2(qt - 1)
            stage2(npipe - 1)
            for qt in range(npipe, NT):
                stage1(qt)
                stage2(qt)
            act_head_end = P.v("s_act")
            dve_z_end.append(P.v("s_dve"))
            st_z_end.append(P.v("s_st"))
            te_head_end.append(P.v("s_te"))
            if h + 2 < H:
                emit_wcasts(h + 2, [("s_te", te_head_end[h])])

        while do_gamma and gam_i[0] < len(gam):
            emit_gamma(P)
        if do_gamma:
            dve_g = P.v("s_dve")
            P.emit("te", lambda e: e.matmul(
                PS_B[0:1, 0:3], onesC[:], gacc[:, 0:3], start=True, stop=True),
                waits=[("s_dve", dve_g), ("s_dve", dve_z_end[-1])], inc="s_te")
            te_g = P.v("s_te")
            P.emit("dve", lambda e: e.tensor_scalar(
                stats[0:1, 32:35], PS_B[0:1, 0:3], 0.5 / NWELEM, 0.5e-8,
                OP.mult, OP.add), waits=[("s_te", te_g)], inc="s_dve")
            dve_t = P.v("s_dve")
            P.emit("te", lambda e: e.matmul(
                PS_B[:, 16:19], onesRF[:], stats[0:1, 32:35],
                start=True, stop=True), waits=[("s_dve", dve_t)], inc="s_te")
            te_t = P.v("s_te")
            P.emit("dve", lambda e: e.tensor_copy(thrbc[:, 0:3], PS_B[:, 16:19]),
                   waits=[("s_te", te_t)])
            P.emit("dve", lambda e: e.tensor_scalar(
                thrbc[:, 4:7], thrbc[:, 0:3], -1.0, None, OP.mult), inc="s_dve")
        bst["big1_te"] = te_head_end[-1]
        bst["wbuf_te"] = te_head_end[-1]

        P = NP()
        pl_bo = 0
        if variant == "a":
            P.emit("pl", lambda e: e.dma_start(bosb, d_bo.ap()),
                   inc="s_pl", incn=16)
            pl_bo = P.v("s_pl")
        te_end = [0] * NT
        dve_add_l = [0] * NT
        dve_ln_l = [0] * NT
        st_q = [0] * NT
        for qt in range(NT):
            par = qt % 2
            zt = qkv[:, par * 2048:(par + 1) * 2048]
            xa = xrow[:, par * 2048:(par + 1) * 2048]
            wz = [("s_te", te_end[qt - 2])] if qt >= 2 else []
            P.emit("sp", lambda e, qt=qt, d=zt: e.dma_start(
                d.rearrange("p (hh e) -> p hh e", hh=H),
                d_z.ap()[qt].rearrange("hh p e -> p hh e")),
                waits=wz, inc="s_sp", incn=16)
            wx = []
            if qt >= 2:
                wx = [("s_st", st_q[qt - 2]), ("s_dve", dve_ln_l[qt - 2])]
            P.emit("sp", lambda e, qt=qt, d=xa: e.dma_start(
                d, resid_src.ap()[qt * 128:(qt + 1) * 128, :]),
                waits=wx, inc="s_sp", incn=16)
            sp_q = P.v("s_sp")
            PS = PS_A if par == 0 else PS_B
            wts0 = [("s_sp", sp_q), ("s_pl", wo_pl[-1])]
            if qt >= 2:
                wts0.append(("s_dve", dve_add_l[qt - 2]))
            lastb = (variant != "a")
            for hh in range(H):
                for j in range(4):
                    last = (hh == H - 1 and j == 3)
                    P.emit("te", lambda e, hh=hh, j=j, z=zt, PS=PS: e.matmul(
                        PS[:, j * 512:(j + 1) * 512],
                        z[:, hh * 128:(hh + 1) * 128],
                        big2[:, hh * Dm + j * 512: hh * Dm + (j + 1) * 512],
                        start=(hh == 0), stop=(hh == H - 1 and lastb)),
                        waits=wts0 if (hh == 0 and j == 0) else (),
                        inc="s_te" if (last and lastb) else None)
            if variant == "a":
                for j in range(4):
                    P.emit("te", lambda e, j=j, PS=PS: e.matmul(
                        PS[:, j * 512:(j + 1) * 512], ones1[:],
                        bosb[0:1, j * 512:(j + 1) * 512],
                        start=False, stop=True, skip_group_check=True),
                        waits=[("s_pl", pl_bo)] if j == 0 else (),
                        inc="s_te" if j == 3 else None)
            te_end[qt] = P.v("s_te")
            P.emit("dve", lambda e, xa=xa, PS=PS: e.tensor_add(xa, xa, PS[:, :]),
                   waits=[("s_te", te_end[qt])], inc="s_dve")
            dve_add_l[qt] = P.v("s_dve")
            P.emit("pl", lambda e, qt=qt, xa=xa, dst=out_dst: e.dma_start(
                dst.ap()[qt * 128:(qt + 1) * 128, :], xa),
                waits=[("s_dve", dve_add_l[qt])], inc="s_st", incn=16)
            if do_ln_next:
                xo = xlnst[:, par * 2048:(par + 1) * 2048]
                war = [("s_st", st_q[qt - 2])] if qt >= 2 else []
                dv = ln_chain(P, xa, xo, par,
                              wait_first=[("s_dve", dve_add_l[qt])],
                              war_apply=war)
                dve_ln_l[qt] = dv
                P.emit("pl", lambda e, qt=qt, o=xo: e.dma_start(
                    d_xln.ap()[qt * 128:(qt + 1) * 128, :], o),
                    waits=[("s_dve", dv)], inc="s_st", incn=16)
            else:
                dve_ln_l[qt] = dve_add_l[qt]
            st_q[qt] = P.v("s_st")
        bst["xln_st"] = P.v("s_st")

    def ffn_sublayer(first, resid_src, out_dst, do_ln_next):
        SUP = 4
        nsup = NHF // SUP
        emit_xlnt_treads()
        P = NP()
        sp_xlnt = bst["sp_xlnt"]

        qstate = {"par": 0, "dq": {0: 0, 1: 0}, "last_st": 0}

        def quantize_chunk(P, src_ap, dst_ap, mi, width, q="pl"):
            par_f = qstate["par"]
            qstate["par"] ^= 1
            fa = wf32[:, par_f * 2048:par_f * 2048 + width]
            P.emit("pl", lambda e, d=fa, a=src_ap: e.dma_start(d, a),
                   waits=[("s_dve", qstate["dq"][par_f])], inc="s_q", incn=16)
            sp_w = P.v("s_q")
            pbuf = fscr[:, 2048:2048 + width]
            nbuf = xlnst[:, 0:width]
            qbuf = qkv[:, 4096:4096 + width]
            P.emit("dve", lambda e, f=fa, p=pbuf, mi=mi: e.tensor_single_scalar(
                p, f, thrbc[:, mi:mi + 1], OP.is_gt),
                waits=[("s_q", sp_w)])
            P.emit("dve", lambda e, f=fa, n=nbuf, mi=mi: e.tensor_single_scalar(
                n, f, thrbc[:, 4 + mi:4 + mi + 1], OP.is_lt))
            P.emit("dve", lambda e, p=pbuf, n=nbuf, q=qbuf: e.tensor_sub(q, p, n),
                   waits=[("s_st", qstate["last_st"])], inc="s_dve")
            dq = P.v("s_dve")
            qstate["dq"][par_f] = dq
            P.emit("pl", lambda e, d=dst_ap, q=qbuf: e.dma_start(d, q),
                   waits=[("s_dve", dq)], inc="s_st", incn=16)
            qstate["last_st"] = P.v("s_st")
            return qstate["last_st"]

        w3chunks = []
        for dc in range(ND):
            for c0, w in ((0, 2048), (2048, 2048), (4096, 1536)):
                w3chunks.append((dc, c0, w))
        w3_i = [0]

        def emit_w3_quant(P):
            if not first or w3_i[0] >= len(w3chunks):
                return
            dc, c0, w = w3chunks[w3_i[0]]
            quantize_chunk(P, d_w3.ap()[dc * 128:(dc + 1) * 128, c0:c0 + w],
                           d_w3qc.ap()[dc * 128:(dc + 1) * 128, c0:c0 + w], 2, w,
                           q="sp")
            w3_i[0] += 1

        st_after_q = [0] * nsup

        def quantize_super(s_):
            for hf in range(s_ * SUP, (s_ + 1) * SUP):
                quantize_chunk(P, d_w1.ap()[hf * 128:(hf + 1) * 128, :],
                               d_w1qc.ap()[hf * 128:(hf + 1) * 128, :], 0, 2048)
                quantize_chunk(P, d_w2.ap()[hf * 128:(hf + 1) * 128, :],
                               d_w2qc.ap()[hf * 128:(hf + 1) * 128, :], 1, 2048)
            st_after_q[s_] = P.v("s_st")

        if first and FFN_PART >= 1:
            quantize_super(0)
            quantize_super(1)

        sp_tread = [0] * nsup
        te_super_done = [0] * nsup
        act_sig = 0
        dve_hid = 0
        st_hid = [0] * NHF

        def emit_treads(s_):
            war = [("s_te", te_super_done[s_ - 2])] if s_ >= 2 else []
            if first:
                war.append(("s_st", st_after_q[s_]))
            par = s_ % 2
            base = par * 16384
            for mi, qc in ((0, d_w1qc), (1, d_w2qc)):
                for dc in range(ND):
                    P.emit("sp", lambda e, qc=qc, s_=s_, dc=dc, mi=mi, base=base: e.dma_start_transpose(
                        big2[:, base + mi * 8192 + dc * 512: base + mi * 8192 + dc * 512 + 512],
                        qc.ap()[s_ * 512:(s_ + 1) * 512, dc * 128:(dc + 1) * 128]),
                        waits=war if (mi == 0 and dc == 0) else (),
                        inc="s_sp", incn=16)
            sp_tread[s_] = P.v("s_sp")

        if FFN_PART >= 2:
            emit_treads(0)
        for s_ in range(nsup if FFN_PART >= 2 else 0):
            par = s_ % 2
            base = par * 16384
            if first and FFN_PART >= 1 and s_ + 2 < nsup:
                quantize_super(s_ + 2)
            if s_ + 1 < nsup:
                emit_treads(s_ + 1)
            for ii in range(SUP):
                hf = s_ * SUP + ii
                for mi, PS in ((0, PS_A), (1, PS_B)):
                    wts = [("s_sp", sp_tread[s_]), ("s_sp", sp_xlnt)]
                    if hf > 0:
                        wts.append(("s_act", act_sig) if mi == 0 else
                                   ("s_dve", dve_hid))
                    for sti in range(4):
                        for dc in range(ND):
                            last = (sti == 3 and dc == ND - 1)
                            P.emit("te", lambda e, mi=mi, PS=PS, sti=sti, dc=dc, base=base, ii=ii: e.matmul(
                                PS[:, sti * 512:(sti + 1) * 512],
                                big2[:, base + mi * 8192 + dc * 512 + ii * 128: base + mi * 8192 + dc * 512 + ii * 128 + 128],
                                big1[:, dc * S + sti * 512: dc * S + (sti + 1) * 512],
                                start=(dc == 0), stop=(dc == ND - 1)),
                                waits=wts if (sti == 0 and dc == 0) else (),
                                inc="s_te" if last else None)
                te_hf = P.v("s_te")
                s1 = qkv[:, 0:2048]
                hidb = qkv[:, 2048:4096]
                P.emit("act", lambda e: e.activation(s1, PS_A[:, :], AF.Sigmoid),
                       waits=[("s_te", te_hf - 1), ("s_dve", dve_hid)],
                       inc="s_act")
                act_sig = P.v("s_act")
                hwar = [("s_st", st_hid[hf - 1])] if hf >= 1 else []
                P.emit("dve", lambda e: e.tensor_mul(hidb, s1, PS_B[:, :]),
                       waits=[("s_act", act_sig), ("s_te", te_hf)] + hwar,
                       inc="s_dve")
                dve_hid = P.v("s_dve")
                P.emit("pl", lambda e, hf=hf: e.dma_start(
                    d_hid.ap()[hf].rearrange("st p f -> p st f"),
                    hidb.rearrange("p (st f) -> p st f", st=4)),
                    waits=[("s_dve", dve_hid)], inc="s_st", incn=16)
                st_hid[hf] = P.v("s_st")
                emit_w3_quant(P)
            te_super_done[s_] = P.v("s_te")
        while first and FFN_PART >= 1 and w3_i[0] < len(w3chunks):
            emit_w3_quant(P)
        st_w3_done = P.v("s_st")
        st_hid_done = st_hid[-1]

        GRP = 6
        ngrp = (NHF + GRP - 1) // GRP

        def emit_group_loads(Pl, half, g, war):
            lo = g * GRP
            hi = min(NHF, lo + GRP)
            n = hi - lo
            gpar = g % 2
            for i in range(n):
                hf = lo + i
                Pl.emit("sp", lambda e, hf=hf, i=i, gpar=gpar: e.dma_start_transpose(
                    big1[:, gpar * 12288 + i * 2048: gpar * 12288 + (i + 1) * 2048],
                    d_w3qc.ap()[:, hf * 128:(hf + 1) * 128]),
                    waits=war if i == 0 else (), inc="s_sp", incn=16)
            for i in range(n):
                hf = lo + i
                Pl.emit("sp", lambda e, hf=hf, i=i, gpar=gpar, half=half: e.dma_start(
                    wbuf[:, gpar * 6144 + i * 1024: gpar * 6144 + (i + 1) * 1024].rearrange("p (st f) -> p st f", st=2),
                    d_hid.ap()[hf][half * 2:half * 2 + 2].rearrange("st p f -> p st f")),
                    inc="s_sp", incn=16)
            return Pl.v("s_sp")

        pref = {}
        if FFN_PART >= 21:
            pref[(0, 0)] = emit_group_loads(
                P, 0, 0, [("s_te", te_super_done[-1] if FFN_PART >= 2 else 0),
                          ("s_st", P.v("s_st")), ("s_dve", P.v("s_dve"))])
        for half in range(2 if FFN_PART >= 21 else 0):
            prev_plan = plans[-1]
            P = NP()
            te_grp_done = [0] * ngrp
            dve_acc = [0] * 8
            last_ps = {0: 0, 1: 0}
            for g in range(ngrp):
                lo = g * GRP
                hi = min(NHF, lo + GRP)
                n = hi - lo
                gpar = g % 2
                if (half, g) in pref:
                    sp_g = pref[(half, g)]
                else:
                    war = [("s_te", te_grp_done[g - 2])] if g >= 2 else []
                    sp_g = emit_group_loads(P, half, g, war)
                for sc in range(8 if FFN_PART >= 25 else 0):
                    PS = PS_A if sc % 2 == 0 else PS_B
                    for i in range(n):
                        for j in range(4):
                            last = (i == n - 1 and j == 3)
                            P.emit("te", lambda e, i=i, j=j, sc=sc, gpar=gpar, PS=PS, n=n: e.matmul(
                                PS[:, j * 512:(j + 1) * 512],
                                wbuf[:, gpar * 6144 + i * 1024 + sc * 128: gpar * 6144 + i * 1024 + sc * 128 + 128],
                                big1[:, gpar * 12288 + i * 2048 + j * 512: gpar * 12288 + i * 2048 + (j + 1) * 512],
                                start=(i == 0), stop=(i == n - 1)),
                                waits=([("s_sp", sp_g), ("s_dve", last_ps[sc % 2])]
                                       if (i == 0 and j == 0) else ()),
                                inc="s_te" if last else None)
                    te_sc = P.v("s_te")
                    acc = accv[:, sc * 2048:(sc + 1) * 2048]
                    if g == 0:
                        P.emit("dve", lambda e, acc=acc, PS=PS: e.tensor_copy(
                            acc, PS[:, :]), waits=[("s_te", te_sc)], inc="s_dve")
                    else:
                        P.emit("dve", lambda e, acc=acc, PS=PS: e.tensor_add(
                            acc, acc, PS[:, :]), waits=[("s_te", te_sc)], inc="s_dve")
                    dve_acc[sc] = P.v("s_dve")
                    last_ps[sc % 2] = dve_acc[sc]
                te_grp_done[g] = P.v("s_te")
            st_sc = [0] * 8
            dve_ln_sc = [0] * 8
            for sc in range(8 if FFN_PART >= 30 else 0):
                t = half * 8 + sc
                par = sc % 2
                xa = xrow[:, par * 2048:(par + 1) * 2048]
                wx = []
                if sc >= 2:
                    wx = [("s_st", st_sc[sc - 2]), ("s_dve", dve_ln_sc[sc - 2])]
                P.emit("sp", lambda e, t=t, d=xa: e.dma_start(
                    d, resid_src.ap()[t * 128:(t + 1) * 128, :]),
                    waits=wx, inc="s_sp", incn=16)
                sp_x = P.v("s_sp")
                acc = accv[:, sc * 2048:(sc + 1) * 2048]
                P.emit("dve", lambda e, xa=xa, acc=acc: e.tensor_add(xa, xa, acc),
                       waits=[("s_sp", sp_x), ("s_dve", dve_acc[sc])], inc="s_dve")
                dve_add = P.v("s_dve")
                P.emit("pl", lambda e, t=t, xa=xa, dst=out_dst: e.dma_start(
                    dst.ap()[t * 128:(t + 1) * 128, :], xa),
                    waits=[("s_dve", dve_add)], inc="s_st", incn=16)
                if do_ln_next:
                    xo = xlnst[:, par * 2048:(par + 1) * 2048]
                    war = [("s_st", st_sc[sc - 2])] if sc >= 2 else []
                    dv = ln_chain(P, xa, xo, par,
                                  wait_first=[("s_dve", dve_add)],
                                  war_apply=war)
                    dve_ln_sc[sc] = dv
                    P.emit("pl", lambda e, t=t, o=xo: e.dma_start(
                        d_xln.ap()[t * 128:(t + 1) * 128, :], o),
                        waits=[("s_dve", dv)], inc="s_st", incn=16)
                else:
                    dve_ln_sc[sc] = dve_add
                st_sc[sc] = P.v("s_st")
            if half == 0 and FFN_PART >= 21:
                pref[(1, 0)] = emit_group_loads(
                    P, 1, 0, [("s_te", te_grp_done[ngrp - 2])])
            if half == 1:
                bst["xln_st"] = P.v("s_st")
                bst["big1_te"] = te_grp_done[ngrp - 1]
                bst["wbuf_te"] = te_grp_done[ngrp - 1]

    attn_sublayer("a", d_x, d_xres if stage > 1 else d_y,
                  do_ln_next=(stage > 1), do_gamma=(stage > 1))
    if stage > 1:
        ffn_sublayer(True, d_xres, d_xres if stage > 2 else d_y,
                     do_ln_next=(stage > 2))
    if stage > 2:
        attn_sublayer("b", d_xres, d_xres if stage > 3 else d_y,
                      do_ln_next=(stage > 3), do_gamma=False)
    if stage > 3:
        ffn_sublayer(False, d_xres, d_y, do_ln_next=False)

    P = NP()
    total_st = cnt.get("s_st", 0)
    P.emit("pl", lambda e: e.engine_nop(), waits=[("s_st", total_st)])

    _run_blocks(nc, plans, sems)
    return nc


def make_consts():
    cI = np.eye(128, dtype=np.float32)
    cShift = np.eye(128, k=-1, dtype=np.float32)
    cTri = np.triu(np.ones((128, 128), np.float32)) * NEG_CAUSAL
    uLr = np.zeros((1, 128), np.float32); uLr[0, 64:] = 1.0
    vLr = np.zeros((1, 128), np.float32); vLr[0, :64] = NEG_BAND
    uRr = np.zeros((1, 128), np.float32); uRr[0, :64] = 1.0
    vRr = np.zeros((1, 128), np.float32); vRr[0, 64:] = NEG_BAND
    b = lambda a: np.ascontiguousarray(a.astype(BF))
    return {
        "cI": b(cI), "cShift": b(cShift), "cTri": b(cTri),
        "cuL": b(uLr), "cvL": b(vLr), "cuR": b(uRr), "cvR": b(vRr),
        "cOnes1": b(np.ones((1, 128), np.float32)),
        "cOnesC": np.ones((128, 1), np.float32),
        "cOnesRF": np.ones((1, 128), np.float32),
    }


def kernel(L0_attn_a_x, L0_attn_a_W_Q, L0_attn_a_W_K, L0_attn_a_W_V,
           L0_attn_a_W_O, L0_attn_a_b_Q, L0_attn_a_b_K, L0_attn_a_b_V,
           L0_attn_a_b_O, L0_attn_b_Wq, L0_attn_b_Wk, L0_attn_b_Wv,
           L0_attn_b_Wo, L0_ffn_w1, L0_ffn_w2, L0_ffn_w3):
    consts = make_consts()
    shared = {
        "wqA": np.ascontiguousarray(L0_attn_a_W_Q, np.float32),
        "wkA": np.ascontiguousarray(L0_attn_a_W_K, np.float32),
        "wvA": np.ascontiguousarray(L0_attn_a_W_V, np.float32),
        "woA": np.ascontiguousarray(L0_attn_a_W_O, np.float32),
        "bq": np.ascontiguousarray(L0_attn_a_b_Q, np.float32),
        "bk": np.ascontiguousarray(L0_attn_a_b_K, np.float32),
        "bv": np.ascontiguousarray(L0_attn_a_b_V, np.float32),
        "bo": np.ascontiguousarray(L0_attn_a_b_O, np.float32).reshape(1, Dm),
        "wqB": np.ascontiguousarray(L0_attn_b_Wq, np.float32),
        "wkB": np.ascontiguousarray(L0_attn_b_Wk, np.float32),
        "wvB": np.ascontiguousarray(L0_attn_b_Wv, np.float32),
        "woB": np.ascontiguousarray(L0_attn_b_Wo, np.float32),
        "w1": np.ascontiguousarray(L0_ffn_w1, np.float32),
        "w2": np.ascontiguousarray(L0_ffn_w2, np.float32),
        "w3": np.ascontiguousarray(L0_ffn_w3, np.float32),
        **consts,
    }
    x = np.ascontiguousarray(L0_attn_a_x, np.float32)
    in_maps = [dict(shared, x=x[c]) for c in range(8)]
    nc = build(STAGE)
    res = run_bass_kernel_spmd(nc, in_maps, core_ids=list(range(8)))
    return np.stack([np.asarray(res.results[c]["y"], np.float32)
                     for c in range(8)], axis=0)


# version 06
# speedup vs baseline: 1.1743x; 1.0205x over previous
import numpy as np
import ml_dtypes
import concourse.bass as bass
import concourse.mybir as mybir
from concourse.bass_utils import run_bass_kernel_spmd
from concourse.alu_op_type import AluOpType as OP

F32 = mybir.dt.float32
BF16 = mybir.dt.bfloat16
AF = mybir.ActivationFunctionType
AX = mybir.AxisListType
BF = ml_dtypes.bfloat16

S = 2048
Dm = 2048
H = 16
DH = 128
HFD = 5632
NT = S // 128
ND = Dm // 128
NHF = HFD // 128
NWELEM = float(HFD * Dm)
INV_SCALE = float(1.0 / np.sqrt(np.float32(128.0)))
NEG_CAUSAL = -100000.0
NEG_BAND = -1.0e30

STAGE = 4
USE_ACCUM_SUMS = True
PARANOID = False
PARANOID_ENGS = ("te", "dve", "act")
PARANOID_MODE = "full"
FFN_PART = 30

ENGS = ("te", "dve", "act", "sp", "pl")


class Plan:

    def __init__(self, cnt, known):
        self.ops = {k: [] for k in ENGS}
        self.cnt = cnt
        self.known = known

    def emit(self, eng, fn, waits=(), inc=None, incn=1):
        k = self.known[eng]
        if PARANOID and eng in PARANOID_ENGS:
            if PARANOID_MODE in ("full", "waits"):
                waits = list(waits) + [(s, self.cnt.get(s, 0))
                                       for s in ("s_te", "s_dve", "s_act")]
            if inc is None and PARANOID_MODE in ("full", "incs"):
                inc = {"te": "s_te", "dve": "s_dve", "act": "s_act"}[eng]
        real_waits = []
        for s, v in waits:
            if v is None or v <= 0:
                continue
            if k.get(s, 0) >= v:
                continue
            real_waits.append((s, v))
            k[s] = v
        self.ops[eng].append((fn, real_waits, inc, incn))
        if inc is not None:
            self.cnt[inc] = self.cnt.get(inc, 0) + incn
            if eng not in ("sp", "pl"):
                k[inc] = self.cnt[inc]
        return self.cnt.get(inc)

    def v(self, sem):
        return self.cnt.get(sem, 0)


def _replay(plan, sems, eng):
    drain_all = eng == "dve"
    drain_inc = eng in ("dve", "act")

    def run(e):
        for fn, waits, inc, incn in plan.ops[eng]:
            for s, v in waits:
                e.wait_ge(sems[s], v)
            ins = fn(e)
            if drain_all:
                d = e.drain()
                if inc is not None:
                    d.then_inc(sems[inc], incn)
            elif inc is not None:
                if drain_inc:
                    e.drain().then_inc(sems[inc], incn)
                else:
                    ins.then_inc(sems[inc], incn)
    return run


def _run_blocks(nc, plans, sems):
    for plan in plans:
        with nc.Block() as b:
            if plan.ops["sp"]:
                b.sync(_replay(plan, sems, "sp"))
            if plan.ops["pl"]:
                b.gpsimd(_replay(plan, sems, "pl"))
            if plan.ops["te"]:
                b.tensor(_replay(plan, sems, "te"))
            if plan.ops["dve"]:
                b.vector(_replay(plan, sems, "dve"))
            if plan.ops["act"]:
                b.scalar(_replay(plan, sems, "act"))


def build(stage, debug=False):
    nc = bass.Bass("TRN2", target_bir_lowering=False)
    dk = "ExternalOutput" if debug else "Internal"

    d_x = nc.dram_tensor("x", [S, Dm], F32, kind="ExternalInput")
    d_wqA = nc.dram_tensor("wqA", [H, Dm, DH], F32, kind="ExternalInput")
    d_wkA = nc.dram_tensor("wkA", [H, Dm, DH], F32, kind="ExternalInput")
    d_wvA = nc.dram_tensor("wvA", [H, Dm, DH], F32, kind="ExternalInput")
    d_woA = nc.dram_tensor("woA", [H, DH, Dm], F32, kind="ExternalInput")
    d_bq = nc.dram_tensor("bq", [H, DH], F32, kind="ExternalInput")
    d_bk = nc.dram_tensor("bk", [H, DH], F32, kind="ExternalInput")
    d_bv = nc.dram_tensor("bv", [H, DH], F32, kind="ExternalInput")
    d_bo = nc.dram_tensor("bo", [1, Dm], F32, kind="ExternalInput")
    d_wqB = nc.dram_tensor("wqB", [Dm, Dm], F32, kind="ExternalInput")
    d_wkB = nc.dram_tensor("wkB", [Dm, Dm], F32, kind="ExternalInput")
    d_wvB = nc.dram_tensor("wvB", [Dm, Dm], F32, kind="ExternalInput")
    d_woB = nc.dram_tensor("woB", [Dm, Dm], F32, kind="ExternalInput")
    d_w1 = nc.dram_tensor("w1", [HFD, Dm], F32, kind="ExternalInput")
    d_w2 = nc.dram_tensor("w2", [HFD, Dm], F32, kind="ExternalInput")
    d_w3 = nc.dram_tensor("w3", [Dm, HFD], F32, kind="ExternalInput")
    d_cI = nc.dram_tensor("cI", [128, 128], BF16, kind="ExternalInput")
    d_cShift = nc.dram_tensor("cShift", [128, 128], BF16, kind="ExternalInput")
    d_cTri = nc.dram_tensor("cTri", [128, 128], BF16, kind="ExternalInput")
    d_cuL = nc.dram_tensor("cuL", [1, 128], BF16, kind="ExternalInput")
    d_cvL = nc.dram_tensor("cvL", [1, 128], BF16, kind="ExternalInput")
    d_cuR = nc.dram_tensor("cuR", [1, 128], BF16, kind="ExternalInput")
    d_cvR = nc.dram_tensor("cvR", [1, 128], BF16, kind="ExternalInput")
    d_cOnes1 = nc.dram_tensor("cOnes1", [1, 128], BF16, kind="ExternalInput")
    d_cOnesC = nc.dram_tensor("cOnesC", [128, 1], F32, kind="ExternalInput")
    d_cOnesRF = nc.dram_tensor("cOnesRF", [1, 128], F32, kind="ExternalInput")

    d_y = nc.dram_tensor("y", [S, Dm], F32, kind="ExternalOutput")
    d_xres = nc.dram_tensor("xres", [S, Dm], F32, kind=dk)
    d_xln = nc.dram_tensor("xln", [S, Dm], BF16, kind=dk)
    d_z = nc.dram_tensor("zb", [NT, H, DH, 128], BF16, kind=dk)
    d_hid = nc.dram_tensor("hid", [NHF, 4, 128, 512], BF16, kind=dk)
    d_w1qc = nc.dram_tensor("w1qc", [HFD, Dm], BF16, kind=dk)
    d_w2qc = nc.dram_tensor("w2qc", [HFD, Dm], BF16, kind=dk)
    d_w3qc = nc.dram_tensor("w3qc", [Dm, HFD], BF16, kind=dk)
    d_dbg = None
    if debug:
        d_dbg = nc.dram_tensor("dbg", [NT, 128, 2048], BF16, kind="ExternalOutput")
        d_dbg2 = nc.dram_tensor("dbg2", [NT, 128, 8], F32, kind="ExternalOutput")

    A = nc.alloc_sbuf_tensor
    big1 = A("big1", [128, ND * S], BF16)
    big2 = A("big2", [128, H * Dm], BF16)
    wbuf = A("wbuf", [128, 12288], BF16)
    qkv = A("qkv", [128, 6144], BF16)
    xz = A("xz", [128, 8192], BF16)
    fscr = A("fscr", [128, 4096], BF16)
    anx = A("anx", [128, 3072], BF16)
    xlnst = A("xlnst", [128, 4096], BF16)
    I_sb = A("I_sb", [128, 128], BF16)
    shA = A("shA", [128, 128], BF16)
    triB = A("triB", [128, 128], BF16)
    uL = A("uL", [1, 128], BF16)
    vL = A("vL", [1, 128], BF16)
    uR = A("uR", [1, 128], BF16)
    vR = A("vR", [1, 128], BF16)
    ones1 = A("ones1", [1, 128], BF16)
    onesC = A("onesC", [128, 1], F32)
    onesRF = A("onesRF", [1, 128], F32)
    bnat = wbuf[0:16, 0:384]
    bosb = qkv[0:1, 4096:6144]
    bqkv = A("bqkv", [128, 48], F32)
    cvals = A("cvals", [128, 2], F32)
    stats = A("stats", [128, 64], F32)
    gacc = A("gacc", [128, 4], F32)
    thrbc = A("thrbc", [128, 8], F32)

    xrow = xz.ap().bitcast(F32)
    wf32 = wbuf.ap().bitcast(F32)
    fscr32 = fscr.ap().bitcast(F32)
    accv = big2.ap().bitcast(F32)
    gstage = xlnst.ap().bitcast(F32)

    PS_A = nc.alloc_psum_tensor("psA", [128, 2048], F32)
    PS_B = nc.alloc_psum_tensor("psB", [128, 2048], F32)

    sems = {}
    import contextlib
    _stack = contextlib.ExitStack()
    for name in ("s_te", "s_dve", "s_act", "s_sp", "s_pl", "s_st", "s_q"):
        sems[name] = _stack.enter_context(nc.semaphore(name))

    cnt = {}
    known = {e: {} for e in ENGS}
    plans = []

    def NP():
        p = Plan(cnt, known)
        plans.append(p)
        return p

    def st(i):
        return stats[:, i:i + 1]

    NEGMAX, SUMS, RECIP, GTMP = 0, 1, 2, 3
    SUMX, SSQ, NEGM, MSQ, VR = 0, 1, 2, 3, 4

    def ln_chain(P, xr_ap, out_bf_ap, par, wait_first=(), war_apply=()):
        b = 8 + 8 * par

        def s(i):
            return stats[:, b + i:b + i + 1]
        P.emit("dve", lambda e, a=xr_ap, s0=s(SUMX): e.reduce_sum(s0, a, axis=AX.X),
               waits=wait_first)
        P.emit("act", lambda e, a=xr_ap, s1=s(SSQ): e.activation(
            fscr32[:, 0:2048], a, AF.Square, accum_out=s1),
            waits=list(wait_first) + list(ln_chain.act_war[par]), inc="s_act")
        a_sq = P.v("s_act")
        P.emit("dve", lambda e, s0=s(SUMX), s2=s(NEGM): e.tensor_scalar(
            s2, s0, -1.0 / Dm, None, OP.mult))
        P.emit("dve", lambda e, s2=s(NEGM), s3=s(MSQ): e.tensor_mul(s3, s2, s2),
               waits=[("s_act", a_sq)])
        P.emit("dve", lambda e, s1=s(SSQ), s3=s(MSQ), s4=s(VR): e.scalar_tensor_tensor(
            s4, s1, 1.0 / Dm, s3, OP.mult, OP.subtract),
            waits=list(ln_chain.var_war[par]), inc="s_dve")
        a_var = P.v("s_dve")
        P.emit("act", lambda e, s4=s(VR): e.activation(
            s4, s4, AF.Ln, bias=cvals[:, 0:1]), waits=[("s_dve", a_var)])
        P.emit("act", lambda e, s4=s(VR): e.activation(
            s4, s4, AF.Exp, scale=cvals[:, 1:2]), inc="s_act")
        a_rstd = P.v("s_act")
        P.emit("dve", lambda e, a=xr_ap, o=out_bf_ap, s2=s(NEGM), s4=s(VR): e.tensor_scalar(
            o, a, s2, s4, OP.add, OP.mult),
            waits=[("s_act", a_rstd)] + list(war_apply), inc="s_dve")
        dv = P.v("s_dve")
        ln_chain.act_war[par] = [("s_dve", dv)]
        ln_chain.var_war[par] = [("s_act", a_rstd)]
        return dv
    ln_chain.act_war = {0: [], 1: []}
    ln_chain.var_war = {0: [], 1: []}

    bst = {"xln_st": 0, "big1_te": 0, "wbuf_te": 0, "sp_xlnt": 0}

    def emit_xlnt_treads():
        P = plans[-1]
        w = [("s_st", bst["xln_st"]), ("s_te", bst["big1_te"])]
        for dc in range(ND):
            P.emit("sp", lambda e, dc=dc: e.dma_start_transpose(
                big1[:, dc * S:(dc + 1) * S], d_xln.ap()[:, dc * 128:(dc + 1) * 128]),
                waits=w if dc == 0 else (), inc="s_sp", incn=16)
        bst["sp_xlnt"] = P.v("s_sp")

    def gamma_chunk_list():
        out = []
        for mi, dt_ in ((0, d_w1), (1, d_w2)):
            for hf in range(NHF):
                out.append((mi, dt_.ap()[hf * 128:(hf + 1) * 128, :], 2048))
        for dc in range(ND):
            for c0, w in ((0, 2048), (2048, 2048), (4096, 1536)):
                out.append((2, d_w3.ap()[dc * 128:(dc + 1) * 128, c0:c0 + w], w))
        return out

    P = NP()
    for dst, src in ((I_sb, d_cI), (shA, d_cShift), (triB, d_cTri),
                     (uL, d_cuL), (vL, d_cvL), (uR, d_cuR), (vR, d_cvR),
                     (ones1, d_cOnes1), (onesC, d_cOnesC), (onesRF, d_cOnesRF)):
        P.emit("sp", lambda e, d=dst, s=src: e.dma_start(d[:], s.ap()),
               inc="s_sp", incn=16)
    for j, src in enumerate((d_bq, d_bk, d_bv)):
        P.emit("pl", lambda e, j=j, s=src: e.dma_start(
            bnat[:, j * 128:(j + 1) * 128], s.ap()), inc="s_pl", incn=16)
    pl_b = P.v("s_pl")
    sp_c = P.v("s_sp")
    P.emit("dve", lambda e: e.memset(gacc[:], 0.0))
    P.emit("dve", lambda e: e.memset(cvals[:, 0:1], 1e-5))
    P.emit("dve", lambda e: e.memset(cvals[:, 1:2], -0.5), inc="s_dve")
    for j in range(3):
        P.emit("te", lambda e, j=j: e.matmul(
            PS_B[:, j * 16:(j + 1) * 16], bnat[:, j * 128:(j + 1) * 128],
            I_sb[0:16, 0:16], start=True, stop=True),
            waits=[("s_pl", pl_b), ("s_sp", sp_c)],
            inc="s_te" if j == 2 else None)
    te_b = P.v("s_te")
    P.emit("dve", lambda e: e.tensor_copy(bqkv[:], PS_B[:, 0:48]),
           waits=[("s_te", te_b)], inc="s_dve")

    dve_apply = [0] * NT
    st_xln = [0] * NT
    for t in range(NT):
        par = t % 2
        xa = xrow[:, par * 2048:(par + 1) * 2048]
        wload = [("s_dve", dve_apply[t - 2])] if t >= 2 else []
        P.emit("sp", lambda e, t=t, a=xa: e.dma_start(
            a, d_x.ap()[t * 128:(t + 1) * 128, :]), waits=wload,
            inc="s_sp", incn=16)
        sp_t = P.v("s_sp")
        xo = xlnst[:, par * 2048:(par + 1) * 2048]
        war = [("s_st", st_xln[t - 2])] if t >= 2 else []
        dv = ln_chain(P, xa, xo, par, wait_first=[("s_sp", sp_t)], war_apply=war)
        dve_apply[t] = dv
        P.emit("pl", lambda e, t=t, o=xo: e.dma_start(
            d_xln.ap()[t * 128:(t + 1) * 128, :], o), waits=[("s_dve", dv)],
            inc="s_st", incn=16)
        st_xln[t] = P.v("s_st")
    bst["xln_st"] = P.v("s_st")
    bst["wbuf_te"] = te_b

    def attn_sublayer(variant, resid_src, out_dst, do_ln_next, do_gamma):
        emit_xlnt_treads()
        prevP = plans[-1]
        P = NP()
        sp_xlnt = bst["sp_xlnt"]

        gam = gamma_chunk_list() if do_gamma else []
        gam_i = [0]
        gam_dve = {0: 0, 1: 0}

        def emit_gamma(P):
            if not do_gamma or gam_i[0] >= len(gam):
                return
            i = gam_i[0]
            mi, gap, gw_ = gam[i]
            gdst = gstage[:, 0:gw_]
            P.emit("sp", lambda e, d=gdst, a=gap: e.dma_start(d, a),
                   waits=[("s_dve", gam_dve[0])], inc="s_sp", incn=16)
            sp_g = P.v("s_sp")
            P.emit("dve", lambda e, d=gdst: e.reduce_sum(
                st(GTMP), d, axis=AX.X, apply_absolute_value=True),
                waits=[("s_sp", sp_g)])
            P.emit("dve", lambda e, mi=mi: e.tensor_add(
                gacc[:, mi:mi + 1], gacc[:, mi:mi + 1], st(GTMP)), inc="s_dve")
            gam_dve[0] = P.v("s_dve")
            gam_i[0] += 1

        wo_pl = []
        wq_pl = []
        te_head_end = []
        dve_z_end = []
        st_z_end = []
        act_head_end = 0

        def w_cast(e, h, j):
            par = h % 2
            dst = wbuf[:, (par * 3 + j) * 2048:(par * 3 + j) * 2048 + 2048]
            if variant == "a":
                t = (d_wqA, d_wkA, d_wvA)[j]
                ap = t.ap()[h].rearrange("(dc p) e -> p dc e", p=128)
            else:
                t = (d_wqB, d_wkB, d_wvB)[j]
                ap = t.ap()[:, h * 128:(h + 1) * 128].rearrange(
                    "(dc p) e -> p dc e", p=128)
            return e.dma_start(dst.rearrange("p (dc e) -> p dc e", dc=ND), ap)

        def emit_wq(h, war, plan):
            for j in range(3):
                plan.emit("pl", lambda e, h=h, j=j: w_cast(e, h, j),
                          waits=war if j == 0 else (), inc="s_pl", incn=16)
            wq_pl.append(plan.v("s_pl"))

        def emit_wo(h, plan):
            if variant == "a":
                woap = d_woA.ap()[h]
            else:
                woap = d_woB.ap()[h * 128:(h + 1) * 128, :]
            plan.emit("pl", lambda e, h=h, a=woap: e.dma_start(
                big2[:, h * Dm:(h + 1) * Dm], a), inc="s_pl", incn=16)
            wo_pl.append(plan.v("s_pl"))

        def emit_wcasts(h, war):
            emit_wq(h, war, P)
            emit_wo(h, P)

        emit_wq(0, [("s_te", bst["wbuf_te"])], prevP)
        emit_wq(1, [], prevP)
        emit_wo(0, P)
        emit_wo(1, P)


        Ksb = qkv[:, 0:2048]
        Qsb = qkv[:, 2048:4096]
        Vsb = qkv[:, 4096:6144]
        vfm = fscr[:, 0:2048]

        for h in range(H):
            par = h % 2
            wq_s = wbuf[:, (par * 3 + 0) * 2048:(par * 3 + 0) * 2048 + 2048]
            wk_s = wbuf[:, (par * 3 + 1) * 2048:(par * 3 + 1) * 2048 + 2048]
            wv_s = wbuf[:, (par * 3 + 2) * 2048:(par * 3 + 2) * 2048 + 2048]

            wts = [("s_pl", wq_pl[h]), ("s_sp", sp_xlnt)]
            if h > 0:
                wts += [("s_act", act_head_end), ("s_dve", dve_z_end[h - 1])]
            for sti in range(4):
                for dc in range(ND):
                    last = (sti == 3 and dc == ND - 1)
                    P.emit("te", lambda e, sti=sti, dc=dc, w=wk_s: e.matmul(
                        PS_A[:, sti * 512:(sti + 1) * 512],
                        w[:, dc * 128:(dc + 1) * 128],
                        big1[:, dc * S + sti * 512: dc * S + (sti + 1) * 512],
                        start=(dc == 0), stop=(dc == ND - 1)),
                        waits=wts if (sti == 0 and dc == 0) else (),
                        inc="s_te" if last else None)
            te_k = P.v("s_te")
            if variant == "a":
                P.emit("dve", lambda e, h=h: e.tensor_scalar(
                    Ksb, PS_A[:, :], bqkv[:, 16 + h:16 + h + 1], None, OP.add),
                    waits=[("s_te", te_k)], inc="s_dve")
            else:
                P.emit("dve", lambda e: e.tensor_copy(Ksb, PS_A[:, :]),
                       waits=[("s_te", te_k)], inc="s_dve")
            dve_k = P.v("s_dve")
            emit_gamma(P)
            qwts = ([("s_dve", dve_z_end[h - 1]), ("s_act", act_head_end)]
                    if h > 0 else [])
            for sti in range(4):
                for dc in range(ND):
                    last = (sti == 3 and dc == ND - 1)
                    P.emit("te", lambda e, sti=sti, dc=dc, w=wq_s: e.matmul(
                        PS_B[:, sti * 512:(sti + 1) * 512],
                        w[:, dc * 128:(dc + 1) * 128],
                        big1[:, dc * S + sti * 512: dc * S + (sti + 1) * 512],
                        start=(dc == 0), stop=(dc == ND - 1)),
                        waits=qwts if (sti == 0 and dc == 0) else (),
                        inc="s_te" if last else None)
            te_q = P.v("s_te")
            if variant == "a":
                P.emit("dve", lambda e, h=h: e.tensor_scalar(
                    Qsb, PS_B[:, :], bqkv[:, h:h + 1], INV_SCALE, OP.add, OP.mult),
                    waits=[("s_te", te_q)], inc="s_dve")
            else:
                P.emit("dve", lambda e: e.tensor_scalar(
                    Qsb, PS_B[:, :], INV_SCALE, None, OP.mult),
                    waits=[("s_te", te_q)], inc="s_dve")
            dve_q = P.v("s_dve")
            emit_gamma(P)
            for sti in range(4):
                for dc in range(ND):
                    last = (sti == 3 and dc == ND - 1)
                    P.emit("te", lambda e, sti=sti, dc=dc, w=wv_s: e.matmul(
                        PS_A[:, sti * 512:(sti + 1) * 512],
                        w[:, dc * 128:(dc + 1) * 128],
                        big1[:, dc * S + sti * 512: dc * S + (sti + 1) * 512],
                        start=(dc == 0), stop=(dc == ND - 1)),
                        waits=[("s_dve", dve_k)] if (sti == 0 and dc == 0) else (),
                        inc="s_te" if last else None)
            te_v = P.v("s_te")
            P.emit("dve", lambda e: e.tensor_copy(vfm, PS_A[:, :]),
                   waits=[("s_te", te_v)], inc="s_dve")
            dve_vf = P.v("s_dve")
            for tt in range(NT):
                P.emit("te", lambda e, tt=tt: e.matmul(
                    PS_B[:, tt * 128:(tt + 1) * 128],
                    vfm[:, tt * 128:(tt + 1) * 128], I_sb[:],
                    start=True, stop=True),
                    waits=[("s_dve", dve_q), ("s_dve", dve_vf)] if tt == 0 else (),
                    inc="s_te" if tt == NT - 1 else None)
            te_vt = P.v("s_te")
            P.emit("dve", lambda e: e.tensor_copy(Vsb, PS_B[:, :]),
                   waits=[("s_te", te_vt)], inc="s_dve")
            dve_vt = P.v("s_dve")

            act_exp_of = [0] * NT
            te_at_of = [0] * NT
            te_z_of = [0] * NT
            dve_diag_of = [0] * NT

            def kts_of(qt):
                if variant == "a":
                    return list(range(qt + 1))
                return [k for k in (qt - 1, qt, qt + 1) if 0 <= k < NT]

            def base_of(qt):
                par = qt % 2
                if variant == "a":
                    return (par * 1024) if qt < 8 else 0
                return par * 512

            def stage1(qt):
                par = qt % 2
                kts = kts_of(qt)
                kw = len(kts) * 128
                base = base_of(qt)
                dslot = 2048 + par * 128
                if variant == "a" and qt >= 8:
                    scw = [("s_act", act_exp_of[qt - 1])]
                elif qt >= 2:
                    scw = [("s_act", act_exp_of[qt - 2])]
                else:
                    scw = []
                if qt <= 1:
                    scw = scw + [("s_dve", dve_vt)]
                if variant == "a":
                    nfull = (qt * 128) // 512
                    for j in range(nfull):
                        P.emit("te", lambda e, qt=qt, j=j, base=base: e.matmul(
                            PS_A[:, base + j * 512:base + (j + 1) * 512],
                            Qsb[:, qt * 128:(qt + 1) * 128],
                            Ksb[:, j * 512:(j + 1) * 512],
                            start=True, stop=True),
                            waits=scw if j == 0 else ())
                    rem0 = nfull * 512
                    remw = kw - rem0
                    P.emit("te", lambda e, qt=qt, rem0=rem0, remw=remw, base=base: e.matmul(
                        PS_A[:, base + rem0:base + rem0 + remw],
                        Qsb[:, qt * 128:(qt + 1) * 128],
                        Ksb[:, rem0:rem0 + remw],
                        start=True, stop=True),
                        waits=scw if nfull == 0 else ())
                    P.emit("te", lambda e, qt=qt, base=base: e.matmul(
                        PS_A[:, base + qt * 128:base + (qt + 1) * 128],
                        shA[:], triB[:],
                        start=False, stop=True, skip_group_check=True),
                        inc="s_te")
                else:
                    k0 = kts[0] * 128
                    P.emit("te", lambda e, qt=qt, k0=k0, kw=kw, base=base: e.matmul(
                        PS_A[:, base:base + kw],
                        Qsb[:, qt * 128:(qt + 1) * 128], Ksb[:, k0:k0 + kw],
                        start=True, stop=True), waits=scw)
                    nmask = sum(1 for kt in kts if kt in (qt - 1, qt + 1))
                    mdone = 0
                    for i, kt in enumerate(kts):
                        if kt == qt - 1:
                            uu, vv = uL, vL
                        elif kt == qt + 1:
                            uu, vv = uR, vR
                        else:
                            continue
                        mdone += 1
                        P.emit("te", lambda e, i=i, uu=uu, vv=vv, base=base: e.matmul(
                            PS_A[:, base + i * 128:base + (i + 1) * 128],
                            uu[:], vv[:],
                            start=False, stop=True, skip_group_check=True),
                            inc="s_te" if mdone == nmask else None)
                te_sc = P.v("s_te")
                P.emit("dve", lambda e, kw=kw, base=base: e.reduce_max(
                    st(NEGMAX), PS_A[:, base:base + kw], axis=AX.X, negate=True),
                    waits=[("s_te", te_sc)], inc="s_dve")
                dve_nm = P.v("s_dve")
                if variant == "a" and qt >= 8:
                    anx_war = te_at_of[qt - 1]
                else:
                    anx_war = te_at_of[qt - 2] if qt >= 2 else 0
                P.emit("act", lambda e, kw=kw, base=base: e.activation(
                    anx[:, base:base + kw], PS_A[:, base:base + kw], AF.Exp,
                    bias=st(NEGMAX), accum_out=st(SUMS)),
                    waits=[("s_dve", dve_nm), ("s_te", anx_war)],
                    inc="s_act")
                act_exp_of[qt] = P.v("s_act")
                P.emit("dve", lambda e: e.reciprocal(st(RECIP), st(SUMS)),
                       waits=[("s_act", act_exp_of[qt])])
                P.emit("dve", lambda e, dslot=dslot: e.tensor_scalar(
                    fscr[:, dslot:dslot + 128], I_sb[:], st(RECIP), None, OP.mult),
                    waits=[("s_te", anx_war)], inc="s_dve")
                dve_diag_of[qt] = P.v("s_dve")

            def stage2(qt):
                par = qt % 2
                kts = kts_of(qt)
                kw = len(kts) * 128
                base = base_of(qt)
                base_t = par * 512
                base_e = 2048 + par * 512
                dslot = 2048 + par * 128
                zslot = 1024 + par * 128
                n_k = len(kts)
                dve_attnT = 0
                for g0 in range(0, n_k, 4):
                    gkts = kts[g0:g0 + 4]
                    ng = len(gkts)
                    for i in range(ng):
                        P.emit("te", lambda e, i=i, g0=g0, base=base, base_t=base_t, dslot=dslot: e.matmul(
                            PS_B[:, base_t + i * 128:base_t + (i + 1) * 128],
                            anx[:, base + (g0 + i) * 128:base + (g0 + i + 1) * 128],
                            fscr[:, dslot:dslot + 128], start=True, stop=True),
                            waits=([("s_dve", dve_diag_of[qt]), ("s_dve", dve_attnT)]
                                   if i == 0 else ()),
                            inc="s_te" if i == ng - 1 else None)
                    te_at = P.v("s_te")
                    P.emit("dve", lambda e, gw=ng * 128, base_t=base_t, base_e=base_e: e.tensor_copy(
                        anx[:, base_e:base_e + gw], PS_B[:, base_t:base_t + gw]),
                        waits=[("s_te", te_at)], inc="s_dve")
                    dve_attnT = P.v("s_dve")
                    for i, kt in enumerate(gkts):
                        glast = (g0 + i == n_k - 1)
                        gfirst = (g0 == 0 and i == 0)
                        P.emit("te", lambda e, i=i, kt=kt, zs=zslot, be=base_e, st_=gfirst, sp_=glast: e.matmul(
                            PS_B[:, zs:zs + 128],
                            Vsb[:, kt * 128:(kt + 1) * 128],
                            anx[:, be + i * 128:be + (i + 1) * 128],
                            start=st_, stop=sp_),
                            waits=([("s_dve", dve_attnT)] if i == 0 else ()),
                            inc="s_te" if glast else None)
                te_at_of[qt] = te_at
                te_z = P.v("s_te")
                te_z_of[qt] = te_z
                zdst = xz[:, (h % 2) * 2048 + qt * 128:(h % 2) * 2048 + (qt + 1) * 128]
                zwar = [("s_st", st_z_end[h - 2])] if h >= 2 else []
                if variant == "a":
                    P.emit("act", lambda e, h=h, zd=zdst, zs=zslot: e.activation(
                        zd, PS_B[:, zs:zs + 128], AF.Identity,
                        bias=bqkv[:, 32 + h:32 + h + 1]),
                        waits=[("s_te", te_z)] + zwar, inc="s_act")
                else:
                    P.emit("act", lambda e, zd=zdst, zs=zslot: e.activation(
                        zd, PS_B[:, zs:zs + 128], AF.Identity),
                        waits=[("s_te", te_z)] + zwar, inc="s_act")
                act_zev = P.v("s_act")
                P.emit("pl", lambda e, qt=qt, h=h, zd=zdst: e.dma_start(
                    d_z.ap()[qt, h], zd), waits=[("s_act", act_zev)],
                    inc="s_st", incn=16)
                if (h * NT + qt) % 2 == 0:
                    emit_gamma(P)

            npipe = 8 if variant == "a" else NT
            stage1(0)
            for qt in range(1, npipe):
                stage1(qt)
                stage2(qt - 1)
            stage2(npipe - 1)
            for qt in range(npipe, NT):
                stage1(qt)
                stage2(qt)
            act_head_end = P.v("s_act")
            dve_z_end.append(P.v("s_dve"))
            st_z_end.append(P.v("s_st"))
            te_head_end.append(P.v("s_te"))
            if h + 2 < H:
                emit_wcasts(h + 2, [("s_te", te_head_end[h])])

        while do_gamma and gam_i[0] < len(gam):
            emit_gamma(P)
        if do_gamma:
            dve_g = P.v("s_dve")
            P.emit("te", lambda e: e.matmul(
                PS_B[0:1, 0:3], onesC[:], gacc[:, 0:3], start=True, stop=True),
                waits=[("s_dve", dve_g), ("s_dve", dve_z_end[-1])], inc="s_te")
            te_g = P.v("s_te")
            P.emit("dve", lambda e: e.tensor_scalar(
                stats[0:1, 32:35], PS_B[0:1, 0:3], 0.5 / NWELEM, 0.5e-8,
                OP.mult, OP.add), waits=[("s_te", te_g)], inc="s_dve")
            dve_t = P.v("s_dve")
            P.emit("te", lambda e: e.matmul(
                PS_B[:, 16:19], onesRF[:], stats[0:1, 32:35],
                start=True, stop=True), waits=[("s_dve", dve_t)], inc="s_te")
            te_t = P.v("s_te")
            P.emit("dve", lambda e: e.tensor_copy(thrbc[:, 0:3], PS_B[:, 16:19]),
                   waits=[("s_te", te_t)])
            P.emit("dve", lambda e: e.tensor_scalar(
                thrbc[:, 4:7], thrbc[:, 0:3], -1.0, None, OP.mult), inc="s_dve")
        bst["big1_te"] = te_head_end[-1]
        bst["wbuf_te"] = te_head_end[-1]

        P = NP()
        pl_bo = 0
        if variant == "a":
            P.emit("pl", lambda e: e.dma_start(bosb, d_bo.ap()),
                   inc="s_pl", incn=16)
            pl_bo = P.v("s_pl")
        te_end = [0] * NT
        dve_add_l = [0] * NT
        dve_ln_l = [0] * NT
        st_q = [0] * NT
        for qt in range(NT):
            par = qt % 2
            zt = qkv[:, par * 2048:(par + 1) * 2048]
            xa = xrow[:, par * 2048:(par + 1) * 2048]
            wz = [("s_te", te_end[qt - 2])] if qt >= 2 else []
            P.emit("sp", lambda e, qt=qt, d=zt: e.dma_start(
                d.rearrange("p (hh e) -> p hh e", hh=H),
                d_z.ap()[qt].rearrange("hh p e -> p hh e")),
                waits=wz, inc="s_sp", incn=16)
            wx = []
            if qt >= 2:
                wx = [("s_st", st_q[qt - 2]), ("s_dve", dve_ln_l[qt - 2])]
            P.emit("sp", lambda e, qt=qt, d=xa: e.dma_start(
                d, resid_src.ap()[qt * 128:(qt + 1) * 128, :]),
                waits=wx, inc="s_sp", incn=16)
            sp_q = P.v("s_sp")
            PS = PS_A if par == 0 else PS_B
            wts0 = [("s_sp", sp_q), ("s_pl", wo_pl[-1])]
            if qt >= 2:
                wts0.append(("s_dve", dve_add_l[qt - 2]))
            lastb = (variant != "a")
            for hh in range(H):
                for j in range(4):
                    last = (hh == H - 1 and j == 3)
                    P.emit("te", lambda e, hh=hh, j=j, z=zt, PS=PS: e.matmul(
                        PS[:, j * 512:(j + 1) * 512],
                        z[:, hh * 128:(hh + 1) * 128],
                        big2[:, hh * Dm + j * 512: hh * Dm + (j + 1) * 512],
                        start=(hh == 0), stop=(hh == H - 1 and lastb)),
                        waits=wts0 if (hh == 0 and j == 0) else (),
                        inc="s_te" if (last and lastb) else None)
            if variant == "a":
                for j in range(4):
                    P.emit("te", lambda e, j=j, PS=PS: e.matmul(
                        PS[:, j * 512:(j + 1) * 512], ones1[:],
                        bosb[0:1, j * 512:(j + 1) * 512],
                        start=False, stop=True, skip_group_check=True),
                        waits=[("s_pl", pl_bo)] if j == 0 else (),
                        inc="s_te" if j == 3 else None)
            te_end[qt] = P.v("s_te")
            P.emit("dve", lambda e, xa=xa, PS=PS: e.tensor_add(xa, xa, PS[:, :]),
                   waits=[("s_te", te_end[qt])], inc="s_dve")
            dve_add_l[qt] = P.v("s_dve")
            P.emit("pl", lambda e, qt=qt, xa=xa, dst=out_dst: e.dma_start(
                dst.ap()[qt * 128:(qt + 1) * 128, :], xa),
                waits=[("s_dve", dve_add_l[qt])], inc="s_st", incn=16)
            if do_ln_next:
                xo = xlnst[:, par * 2048:(par + 1) * 2048]
                war = [("s_st", st_q[qt - 2])] if qt >= 2 else []
                dv = ln_chain(P, xa, xo, par,
                              wait_first=[("s_dve", dve_add_l[qt])],
                              war_apply=war)
                dve_ln_l[qt] = dv
                P.emit("pl", lambda e, qt=qt, o=xo: e.dma_start(
                    d_xln.ap()[qt * 128:(qt + 1) * 128, :], o),
                    waits=[("s_dve", dv)], inc="s_st", incn=16)
            else:
                dve_ln_l[qt] = dve_add_l[qt]
            st_q[qt] = P.v("s_st")
        bst["xln_st"] = P.v("s_st")

    def ffn_sublayer(first, resid_src, out_dst, do_ln_next):
        SUP = 4
        nsup = NHF // SUP
        emit_xlnt_treads()
        P = NP()
        sp_xlnt = bst["sp_xlnt"]

        qstate = {"par": 0, "dq": {0: 0, 1: 0, 2: 0}, "last_st": 0}

        def quantize_chunk(P, src_ap, dst_ap, mi, width, q="pl"):
            par_f = qstate["par"]
            qstate["par"] = (par_f + 1) % 3
            fa = wf32[:, par_f * 2048:par_f * 2048 + width]
            P.emit("pl", lambda e, d=fa, a=src_ap: e.dma_start(d, a),
                   waits=[("s_dve", qstate["dq"][par_f])], inc="s_q", incn=16)
            sp_w = P.v("s_q")
            pbuf = fscr[:, 2048:2048 + width]
            nbuf = xlnst[:, 0:width]
            qbuf = qkv[:, 4096:4096 + width]
            P.emit("dve", lambda e, f=fa, p=pbuf, mi=mi: e.tensor_single_scalar(
                p, f, thrbc[:, mi:mi + 1], OP.is_gt),
                waits=[("s_q", sp_w)])
            P.emit("dve", lambda e, f=fa, n=nbuf, mi=mi: e.tensor_single_scalar(
                n, f, thrbc[:, 4 + mi:4 + mi + 1], OP.is_lt))
            P.emit("dve", lambda e, p=pbuf, n=nbuf, q=qbuf: e.tensor_sub(q, p, n),
                   waits=[("s_st", qstate["last_st"])], inc="s_dve")
            dq = P.v("s_dve")
            qstate["dq"][par_f] = dq
            P.emit("pl", lambda e, d=dst_ap, q=qbuf: e.dma_start(d, q),
                   waits=[("s_dve", dq)], inc="s_st", incn=16)
            qstate["last_st"] = P.v("s_st")
            return qstate["last_st"]

        w3chunks = []
        for dc in range(ND):
            for c0, w in ((0, 2048), (2048, 2048), (4096, 1536)):
                w3chunks.append((dc, c0, w))
        w3_i = [0]

        def emit_w3_quant(P):
            if not first or w3_i[0] >= len(w3chunks):
                return
            dc, c0, w = w3chunks[w3_i[0]]
            quantize_chunk(P, d_w3.ap()[dc * 128:(dc + 1) * 128, c0:c0 + w],
                           d_w3qc.ap()[dc * 128:(dc + 1) * 128, c0:c0 + w], 2, w,
                           q="sp")
            w3_i[0] += 1

        st_after_q = [0] * nsup

        def quantize_super(s_):
            for hf in range(s_ * SUP, (s_ + 1) * SUP):
                quantize_chunk(P, d_w1.ap()[hf * 128:(hf + 1) * 128, :],
                               d_w1qc.ap()[hf * 128:(hf + 1) * 128, :], 0, 2048)
                quantize_chunk(P, d_w2.ap()[hf * 128:(hf + 1) * 128, :],
                               d_w2qc.ap()[hf * 128:(hf + 1) * 128, :], 1, 2048)
            st_after_q[s_] = P.v("s_st")

        if first and FFN_PART >= 1:
            quantize_super(0)
            quantize_super(1)

        sp_tread = [0] * nsup
        te_super_done = [0] * nsup
        act_sig = 0
        dve_hid = 0
        st_hid = [0] * NHF

        def emit_treads(s_):
            war = [("s_te", te_super_done[s_ - 2])] if s_ >= 2 else []
            if first:
                war.append(("s_st", st_after_q[s_]))
            par = s_ % 2
            base = par * 16384
            for mi, qc in ((0, d_w1qc), (1, d_w2qc)):
                for dc in range(ND):
                    P.emit("sp", lambda e, qc=qc, s_=s_, dc=dc, mi=mi, base=base: e.dma_start_transpose(
                        big2[:, base + mi * 8192 + dc * 512: base + mi * 8192 + dc * 512 + 512],
                        qc.ap()[s_ * 512:(s_ + 1) * 512, dc * 128:(dc + 1) * 128]),
                        waits=war if (mi == 0 and dc == 0) else (),
                        inc="s_sp", incn=16)
            sp_tread[s_] = P.v("s_sp")

        if FFN_PART >= 2:
            emit_treads(0)
        for s_ in range(nsup if FFN_PART >= 2 else 0):
            par = s_ % 2
            base = par * 16384
            if first and FFN_PART >= 1 and s_ + 2 < nsup:
                quantize_super(s_ + 2)
            if s_ + 1 < nsup:
                emit_treads(s_ + 1)
            for ii in range(SUP):
                hf = s_ * SUP + ii
                for mi, PS in ((0, PS_A), (1, PS_B)):
                    wts = [("s_sp", sp_tread[s_]), ("s_sp", sp_xlnt)]
                    if hf > 0:
                        wts.append(("s_act", act_sig) if mi == 0 else
                                   ("s_dve", dve_hid))
                    for sti in range(4):
                        for dc in range(ND):
                            last = (sti == 3 and dc == ND - 1)
                            P.emit("te", lambda e, mi=mi, PS=PS, sti=sti, dc=dc, base=base, ii=ii: e.matmul(
                                PS[:, sti * 512:(sti + 1) * 512],
                                big2[:, base + mi * 8192 + dc * 512 + ii * 128: base + mi * 8192 + dc * 512 + ii * 128 + 128],
                                big1[:, dc * S + sti * 512: dc * S + (sti + 1) * 512],
                                start=(dc == 0), stop=(dc == ND - 1)),
                                waits=wts if (sti == 0 and dc == 0) else (),
                                inc="s_te" if last else None)
                te_hf = P.v("s_te")
                s1 = qkv[:, 0:2048]
                hidb = qkv[:, 2048:4096]
                P.emit("act", lambda e: e.activation(s1, PS_A[:, :], AF.Sigmoid),
                       waits=[("s_te", te_hf - 1), ("s_dve", dve_hid)],
                       inc="s_act")
                act_sig = P.v("s_act")
                hwar = [("s_st", st_hid[hf - 1])] if hf >= 1 else []
                P.emit("dve", lambda e: e.tensor_mul(hidb, s1, PS_B[:, :]),
                       waits=[("s_act", act_sig), ("s_te", te_hf)] + hwar,
                       inc="s_dve")
                dve_hid = P.v("s_dve")
                P.emit("pl", lambda e, hf=hf: e.dma_start(
                    d_hid.ap()[hf].rearrange("st p f -> p st f"),
                    hidb.rearrange("p (st f) -> p st f", st=4)),
                    waits=[("s_dve", dve_hid)], inc="s_st", incn=16)
                st_hid[hf] = P.v("s_st")
                emit_w3_quant(P)
            te_super_done[s_] = P.v("s_te")
        while first and FFN_PART >= 1 and w3_i[0] < len(w3chunks):
            emit_w3_quant(P)
        st_w3_done = P.v("s_st")
        st_hid_done = st_hid[-1]

        GRP = 6
        ngrp = (NHF + GRP - 1) // GRP

        def emit_group_loads(Pl, half, g, war):
            lo = g * GRP
            hi = min(NHF, lo + GRP)
            n = hi - lo
            gpar = g % 2
            for i in range(n):
                hf = lo + i
                Pl.emit("sp", lambda e, hf=hf, i=i, gpar=gpar: e.dma_start_transpose(
                    big1[:, gpar * 12288 + i * 2048: gpar * 12288 + (i + 1) * 2048],
                    d_w3qc.ap()[:, hf * 128:(hf + 1) * 128]),
                    waits=war if i == 0 else (), inc="s_sp", incn=16)
            for i in range(n):
                hf = lo + i
                Pl.emit("sp", lambda e, hf=hf, i=i, gpar=gpar, half=half: e.dma_start(
                    wbuf[:, gpar * 6144 + i * 1024: gpar * 6144 + (i + 1) * 1024].rearrange("p (st f) -> p st f", st=2),
                    d_hid.ap()[hf][half * 2:half * 2 + 2].rearrange("st p f -> p st f")),
                    inc="s_sp", incn=16)
            return Pl.v("s_sp")

        pref = {}
        if FFN_PART >= 21:
            pref[(0, 0)] = emit_group_loads(
                P, 0, 0, [("s_te", te_super_done[-1] if FFN_PART >= 2 else 0),
                          ("s_st", P.v("s_st")), ("s_dve", P.v("s_dve"))])
        for half in range(2 if FFN_PART >= 21 else 0):
            prev_plan = plans[-1]
            P = NP()
            te_grp_done = [0] * ngrp
            dve_acc = [0] * 8
            last_ps = {0: 0, 1: 0}
            for g in range(ngrp):
                lo = g * GRP
                hi = min(NHF, lo + GRP)
                n = hi - lo
                gpar = g % 2
                if (half, g) in pref:
                    sp_g = pref[(half, g)]
                else:
                    war = [("s_te", te_grp_done[g - 2])] if g >= 2 else []
                    sp_g = emit_group_loads(P, half, g, war)
                for sc in range(8 if FFN_PART >= 25 else 0):
                    PS = PS_A if sc % 2 == 0 else PS_B
                    for i in range(n):
                        for j in range(4):
                            last = (i == n - 1 and j == 3)
                            P.emit("te", lambda e, i=i, j=j, sc=sc, gpar=gpar, PS=PS, n=n: e.matmul(
                                PS[:, j * 512:(j + 1) * 512],
                                wbuf[:, gpar * 6144 + i * 1024 + sc * 128: gpar * 6144 + i * 1024 + sc * 128 + 128],
                                big1[:, gpar * 12288 + i * 2048 + j * 512: gpar * 12288 + i * 2048 + (j + 1) * 512],
                                start=(i == 0), stop=(i == n - 1)),
                                waits=([("s_sp", sp_g), ("s_dve", last_ps[sc % 2])]
                                       if (i == 0 and j == 0) else ()),
                                inc="s_te" if last else None)
                    te_sc = P.v("s_te")
                    acc = accv[:, sc * 2048:(sc + 1) * 2048]
                    if g == 0:
                        P.emit("dve", lambda e, acc=acc, PS=PS: e.tensor_copy(
                            acc, PS[:, :]), waits=[("s_te", te_sc)], inc="s_dve")
                    else:
                        P.emit("dve", lambda e, acc=acc, PS=PS: e.tensor_add(
                            acc, acc, PS[:, :]), waits=[("s_te", te_sc)], inc="s_dve")
                    dve_acc[sc] = P.v("s_dve")
                    last_ps[sc % 2] = dve_acc[sc]
                te_grp_done[g] = P.v("s_te")
            st_sc = [0] * 8
            dve_ln_sc = [0] * 8
            for sc in range(8 if FFN_PART >= 30 else 0):
                t = half * 8 + sc
                par = sc % 2
                xa = xrow[:, par * 2048:(par + 1) * 2048]
                wx = []
                if sc >= 2:
                    wx = [("s_st", st_sc[sc - 2]), ("s_dve", dve_ln_sc[sc - 2])]
                P.emit("sp", lambda e, t=t, d=xa: e.dma_start(
                    d, resid_src.ap()[t * 128:(t + 1) * 128, :]),
                    waits=wx, inc="s_sp", incn=16)
                sp_x = P.v("s_sp")
                acc = accv[:, sc * 2048:(sc + 1) * 2048]
                P.emit("dve", lambda e, xa=xa, acc=acc: e.tensor_add(xa, xa, acc),
                       waits=[("s_sp", sp_x), ("s_dve", dve_acc[sc])], inc="s_dve")
                dve_add = P.v("s_dve")
                P.emit("pl", lambda e, t=t, xa=xa, dst=out_dst: e.dma_start(
                    dst.ap()[t * 128:(t + 1) * 128, :], xa),
                    waits=[("s_dve", dve_add)], inc="s_st", incn=16)
                if do_ln_next:
                    xo = xlnst[:, par * 2048:(par + 1) * 2048]
                    war = [("s_st", st_sc[sc - 2])] if sc >= 2 else []
                    dv = ln_chain(P, xa, xo, par,
                                  wait_first=[("s_dve", dve_add)],
                                  war_apply=war)
                    dve_ln_sc[sc] = dv
                    P.emit("pl", lambda e, t=t, o=xo: e.dma_start(
                        d_xln.ap()[t * 128:(t + 1) * 128, :], o),
                        waits=[("s_dve", dv)], inc="s_st", incn=16)
                else:
                    dve_ln_sc[sc] = dve_add
                st_sc[sc] = P.v("s_st")
            if half == 0 and FFN_PART >= 21:
                pref[(1, 0)] = emit_group_loads(
                    P, 1, 0, [("s_te", te_grp_done[ngrp - 2])])
            if half == 1:
                bst["xln_st"] = P.v("s_st")
                bst["big1_te"] = te_grp_done[ngrp - 1]
                bst["wbuf_te"] = te_grp_done[ngrp - 1]

    attn_sublayer("a", d_x, d_xres if stage > 1 else d_y,
                  do_ln_next=(stage > 1), do_gamma=(stage > 1))
    if stage > 1:
        ffn_sublayer(True, d_xres, d_xres if stage > 2 else d_y,
                     do_ln_next=(stage > 2))
    if stage > 2:
        attn_sublayer("b", d_xres, d_xres if stage > 3 else d_y,
                      do_ln_next=(stage > 3), do_gamma=False)
    if stage > 3:
        ffn_sublayer(False, d_xres, d_y, do_ln_next=False)

    P = NP()
    total_st = cnt.get("s_st", 0)
    P.emit("pl", lambda e: e.engine_nop(), waits=[("s_st", total_st)])

    _run_blocks(nc, plans, sems)
    return nc


def make_consts():
    cI = np.eye(128, dtype=np.float32)
    cShift = np.eye(128, k=-1, dtype=np.float32)
    cTri = np.triu(np.ones((128, 128), np.float32)) * NEG_CAUSAL
    uLr = np.zeros((1, 128), np.float32); uLr[0, 64:] = 1.0
    vLr = np.zeros((1, 128), np.float32); vLr[0, :64] = NEG_BAND
    uRr = np.zeros((1, 128), np.float32); uRr[0, :64] = 1.0
    vRr = np.zeros((1, 128), np.float32); vRr[0, 64:] = NEG_BAND
    b = lambda a: np.ascontiguousarray(a.astype(BF))
    return {
        "cI": b(cI), "cShift": b(cShift), "cTri": b(cTri),
        "cuL": b(uLr), "cvL": b(vLr), "cuR": b(uRr), "cvR": b(vRr),
        "cOnes1": b(np.ones((1, 128), np.float32)),
        "cOnesC": np.ones((128, 1), np.float32),
        "cOnesRF": np.ones((1, 128), np.float32),
    }


def kernel(L0_attn_a_x, L0_attn_a_W_Q, L0_attn_a_W_K, L0_attn_a_W_V,
           L0_attn_a_W_O, L0_attn_a_b_Q, L0_attn_a_b_K, L0_attn_a_b_V,
           L0_attn_a_b_O, L0_attn_b_Wq, L0_attn_b_Wk, L0_attn_b_Wv,
           L0_attn_b_Wo, L0_ffn_w1, L0_ffn_w2, L0_ffn_w3):
    consts = make_consts()
    shared = {
        "wqA": np.ascontiguousarray(L0_attn_a_W_Q, np.float32),
        "wkA": np.ascontiguousarray(L0_attn_a_W_K, np.float32),
        "wvA": np.ascontiguousarray(L0_attn_a_W_V, np.float32),
        "woA": np.ascontiguousarray(L0_attn_a_W_O, np.float32),
        "bq": np.ascontiguousarray(L0_attn_a_b_Q, np.float32),
        "bk": np.ascontiguousarray(L0_attn_a_b_K, np.float32),
        "bv": np.ascontiguousarray(L0_attn_a_b_V, np.float32),
        "bo": np.ascontiguousarray(L0_attn_a_b_O, np.float32).reshape(1, Dm),
        "wqB": np.ascontiguousarray(L0_attn_b_Wq, np.float32),
        "wkB": np.ascontiguousarray(L0_attn_b_Wk, np.float32),
        "wvB": np.ascontiguousarray(L0_attn_b_Wv, np.float32),
        "woB": np.ascontiguousarray(L0_attn_b_Wo, np.float32),
        "w1": np.ascontiguousarray(L0_ffn_w1, np.float32),
        "w2": np.ascontiguousarray(L0_ffn_w2, np.float32),
        "w3": np.ascontiguousarray(L0_ffn_w3, np.float32),
        **consts,
    }
    x = np.ascontiguousarray(L0_attn_a_x, np.float32)
    in_maps = [dict(shared, x=x[c]) for c in range(8)]
    nc = build(STAGE)
    res = run_bass_kernel_spmd(nc, in_maps, core_ids=list(range(8)))
    return np.stack([np.asarray(res.results[c]["y"], np.float32)
                     for c in range(8)], axis=0)


# version 10
# speedup vs baseline: 1.3260x; 1.1292x over previous
import numpy as np
import ml_dtypes
import concourse.bass as bass
import concourse.mybir as mybir
from concourse.bass_utils import run_bass_kernel_spmd
from concourse.alu_op_type import AluOpType as OP

F32 = mybir.dt.float32
BF16 = mybir.dt.bfloat16
AF = mybir.ActivationFunctionType
AX = mybir.AxisListType
BF = ml_dtypes.bfloat16

S = 2048
Dm = 2048
H = 16
DH = 128
HFD = 5632
NT = S // 128
ND = Dm // 128
NHF = HFD // 128
NWELEM = float(HFD * Dm)
INV_SCALE = float(1.0 / np.sqrt(np.float32(128.0)))
NEG_CAUSAL = -100000.0
NEG_BAND = -1.0e30

STAGE = 4
USE_ACCUM_SUMS = True
PARANOID = False
PARANOID_ENGS = ("te", "dve", "act")
PARANOID_MODE = "full"
FFN_PART = 30

ENGS = ("te", "dve", "act", "sp", "pl")


class Plan:

    def __init__(self, cnt, known):
        self.ops = {k: [] for k in ENGS}
        self.cnt = cnt
        self.known = known

    def emit(self, eng, fn, waits=(), inc=None, incn=1):
        k = self.known[eng]
        if PARANOID and eng in PARANOID_ENGS:
            if PARANOID_MODE in ("full", "waits"):
                waits = list(waits) + [(s, self.cnt.get(s, 0))
                                       for s in ("s_te", "s_dve", "s_act")]
            if inc is None and PARANOID_MODE in ("full", "incs"):
                inc = {"te": "s_te", "dve": "s_dve", "act": "s_act"}[eng]
        real_waits = []
        for s, v in waits:
            if v is None or v <= 0:
                continue
            if k.get(s, 0) >= v:
                continue
            real_waits.append((s, v))
            k[s] = v
        self.ops[eng].append((fn, real_waits, inc, incn))
        if inc is not None:
            self.cnt[inc] = self.cnt.get(inc, 0) + incn
            if eng not in ("sp", "pl"):
                k[inc] = self.cnt[inc]
        return self.cnt.get(inc)

    def v(self, sem):
        return self.cnt.get(sem, 0)


def _replay(plan, sems, eng):
    drain_all = eng == "dve"
    drain_inc = eng in ("dve", "act")

    def run(e):
        for fn, waits, inc, incn in plan.ops[eng]:
            for s, v in waits:
                e.wait_ge(sems[s], v)
            ins = fn(e)
            if drain_all:
                d = e.drain()
                if inc is not None:
                    d.then_inc(sems[inc], incn)
            elif inc is not None:
                if drain_inc:
                    e.drain().then_inc(sems[inc], incn)
                else:
                    ins.then_inc(sems[inc], incn)
    return run


def _run_blocks(nc, plans, sems):
    for plan in plans:
        with nc.Block() as b:
            if plan.ops["sp"]:
                b.sync(_replay(plan, sems, "sp"))
            if plan.ops["pl"]:
                b.gpsimd(_replay(plan, sems, "pl"))
            if plan.ops["te"]:
                b.tensor(_replay(plan, sems, "te"))
            if plan.ops["dve"]:
                b.vector(_replay(plan, sems, "dve"))
            if plan.ops["act"]:
                b.scalar(_replay(plan, sems, "act"))


def build(stage, debug=False):
    nc = bass.Bass("TRN2", target_bir_lowering=False)
    dk = "ExternalOutput" if debug else "Internal"

    d_x = nc.dram_tensor("x", [S, Dm], F32, kind="ExternalInput")
    d_wqA = nc.dram_tensor("wqA", [H, Dm, DH], F32, kind="ExternalInput")
    d_wkA = nc.dram_tensor("wkA", [H, Dm, DH], F32, kind="ExternalInput")
    d_wvA = nc.dram_tensor("wvA", [H, Dm, DH], F32, kind="ExternalInput")
    d_woA = nc.dram_tensor("woA", [H, DH, Dm], F32, kind="ExternalInput")
    d_bq = nc.dram_tensor("bq", [H, DH], F32, kind="ExternalInput")
    d_bk = nc.dram_tensor("bk", [H, DH], F32, kind="ExternalInput")
    d_bv = nc.dram_tensor("bv", [H, DH], F32, kind="ExternalInput")
    d_bo = nc.dram_tensor("bo", [1, Dm], F32, kind="ExternalInput")
    d_wqB = nc.dram_tensor("wqB", [Dm, Dm], F32, kind="ExternalInput")
    d_wkB = nc.dram_tensor("wkB", [Dm, Dm], F32, kind="ExternalInput")
    d_wvB = nc.dram_tensor("wvB", [Dm, Dm], F32, kind="ExternalInput")
    d_woB = nc.dram_tensor("woB", [Dm, Dm], F32, kind="ExternalInput")
    d_w1 = nc.dram_tensor("w1", [HFD, Dm], F32, kind="ExternalInput")
    d_w2 = nc.dram_tensor("w2", [HFD, Dm], F32, kind="ExternalInput")
    d_w3 = nc.dram_tensor("w3", [Dm, HFD], F32, kind="ExternalInput")
    d_cI = nc.dram_tensor("cI", [128, 128], BF16, kind="ExternalInput")
    d_cShift = nc.dram_tensor("cShift", [128, 128], BF16, kind="ExternalInput")
    d_cTri = nc.dram_tensor("cTri", [128, 128], BF16, kind="ExternalInput")
    d_cuL = nc.dram_tensor("cuL", [1, 128], BF16, kind="ExternalInput")
    d_cvL = nc.dram_tensor("cvL", [1, 128], BF16, kind="ExternalInput")
    d_cuR = nc.dram_tensor("cuR", [1, 128], BF16, kind="ExternalInput")
    d_cvR = nc.dram_tensor("cvR", [1, 128], BF16, kind="ExternalInput")
    d_cOnes1 = nc.dram_tensor("cOnes1", [1, 128], BF16, kind="ExternalInput")
    d_cOnesC = nc.dram_tensor("cOnesC", [128, 1], F32, kind="ExternalInput")
    d_cOnesRF = nc.dram_tensor("cOnesRF", [1, 128], F32, kind="ExternalInput")

    d_y = nc.dram_tensor("y", [S, Dm], F32, kind="ExternalOutput")
    d_xres = nc.dram_tensor("xres", [S, Dm], F32, kind=dk)
    d_xln = nc.dram_tensor("xln", [S, Dm], BF16, kind=dk)
    d_z = nc.dram_tensor("zb", [NT, H, DH, 128], BF16, kind=dk)
    d_hid = nc.dram_tensor("hid", [NHF, 4, 128, 512], BF16, kind=dk)
    d_w1qc = nc.dram_tensor("w1qc", [HFD, Dm], BF16, kind=dk)
    d_w2qc = nc.dram_tensor("w2qc", [HFD, Dm], BF16, kind=dk)
    d_w3qc = nc.dram_tensor("w3qc", [Dm, HFD], BF16, kind=dk)
    d_dbg = None
    if debug:
        d_dbg = nc.dram_tensor("dbg", [NT, 128, 2048], BF16, kind="ExternalOutput")
        d_dbg2 = nc.dram_tensor("dbg2", [NT, 128, 8], F32, kind="ExternalOutput")

    A = nc.alloc_sbuf_tensor
    big1 = A("big1", [128, ND * S], BF16)
    big2 = A("big2", [128, H * Dm], BF16)
    wbuf = A("wbuf", [128, 12288], BF16)
    qkv = A("qkv", [128, 6144], BF16)
    xz = A("xz", [128, 8192], BF16)
    fscr = A("fscr", [128, 4096], BF16)
    anx = A("anx", [128, 3072], BF16)
    xlnst = A("xlnst", [128, 4096], BF16)
    I_sb = A("I_sb", [128, 128], BF16)
    shA = A("shA", [128, 128], BF16)
    triB = A("triB", [128, 128], BF16)
    uL = A("uL", [1, 128], BF16)
    vL = A("vL", [1, 128], BF16)
    uR = A("uR", [1, 128], BF16)
    vR = A("vR", [1, 128], BF16)
    ones1 = A("ones1", [1, 128], BF16)
    onesC = A("onesC", [128, 1], F32)
    onesRF = A("onesRF", [1, 128], F32)
    bnat = wbuf[0:16, 0:384]
    bosb = qkv[0:1, 4096:6144]
    bqkv = A("bqkv", [128, 48], F32)
    cvals = A("cvals", [128, 2], F32)
    stats = A("stats", [128, 64], F32)
    gacc = A("gacc", [128, 4], F32)
    thrbc = A("thrbc", [128, 8], F32)

    xrow = xz.ap().bitcast(F32)
    wf32 = wbuf.ap().bitcast(F32)
    fscr32 = fscr.ap().bitcast(F32)
    accv = big2.ap().bitcast(F32)
    gstage = xlnst.ap().bitcast(F32)

    PS_A = nc.alloc_psum_tensor("psA", [128, 2048], F32)
    PS_B = nc.alloc_psum_tensor("psB", [128, 2048], F32)

    sems = {}
    import contextlib
    _stack = contextlib.ExitStack()
    for name in ("s_te", "s_dve", "s_act", "s_sp", "s_pl", "s_st", "s_q", "s_sq"):
        sems[name] = _stack.enter_context(nc.semaphore(name))

    cnt = {}
    known = {e: {} for e in ENGS}
    plans = []

    def NP():
        p = Plan(cnt, known)
        plans.append(p)
        return p

    def st(i):
        return stats[:, i:i + 1]

    NEGMAX, SUMS, RECIP, GTMP = 0, 1, 2, 3
    SUMX, SSQ, NEGM, MSQ, VR = 0, 1, 2, 3, 4

    def ln_chain(P, xr_ap, out_bf_ap, par, wait_first=(), war_apply=()):
        b = 8 + 8 * par

        def s(i):
            return stats[:, b + i:b + i + 1]
        P.emit("dve", lambda e, a=xr_ap, s0=s(SUMX): e.reduce_sum(s0, a, axis=AX.X),
               waits=wait_first)
        P.emit("act", lambda e, a=xr_ap, s1=s(SSQ): e.activation(
            fscr32[:, 0:2048], a, AF.Square, accum_out=s1),
            waits=list(wait_first) + list(ln_chain.act_war[par]), inc="s_act")
        a_sq = P.v("s_act")
        P.emit("dve", lambda e, s0=s(SUMX), s2=s(NEGM): e.tensor_scalar(
            s2, s0, -1.0 / Dm, None, OP.mult))
        P.emit("dve", lambda e, s2=s(NEGM), s3=s(MSQ): e.tensor_mul(s3, s2, s2),
               waits=[("s_act", a_sq)])
        P.emit("dve", lambda e, s1=s(SSQ), s3=s(MSQ), s4=s(VR): e.scalar_tensor_tensor(
            s4, s1, 1.0 / Dm, s3, OP.mult, OP.subtract),
            waits=list(ln_chain.var_war[par]), inc="s_dve")
        a_var = P.v("s_dve")
        P.emit("act", lambda e, s4=s(VR): e.activation(
            s4, s4, AF.Ln, bias=cvals[:, 0:1]), waits=[("s_dve", a_var)])
        P.emit("act", lambda e, s4=s(VR): e.activation(
            s4, s4, AF.Exp, scale=cvals[:, 1:2]), inc="s_act")
        a_rstd = P.v("s_act")
        P.emit("dve", lambda e, a=xr_ap, o=out_bf_ap, s2=s(NEGM), s4=s(VR): e.tensor_scalar(
            o, a, s2, s4, OP.add, OP.mult),
            waits=[("s_act", a_rstd)] + list(war_apply), inc="s_dve")
        dv = P.v("s_dve")
        ln_chain.act_war[par] = [("s_dve", dv)]
        ln_chain.var_war[par] = [("s_act", a_rstd)]
        return dv
    ln_chain.act_war = {0: [], 1: []}
    ln_chain.var_war = {0: [], 1: []}

    bst = {"xln_st": 0, "big1_te": 0, "wbuf_te": 0, "sp_xlnt": 0}

    def emit_xlnt_treads():
        P = plans[-1]
        w = [("s_st", bst["xln_st"]), ("s_te", bst["big1_te"])]
        for dc in range(ND):
            P.emit("sp", lambda e, dc=dc: e.dma_start_transpose(
                big1[:, dc * S:(dc + 1) * S], d_xln.ap()[:, dc * 128:(dc + 1) * 128]),
                waits=w if dc == 0 else (), inc="s_sp", incn=16)
        bst["sp_xlnt"] = P.v("s_sp")

    def gamma_chunk_list():
        out = []
        for mi, dt_ in ((0, d_w1), (1, d_w2)):
            for hf in range(NHF):
                out.append((mi, dt_.ap()[hf * 128:(hf + 1) * 128, :], 2048))
        for dc in range(ND):
            for c0, w in ((0, 2048), (2048, 2048), (4096, 1536)):
                out.append((2, d_w3.ap()[dc * 128:(dc + 1) * 128, c0:c0 + w], w))
        return out

    P = NP()
    for dst, src in ((I_sb, d_cI), (shA, d_cShift), (triB, d_cTri),
                     (uL, d_cuL), (vL, d_cvL), (uR, d_cuR), (vR, d_cvR),
                     (ones1, d_cOnes1), (onesC, d_cOnesC), (onesRF, d_cOnesRF)):
        P.emit("sp", lambda e, d=dst, s=src: e.dma_start(d[:], s.ap()),
               inc="s_sp", incn=16)
    for j, src in enumerate((d_bq, d_bk, d_bv)):
        P.emit("pl", lambda e, j=j, s=src: e.dma_start(
            bnat[:, j * 128:(j + 1) * 128], s.ap()), inc="s_pl", incn=16)
    pl_b = P.v("s_pl")
    sp_c = P.v("s_sp")
    P.emit("dve", lambda e: e.memset(gacc[:], 0.0))
    P.emit("dve", lambda e: e.memset(cvals[:, 0:1], 1e-5))
    P.emit("dve", lambda e: e.memset(cvals[:, 1:2], -0.5), inc="s_dve")
    for j in range(3):
        P.emit("te", lambda e, j=j: e.matmul(
            PS_B[:, j * 16:(j + 1) * 16], bnat[:, j * 128:(j + 1) * 128],
            I_sb[0:16, 0:16], start=True, stop=True),
            waits=[("s_pl", pl_b), ("s_sp", sp_c)],
            inc="s_te" if j == 2 else None)
    te_b = P.v("s_te")
    P.emit("dve", lambda e: e.tensor_copy(bqkv[:], PS_B[:, 0:48]),
           waits=[("s_te", te_b)], inc="s_dve")

    dve_apply = [0] * NT
    st_xln = [0] * NT
    for t in range(NT):
        par = t % 2
        xa = xrow[:, par * 2048:(par + 1) * 2048]
        wload = [("s_dve", dve_apply[t - 2])] if t >= 2 else []
        P.emit("sp", lambda e, t=t, a=xa: e.dma_start(
            a, d_x.ap()[t * 128:(t + 1) * 128, :]), waits=wload,
            inc="s_sp", incn=16)
        sp_t = P.v("s_sp")
        xo = xlnst[:, par * 2048:(par + 1) * 2048]
        war = [("s_st", st_xln[t - 2])] if t >= 2 else []
        dv = ln_chain(P, xa, xo, par, wait_first=[("s_sp", sp_t)], war_apply=war)
        dve_apply[t] = dv
        P.emit("pl", lambda e, t=t, o=xo: e.dma_start(
            d_xln.ap()[t * 128:(t + 1) * 128, :], o), waits=[("s_dve", dv)],
            inc="s_st", incn=16)
        st_xln[t] = P.v("s_st")
    bst["xln_st"] = P.v("s_st")
    bst["wbuf_te"] = te_b

    def attn_sublayer(variant, resid_src, out_dst, do_ln_next, do_gamma):
        emit_xlnt_treads()
        prevP = plans[-1]
        P = NP()
        sp_xlnt = bst["sp_xlnt"]

        gam = gamma_chunk_list() if do_gamma else []
        gam_i = [0]
        gam_dve = {0: 0, 1: 0}

        def emit_gamma(P):
            if not do_gamma or gam_i[0] >= len(gam):
                return
            i = gam_i[0]
            mi, gap, gw_ = gam[i]
            gdst = gstage[:, 0:gw_]
            P.emit("sp", lambda e, d=gdst, a=gap: e.dma_start(d, a),
                   waits=[("s_dve", gam_dve[0])], inc="s_sp", incn=16)
            sp_g = P.v("s_sp")
            P.emit("dve", lambda e, d=gdst: e.reduce_sum(
                st(GTMP), d, axis=AX.X, apply_absolute_value=True),
                waits=[("s_sp", sp_g)])
            P.emit("dve", lambda e, mi=mi: e.tensor_add(
                gacc[:, mi:mi + 1], gacc[:, mi:mi + 1], st(GTMP)), inc="s_dve")
            gam_dve[0] = P.v("s_dve")
            gam_i[0] += 1

        wo_pl = []
        wq_pl = []
        te_head_end = []
        dve_z_end = []
        st_z_end = []
        act_head_end = 0

        def w_cast(e, h, j):
            par = h % 2
            dst = wbuf[:, (par * 3 + j) * 2048:(par * 3 + j) * 2048 + 2048]
            if variant == "a":
                t = (d_wqA, d_wkA, d_wvA)[j]
                ap = t.ap()[h].rearrange("(dc p) e -> p dc e", p=128)
            else:
                t = (d_wqB, d_wkB, d_wvB)[j]
                ap = t.ap()[:, h * 128:(h + 1) * 128].rearrange(
                    "(dc p) e -> p dc e", p=128)
            return e.dma_start(dst.rearrange("p (dc e) -> p dc e", dc=ND), ap)

        def emit_wq(h, war, plan):
            for j in range(3):
                plan.emit("pl", lambda e, h=h, j=j: w_cast(e, h, j),
                          waits=war if j == 0 else (), inc="s_pl", incn=16)
            wq_pl.append(plan.v("s_pl"))

        def emit_wo(h, plan):
            if variant == "a":
                woap = d_woA.ap()[h]
            else:
                woap = d_woB.ap()[h * 128:(h + 1) * 128, :]
            plan.emit("pl", lambda e, h=h, a=woap: e.dma_start(
                big2[:, h * Dm:(h + 1) * Dm], a), inc="s_pl", incn=16)
            wo_pl.append(plan.v("s_pl"))

        def emit_wcasts(h, war):
            emit_wq(h, war, P)
            emit_wo(h, P)

        emit_wq(0, [("s_te", bst["wbuf_te"])], prevP)
        emit_wq(1, [], prevP)
        emit_wo(0, P)
        emit_wo(1, P)


        Ksb = qkv[:, 0:2048]
        Qsb = qkv[:, 2048:4096]
        Vsb = qkv[:, 4096:6144]
        vfm = fscr[:, 0:2048]

        for h in range(H):
            par = h % 2
            wq_s = wbuf[:, (par * 3 + 0) * 2048:(par * 3 + 0) * 2048 + 2048]
            wk_s = wbuf[:, (par * 3 + 1) * 2048:(par * 3 + 1) * 2048 + 2048]
            wv_s = wbuf[:, (par * 3 + 2) * 2048:(par * 3 + 2) * 2048 + 2048]

            wts = [("s_pl", wq_pl[h]), ("s_sp", sp_xlnt)]
            if h > 0:
                wts += [("s_act", act_head_end), ("s_dve", dve_z_end[h - 1])]
            for sti in range(4):
                for dc in range(ND):
                    last = (sti == 3 and dc == ND - 1)
                    P.emit("te", lambda e, sti=sti, dc=dc, w=wk_s: e.matmul(
                        PS_A[:, sti * 512:(sti + 1) * 512],
                        w[:, dc * 128:(dc + 1) * 128],
                        big1[:, dc * S + sti * 512: dc * S + (sti + 1) * 512],
                        start=(dc == 0), stop=(dc == ND - 1)),
                        waits=wts if (sti == 0 and dc == 0) else (),
                        inc="s_te" if last else None)
            te_k = P.v("s_te")
            if variant == "a":
                P.emit("dve", lambda e, h=h: e.tensor_scalar(
                    Ksb, PS_A[:, :], bqkv[:, 16 + h:16 + h + 1], None, OP.add),
                    waits=[("s_te", te_k)], inc="s_dve")
            else:
                P.emit("dve", lambda e: e.tensor_copy(Ksb, PS_A[:, :]),
                       waits=[("s_te", te_k)], inc="s_dve")
            dve_k = P.v("s_dve")
            emit_gamma(P)
            qwts = ([("s_dve", dve_z_end[h - 1]), ("s_act", act_head_end)]
                    if h > 0 else [])
            for sti in range(4):
                for dc in range(ND):
                    last = (sti == 3 and dc == ND - 1)
                    P.emit("te", lambda e, sti=sti, dc=dc, w=wq_s: e.matmul(
                        PS_B[:, sti * 512:(sti + 1) * 512],
                        w[:, dc * 128:(dc + 1) * 128],
                        big1[:, dc * S + sti * 512: dc * S + (sti + 1) * 512],
                        start=(dc == 0), stop=(dc == ND - 1)),
                        waits=qwts if (sti == 0 and dc == 0) else (),
                        inc="s_te" if last else None)
            te_q = P.v("s_te")
            if variant == "a":
                P.emit("dve", lambda e, h=h: e.tensor_scalar(
                    Qsb, PS_B[:, :], bqkv[:, h:h + 1], INV_SCALE, OP.add, OP.mult),
                    waits=[("s_te", te_q)], inc="s_dve")
            else:
                P.emit("dve", lambda e: e.tensor_scalar(
                    Qsb, PS_B[:, :], INV_SCALE, None, OP.mult),
                    waits=[("s_te", te_q)], inc="s_dve")
            dve_q = P.v("s_dve")
            emit_gamma(P)
            for sti in range(4):
                for dc in range(ND):
                    last = (sti == 3 and dc == ND - 1)
                    P.emit("te", lambda e, sti=sti, dc=dc, w=wv_s: e.matmul(
                        PS_A[:, sti * 512:(sti + 1) * 512],
                        w[:, dc * 128:(dc + 1) * 128],
                        big1[:, dc * S + sti * 512: dc * S + (sti + 1) * 512],
                        start=(dc == 0), stop=(dc == ND - 1)),
                        waits=[("s_dve", dve_k)] if (sti == 0 and dc == 0) else (),
                        inc="s_te" if last else None)
            te_v = P.v("s_te")
            P.emit("dve", lambda e: e.tensor_copy(vfm, PS_A[:, :]),
                   waits=[("s_te", te_v)], inc="s_dve")
            dve_vf = P.v("s_dve")
            for tt in range(NT):
                P.emit("te", lambda e, tt=tt: e.matmul(
                    PS_B[:, tt * 128:(tt + 1) * 128],
                    vfm[:, tt * 128:(tt + 1) * 128], I_sb[:],
                    start=True, stop=True),
                    waits=[("s_dve", dve_q), ("s_dve", dve_vf)] if tt == 0 else (),
                    inc="s_te" if tt == NT - 1 else None)
            te_vt = P.v("s_te")
            P.emit("dve", lambda e: e.tensor_copy(Vsb, PS_B[:, :]),
                   waits=[("s_te", te_vt)], inc="s_dve")
            dve_vt = P.v("s_dve")

            act_exp_of = [0] * NT
            dve_nm_of = [0] * NT
            te_at_of = [0] * NT
            te_z_of = [0] * NT
            dve_diag_of = [0] * NT

            def kts_of(qt):
                if variant == "a":
                    return list(range(qt + 1))
                return [k for k in (qt - 1, qt, qt + 1) if 0 <= k < NT]

            def base_of(qt):
                par = qt % 2
                if variant == "a":
                    return (par * 1024) if qt < 8 else 0
                return par * 512

            def abuf_of(qt):
                par = qt % 2
                if variant == "a" and qt >= 8:
                    return (anx, 0) if par == 0 else (fscr, 0)
                return (anx, base_of(qt))

            def stage1(qt):
                par = qt % 2
                kts = kts_of(qt)
                kw = len(kts) * 128
                base = base_of(qt)
                dslot = 2048 + par * 128
                if variant == "a" and qt >= 8:
                    scw = [("s_act", act_exp_of[qt - 1])]
                elif qt >= 2:
                    scw = [("s_act", act_exp_of[qt - 2])]
                else:
                    scw = []
                if qt <= 1:
                    scw = scw + [("s_dve", dve_vt)]
                if variant == "a":
                    nfull = (qt * 128) // 512
                    for j in range(nfull):
                        P.emit("te", lambda e, qt=qt, j=j, base=base: e.matmul(
                            PS_A[:, base + j * 512:base + (j + 1) * 512],
                            Qsb[:, qt * 128:(qt + 1) * 128],
                            Ksb[:, j * 512:(j + 1) * 512],
                            start=True, stop=True),
                            waits=scw if j == 0 else ())
                    rem0 = nfull * 512
                    remw = kw - rem0
                    P.emit("te", lambda e, qt=qt, rem0=rem0, remw=remw, base=base: e.matmul(
                        PS_A[:, base + rem0:base + rem0 + remw],
                        Qsb[:, qt * 128:(qt + 1) * 128],
                        Ksb[:, rem0:rem0 + remw],
                        start=True, stop=True),
                        waits=scw if nfull == 0 else ())
                    P.emit("te", lambda e, qt=qt, base=base: e.matmul(
                        PS_A[:, base + qt * 128:base + (qt + 1) * 128],
                        shA[:], triB[:],
                        start=False, stop=True, skip_group_check=True),
                        inc="s_te")
                else:
                    k0 = kts[0] * 128
                    P.emit("te", lambda e, qt=qt, k0=k0, kw=kw, base=base: e.matmul(
                        PS_A[:, base:base + kw],
                        Qsb[:, qt * 128:(qt + 1) * 128], Ksb[:, k0:k0 + kw],
                        start=True, stop=True), waits=scw)
                    nmask = sum(1 for kt in kts if kt in (qt - 1, qt + 1))
                    mdone = 0
                    for i, kt in enumerate(kts):
                        if kt == qt - 1:
                            uu, vv = uL, vL
                        elif kt == qt + 1:
                            uu, vv = uR, vR
                        else:
                            continue
                        mdone += 1
                        P.emit("te", lambda e, i=i, uu=uu, vv=vv, base=base: e.matmul(
                            PS_A[:, base + i * 128:base + (i + 1) * 128],
                            uu[:], vv[:],
                            start=False, stop=True, skip_group_check=True),
                            inc="s_te" if mdone == nmask else None)
                te_sc = P.v("s_te")
                P.emit("dve", lambda e, kw=kw, base=base: e.reduce_max(
                    st(NEGMAX), PS_A[:, base:base + kw], axis=AX.X, negate=True),
                    waits=[("s_te", te_sc)], inc="s_dve")
                dve_nm_of[qt] = P.v("s_dve")

            def stage1b(qt):
                par = qt % 2
                kts = kts_of(qt)
                kw = len(kts) * 128
                base = base_of(qt)
                dslot = 2048 + par * 128
                dve_nm = dve_nm_of[qt]
                anx_war = te_at_of[qt - 2] if qt >= 2 else 0
                ab, abase = abuf_of(qt)
                P.emit("act", lambda e, kw=kw, base=base, ab=ab, abase=abase: e.activation(
                    ab[:, abase:abase + kw], PS_A[:, base:base + kw], AF.Exp,
                    bias=st(NEGMAX), accum_out=st(SUMS)),
                    waits=[("s_dve", dve_nm), ("s_te", anx_war)],
                    inc="s_act")
                act_exp_of[qt] = P.v("s_act")
                P.emit("dve", lambda e: e.reciprocal(st(RECIP), st(SUMS)),
                       waits=[("s_act", act_exp_of[qt])])
                P.emit("dve", lambda e, dslot=dslot: e.tensor_scalar(
                    fscr[:, dslot:dslot + 128], I_sb[:], st(RECIP), None, OP.mult),
                    waits=[("s_te", anx_war)], inc="s_dve")
                dve_diag_of[qt] = P.v("s_dve")

            def stage2m(qt):
                par = qt % 2
                kts = kts_of(qt)
                kw = len(kts) * 128
                base = base_of(qt)
                base_t = par * 512
                base_e = 2048 + par * 512
                dslot = 2048 + par * 128
                zslot = 1024 + par * 128
                n_k = len(kts)
                dve_attnT = 0
                prev_ev = None
                for g0 in range(0, n_k, 4):
                    gkts = kts[g0:g0 + 4]
                    ng = len(gkts)
                    for i in range(ng):
                        ab, abase = abuf_of(qt)
                        w0 = [("s_dve", dve_diag_of[qt])]
                        if prev_ev is not None:
                            w0.append(prev_ev)
                        P.emit("te", lambda e, i=i, g0=g0, ab=ab, abase=abase, base_t=base_t, dslot=dslot: e.matmul(
                            PS_B[:, base_t + i * 128:base_t + (i + 1) * 128],
                            ab[:, abase + (g0 + i) * 128:abase + (g0 + i + 1) * 128],
                            fscr[:, dslot:dslot + 128], start=True, stop=True),
                            waits=(w0 if i == 0 else ()),
                            inc="s_te" if i == ng - 1 else None)
                    te_at = P.v("s_te")
                    if (qt + g0 // 4) % 2 == 0:
                        P.emit("dve", lambda e, gw=ng * 128, base_t=base_t, base_e=base_e: e.tensor_copy(
                            anx[:, base_e:base_e + gw], PS_B[:, base_t:base_t + gw]),
                            waits=[("s_te", te_at)], inc="s_dve")
                        ev_sem, ev_val = "s_dve", P.v("s_dve")
                    else:
                        P.emit("act", lambda e, gw=ng * 128, base_t=base_t, base_e=base_e: e.activation(
                            anx[:, base_e:base_e + gw], PS_B[:, base_t:base_t + gw],
                            AF.Identity),
                            waits=[("s_te", te_at)], inc="s_act")
                        ev_sem, ev_val = "s_act", P.v("s_act")
                    prev_ev = (ev_sem, ev_val)
                    for i, kt in enumerate(gkts):
                        glast = (g0 + i == n_k - 1)
                        gfirst = (g0 == 0 and i == 0)
                        P.emit("te", lambda e, i=i, kt=kt, zs=zslot, be=base_e, st_=gfirst, sp_=glast: e.matmul(
                            PS_B[:, zs:zs + 128],
                            Vsb[:, kt * 128:(kt + 1) * 128],
                            anx[:, be + i * 128:be + (i + 1) * 128],
                            start=st_, stop=sp_),
                            waits=([(ev_sem, ev_val)] if i == 0 else ()),
                            inc="s_te" if glast else None)
                te_at_of[qt] = te_at
                te_z_of[qt] = P.v("s_te")

            def stage2z(qt):
                par = qt % 2
                zslot = 1024 + par * 128
                te_z = te_z_of[qt]
                zdst = xz[:, (h % 2) * 2048 + qt * 128:(h % 2) * 2048 + (qt + 1) * 128]
                zwar = [("s_st", st_z_end[h - 2])] if h >= 2 else []
                if variant == "a":
                    P.emit("act", lambda e, h=h, zd=zdst, zs=zslot: e.activation(
                        zd, PS_B[:, zs:zs + 128], AF.Identity,
                        bias=bqkv[:, 32 + h:32 + h + 1]),
                        waits=[("s_te", te_z)] + zwar, inc="s_act")
                else:
                    P.emit("act", lambda e, zd=zdst, zs=zslot: e.activation(
                        zd, PS_B[:, zs:zs + 128], AF.Identity),
                        waits=[("s_te", te_z)] + zwar, inc="s_act")
                act_zev = P.v("s_act")
                P.emit("pl", lambda e, qt=qt, h=h, zd=zdst: e.dma_start(
                    d_z.ap()[qt, h], zd), waits=[("s_act", act_zev)],
                    inc="s_st", incn=16)
                if (h * NT + qt) % 2 == 0:
                    emit_gamma(P)

            stage1(0)
            stage1b(0)
            for qt in range(1, NT):
                stage1(qt)
                stage2m(qt - 1)
                stage1b(qt)
                stage2z(qt - 1)
            stage2m(NT - 1)
            stage2z(NT - 1)
            act_head_end = P.v("s_act")
            dve_z_end.append(P.v("s_dve"))
            st_z_end.append(P.v("s_st"))
            te_head_end.append(P.v("s_te"))
            if h + 2 < H:
                emit_wcasts(h + 2, [("s_te", te_head_end[h])])

        while do_gamma and gam_i[0] < len(gam):
            emit_gamma(P)
        if do_gamma:
            dve_g = P.v("s_dve")
            P.emit("te", lambda e: e.matmul(
                PS_B[0:1, 0:3], onesC[:], gacc[:, 0:3], start=True, stop=True),
                waits=[("s_dve", dve_g), ("s_dve", dve_z_end[-1])], inc="s_te")
            te_g = P.v("s_te")
            P.emit("dve", lambda e: e.tensor_scalar(
                stats[0:1, 32:35], PS_B[0:1, 0:3], 0.5 / NWELEM, 0.5e-8,
                OP.mult, OP.add), waits=[("s_te", te_g)], inc="s_dve")
            dve_t = P.v("s_dve")
            P.emit("te", lambda e: e.matmul(
                PS_B[:, 16:19], onesRF[:], stats[0:1, 32:35],
                start=True, stop=True), waits=[("s_dve", dve_t)], inc="s_te")
            te_t = P.v("s_te")
            P.emit("dve", lambda e: e.tensor_copy(thrbc[:, 0:3], PS_B[:, 16:19]),
                   waits=[("s_te", te_t)])
            P.emit("dve", lambda e: e.tensor_scalar(
                thrbc[:, 4:7], thrbc[:, 0:3], -1.0, None, OP.mult), inc="s_dve")
        bst["big1_te"] = te_head_end[-1]
        bst["wbuf_te"] = te_head_end[-1]

        P = NP()
        pl_bo = 0
        if variant == "a":
            P.emit("pl", lambda e: e.dma_start(bosb, d_bo.ap()),
                   inc="s_pl", incn=16)
            pl_bo = P.v("s_pl")
        te_end = [0] * NT
        dve_add_l = [0] * NT
        dve_ln_l = [0] * NT
        st_q = [0] * NT
        for qt in range(NT):
            par = qt % 2
            zt = qkv[:, par * 2048:(par + 1) * 2048]
            xa = xrow[:, par * 2048:(par + 1) * 2048]
            wz = [("s_te", te_end[qt - 2])] if qt >= 2 else []
            P.emit("sp", lambda e, qt=qt, d=zt: e.dma_start(
                d.rearrange("p (hh e) -> p hh e", hh=H),
                d_z.ap()[qt].rearrange("hh p e -> p hh e")),
                waits=wz, inc="s_sp", incn=16)
            wx = []
            if qt >= 2:
                wx = [("s_st", st_q[qt - 2]), ("s_dve", dve_ln_l[qt - 2])]
            P.emit("sp", lambda e, qt=qt, d=xa: e.dma_start(
                d, resid_src.ap()[qt * 128:(qt + 1) * 128, :]),
                waits=wx, inc="s_sp", incn=16)
            sp_q = P.v("s_sp")
            PS = PS_A if par == 0 else PS_B
            wts0 = [("s_sp", sp_q), ("s_pl", wo_pl[-1])]
            if qt >= 2:
                wts0.append(("s_dve", dve_add_l[qt - 2]))
            lastb = (variant != "a")
            for hh in range(H):
                for j in range(4):
                    last = (hh == H - 1 and j == 3)
                    P.emit("te", lambda e, hh=hh, j=j, z=zt, PS=PS: e.matmul(
                        PS[:, j * 512:(j + 1) * 512],
                        z[:, hh * 128:(hh + 1) * 128],
                        big2[:, hh * Dm + j * 512: hh * Dm + (j + 1) * 512],
                        start=(hh == 0), stop=(hh == H - 1 and lastb)),
                        waits=wts0 if (hh == 0 and j == 0) else (),
                        inc="s_te" if (last and lastb) else None)
            if variant == "a":
                for j in range(4):
                    P.emit("te", lambda e, j=j, PS=PS: e.matmul(
                        PS[:, j * 512:(j + 1) * 512], ones1[:],
                        bosb[0:1, j * 512:(j + 1) * 512],
                        start=False, stop=True, skip_group_check=True),
                        waits=[("s_pl", pl_bo)] if j == 0 else (),
                        inc="s_te" if j == 3 else None)
            te_end[qt] = P.v("s_te")
            P.emit("dve", lambda e, xa=xa, PS=PS: e.tensor_add(xa, xa, PS[:, :]),
                   waits=[("s_te", te_end[qt])], inc="s_dve")
            dve_add_l[qt] = P.v("s_dve")
            P.emit("pl", lambda e, qt=qt, xa=xa, dst=out_dst: e.dma_start(
                dst.ap()[qt * 128:(qt + 1) * 128, :], xa),
                waits=[("s_dve", dve_add_l[qt])], inc="s_st", incn=16)
            if do_ln_next:
                xo = xlnst[:, par * 2048:(par + 1) * 2048]
                war = [("s_st", st_q[qt - 2])] if qt >= 2 else []
                dv = ln_chain(P, xa, xo, par,
                              wait_first=[("s_dve", dve_add_l[qt])],
                              war_apply=war)
                dve_ln_l[qt] = dv
                P.emit("pl", lambda e, qt=qt, o=xo: e.dma_start(
                    d_xln.ap()[qt * 128:(qt + 1) * 128, :], o),
                    waits=[("s_dve", dv)], inc="s_st", incn=16)
            else:
                dve_ln_l[qt] = dve_add_l[qt]
            st_q[qt] = P.v("s_st")
        bst["xln_st"] = P.v("s_st")

    def ffn_sublayer(first, resid_src, out_dst, do_ln_next):
        SUP = 4
        nsup = NHF // SUP
        emit_xlnt_treads()
        P = NP()
        sp_xlnt = bst["sp_xlnt"]

        qstate = {"par": 0, "dq": {0: 0, 1: 0, 2: 0},
                  "qpar": 0, "stq": {0: 0, 1: 0}}

        def quantize_chunk(P, src_ap, dst_ap, mi, width, q="pl"):
            par_f = qstate["par"]
            qstate["par"] = (par_f + 1) % 3
            fa = wf32[:, par_f * 2048:par_f * 2048 + width]
            P.emit("pl", lambda e, d=fa, a=src_ap: e.dma_start(d, a),
                   waits=[("s_dve", qstate["dq"][par_f])], inc="s_q", incn=16)
            sp_w = P.v("s_q")
            pbuf = fscr[:, 2048:2048 + width]
            nbuf = xlnst[:, 0:width]
            qp = qstate["qpar"]
            qstate["qpar"] ^= 1
            qbuf = qkv[:, 4096:4096 + width] if qp == 0 else anx[:, 0:width]
            P.emit("dve", lambda e, f=fa, p=pbuf, mi=mi: e.tensor_single_scalar(
                p, f, thrbc[:, mi:mi + 1], OP.is_gt),
                waits=[("s_q", sp_w)])
            P.emit("dve", lambda e, f=fa, n=nbuf, mi=mi: e.tensor_single_scalar(
                n, f, thrbc[:, 4 + mi:4 + mi + 1], OP.is_lt))
            P.emit("dve", lambda e, p=pbuf, n=nbuf, q=qbuf: e.tensor_sub(q, p, n),
                   waits=[("s_sq", qstate["stq"][qp])], inc="s_dve")
            dq = P.v("s_dve")
            qstate["dq"][par_f] = dq
            P.emit("sp", lambda e, d=dst_ap, q=qbuf: e.dma_start(d, q),
                   waits=[("s_dve", dq)], inc="s_sq", incn=16)
            qstate["stq"][qp] = P.v("s_sq")
            return qstate["stq"][qp]

        w3chunks = []
        for dc in range(ND):
            for c0, w in ((0, 2048), (2048, 2048), (4096, 1536)):
                w3chunks.append((dc, c0, w))
        w3_i = [0]

        def emit_w3_quant(P):
            if not first or w3_i[0] >= len(w3chunks):
                return
            dc, c0, w = w3chunks[w3_i[0]]
            quantize_chunk(P, d_w3.ap()[dc * 128:(dc + 1) * 128, c0:c0 + w],
                           d_w3qc.ap()[dc * 128:(dc + 1) * 128, c0:c0 + w], 2, w,
                           q="sp")
            w3_i[0] += 1

        st_after_q = [0] * nsup

        def quantize_super(s_):
            for hf in range(s_ * SUP, (s_ + 1) * SUP):
                quantize_chunk(P, d_w1.ap()[hf * 128:(hf + 1) * 128, :],
                               d_w1qc.ap()[hf * 128:(hf + 1) * 128, :], 0, 2048)
                quantize_chunk(P, d_w2.ap()[hf * 128:(hf + 1) * 128, :],
                               d_w2qc.ap()[hf * 128:(hf + 1) * 128, :], 1, 2048)
            st_after_q[s_] = P.v("s_sq")

        if first and FFN_PART >= 1:
            quantize_super(0)
            quantize_super(1)

        sp_tread = [0] * nsup
        te_super_done = [0] * nsup
        act_sig = 0
        dve_hid = 0
        dve_hid_of = [0] * NHF
        st_hid = [0] * NHF

        def emit_treads(s_):
            war = [("s_te", te_super_done[s_ - 2])] if s_ >= 2 else []
            if first:
                war.append(("s_sq", st_after_q[s_]))
            par = s_ % 2
            base = par * 16384
            for mi, qc in ((0, d_w1qc), (1, d_w2qc)):
                for dc in range(ND):
                    P.emit("sp", lambda e, qc=qc, s_=s_, dc=dc, mi=mi, base=base: e.dma_start_transpose(
                        big2[:, base + mi * 8192 + dc * 512: base + mi * 8192 + dc * 512 + 512],
                        qc.ap()[s_ * 512:(s_ + 1) * 512, dc * 128:(dc + 1) * 128]),
                        waits=war if (mi == 0 and dc == 0) else (),
                        inc="s_sp", incn=16)
            sp_tread[s_] = P.v("s_sp")

        if FFN_PART >= 2:
            emit_treads(0)
        for s_ in range(nsup if FFN_PART >= 2 else 0):
            par = s_ % 2
            base = par * 16384
            if first and FFN_PART >= 1 and s_ + 2 < nsup:
                quantize_super(s_ + 2)
            if s_ + 1 < nsup:
                emit_treads(s_ + 1)
            for ii in range(SUP):
                hf = s_ * SUP + ii
                for mi, PS in ((0, PS_A), (1, PS_B)):
                    wts = [("s_sp", sp_tread[s_]), ("s_sp", sp_xlnt)]
                    if hf > 0:
                        wts.append(("s_act", act_sig) if mi == 0 else
                                   ("s_dve", dve_hid))
                    for sti in range(4):
                        for dc in range(ND):
                            last = (sti == 3 and dc == ND - 1)
                            P.emit("te", lambda e, mi=mi, PS=PS, sti=sti, dc=dc, base=base, ii=ii: e.matmul(
                                PS[:, sti * 512:(sti + 1) * 512],
                                big2[:, base + mi * 8192 + dc * 512 + ii * 128: base + mi * 8192 + dc * 512 + ii * 128 + 128],
                                big1[:, dc * S + sti * 512: dc * S + (sti + 1) * 512],
                                start=(dc == 0), stop=(dc == ND - 1)),
                                waits=wts if (sti == 0 and dc == 0) else (),
                                inc="s_te" if last else None)
                te_hf = P.v("s_te")
                hpar = hf % 2
                s1 = qkv[:, 0:2048] if hpar == 0 else fscr[:, 0:2048]
                hidb = qkv[:, 2048:4096] if hpar == 0 else xlnst[:, 2048:4096]
                sig_war = dve_hid_of[hf - 2] if hf >= 2 else 0
                P.emit("act", lambda e, s1=s1: e.activation(s1, PS_A[:, :], AF.Sigmoid),
                       waits=[("s_te", te_hf - 1), ("s_dve", sig_war)],
                       inc="s_act")
                act_sig = P.v("s_act")
                hwar = [("s_st", st_hid[hf - 2])] if hf >= 2 else []
                P.emit("dve", lambda e, s1=s1, hidb=hidb: e.tensor_mul(hidb, s1, PS_B[:, :]),
                       waits=[("s_act", act_sig), ("s_te", te_hf)] + hwar,
                       inc="s_dve")
                dve_hid = P.v("s_dve")
                dve_hid_of[hf] = dve_hid
                P.emit("pl", lambda e, hf=hf, hidb=hidb: e.dma_start(
                    d_hid.ap()[hf].rearrange("st p f -> p st f"),
                    hidb.rearrange("p (st f) -> p st f", st=4)),
                    waits=[("s_dve", dve_hid)], inc="s_st", incn=16)
                st_hid[hf] = P.v("s_st")
                emit_w3_quant(P)
            te_super_done[s_] = P.v("s_te")
        while first and FFN_PART >= 1 and w3_i[0] < len(w3chunks):
            emit_w3_quant(P)
        st_w3_done = P.v("s_st")
        st_hid_done = st_hid[-1]

        GRP = 6
        ngrp = (NHF + GRP - 1) // GRP

        def emit_group_loads(Pl, half, g, war):
            lo = g * GRP
            hi = min(NHF, lo + GRP)
            n = hi - lo
            gpar = g % 2
            for i in range(n):
                hf = lo + i
                Pl.emit("sp", lambda e, hf=hf, i=i, gpar=gpar: e.dma_start_transpose(
                    big1[:, gpar * 12288 + i * 2048: gpar * 12288 + (i + 1) * 2048],
                    d_w3qc.ap()[:, hf * 128:(hf + 1) * 128]),
                    waits=war if i == 0 else (), inc="s_sp", incn=16)
            for i in range(n):
                hf = lo + i
                Pl.emit("sp", lambda e, hf=hf, i=i, gpar=gpar, half=half: e.dma_start(
                    wbuf[:, gpar * 6144 + i * 1024: gpar * 6144 + (i + 1) * 1024].rearrange("p (st f) -> p st f", st=2),
                    d_hid.ap()[hf][half * 2:half * 2 + 2].rearrange("st p f -> p st f")),
                    inc="s_sp", incn=16)
            return Pl.v("s_sp")

        pref = {}
        if FFN_PART >= 21:
            pref[(0, 0)] = emit_group_loads(
                P, 0, 0, [("s_te", te_super_done[-1] if FFN_PART >= 2 else 0),
                          ("s_st", P.v("s_st")), ("s_sq", P.v("s_sq")),
                          ("s_dve", P.v("s_dve"))])
        for half in range(2 if FFN_PART >= 21 else 0):
            prev_plan = plans[-1]
            P = NP()
            te_grp_done = [0] * ngrp
            dve_acc = [0] * 8
            last_ps = {0: 0, 1: 0}
            for g in range(ngrp):
                lo = g * GRP
                hi = min(NHF, lo + GRP)
                n = hi - lo
                gpar = g % 2
                if (half, g) in pref:
                    sp_g = pref[(half, g)]
                else:
                    war = [("s_te", te_grp_done[g - 2])] if g >= 2 else []
                    sp_g = emit_group_loads(P, half, g, war)
                for sc in range(8 if FFN_PART >= 25 else 0):
                    PS = PS_A if sc % 2 == 0 else PS_B
                    for i in range(n):
                        for j in range(4):
                            last = (i == n - 1 and j == 3)
                            P.emit("te", lambda e, i=i, j=j, sc=sc, gpar=gpar, PS=PS, n=n: e.matmul(
                                PS[:, j * 512:(j + 1) * 512],
                                wbuf[:, gpar * 6144 + i * 1024 + sc * 128: gpar * 6144 + i * 1024 + sc * 128 + 128],
                                big1[:, gpar * 12288 + i * 2048 + j * 512: gpar * 12288 + i * 2048 + (j + 1) * 512],
                                start=(i == 0), stop=(i == n - 1)),
                                waits=([("s_sp", sp_g), ("s_dve", last_ps[sc % 2])]
                                       if (i == 0 and j == 0) else ()),
                                inc="s_te" if last else None)
                    te_sc = P.v("s_te")
                    acc = accv[:, sc * 2048:(sc + 1) * 2048]
                    if g == 0:
                        P.emit("dve", lambda e, acc=acc, PS=PS: e.tensor_copy(
                            acc, PS[:, :]), waits=[("s_te", te_sc)], inc="s_dve")
                    else:
                        P.emit("dve", lambda e, acc=acc, PS=PS: e.tensor_add(
                            acc, acc, PS[:, :]), waits=[("s_te", te_sc)], inc="s_dve")
                    dve_acc[sc] = P.v("s_dve")
                    last_ps[sc % 2] = dve_acc[sc]
                te_grp_done[g] = P.v("s_te")
            st_sc = [0] * 8
            dve_ln_sc = [0] * 8
            for sc in range(8 if FFN_PART >= 30 else 0):
                t = half * 8 + sc
                par = sc % 2
                xa = xrow[:, par * 2048:(par + 1) * 2048]
                wx = []
                if sc >= 2:
                    wx = [("s_st", st_sc[sc - 2]), ("s_dve", dve_ln_sc[sc - 2])]
                P.emit("sp", lambda e, t=t, d=xa: e.dma_start(
                    d, resid_src.ap()[t * 128:(t + 1) * 128, :]),
                    waits=wx, inc="s_sp", incn=16)
                sp_x = P.v("s_sp")
                acc = accv[:, sc * 2048:(sc + 1) * 2048]
                P.emit("dve", lambda e, xa=xa, acc=acc: e.tensor_add(xa, xa, acc),
                       waits=[("s_sp", sp_x), ("s_dve", dve_acc[sc])], inc="s_dve")
                dve_add = P.v("s_dve")
                P.emit("pl", lambda e, t=t, xa=xa, dst=out_dst: e.dma_start(
                    dst.ap()[t * 128:(t + 1) * 128, :], xa),
                    waits=[("s_dve", dve_add)], inc="s_st", incn=16)
                if do_ln_next:
                    xo = xlnst[:, par * 2048:(par + 1) * 2048]
                    war = [("s_st", st_sc[sc - 2])] if sc >= 2 else []
                    dv = ln_chain(P, xa, xo, par,
                                  wait_first=[("s_dve", dve_add)],
                                  war_apply=war)
                    dve_ln_sc[sc] = dv
                    P.emit("pl", lambda e, t=t, o=xo: e.dma_start(
                        d_xln.ap()[t * 128:(t + 1) * 128, :], o),
                        waits=[("s_dve", dv)], inc="s_st", incn=16)
                else:
                    dve_ln_sc[sc] = dve_add
                st_sc[sc] = P.v("s_st")
            if half == 0 and FFN_PART >= 21:
                pref[(1, 0)] = emit_group_loads(
                    P, 1, 0, [("s_te", te_grp_done[ngrp - 2])])
            if half == 1:
                bst["xln_st"] = P.v("s_st")
                bst["big1_te"] = te_grp_done[ngrp - 1]
                bst["wbuf_te"] = te_grp_done[ngrp - 1]

    attn_sublayer("a", d_x, d_xres if stage > 1 else d_y,
                  do_ln_next=(stage > 1), do_gamma=(stage > 1))
    if stage > 1:
        ffn_sublayer(True, d_xres, d_xres if stage > 2 else d_y,
                     do_ln_next=(stage > 2))
    if stage > 2:
        attn_sublayer("b", d_xres, d_xres if stage > 3 else d_y,
                      do_ln_next=(stage > 3), do_gamma=False)
    if stage > 3:
        ffn_sublayer(False, d_xres, d_y, do_ln_next=False)

    P = NP()
    total_st = cnt.get("s_st", 0)
    P.emit("pl", lambda e: e.engine_nop(), waits=[("s_st", total_st)])

    _run_blocks(nc, plans, sems)
    return nc


def make_consts():
    cI = np.eye(128, dtype=np.float32)
    cShift = np.eye(128, k=-1, dtype=np.float32)
    cTri = np.triu(np.ones((128, 128), np.float32)) * NEG_CAUSAL
    uLr = np.zeros((1, 128), np.float32); uLr[0, 64:] = 1.0
    vLr = np.zeros((1, 128), np.float32); vLr[0, :64] = NEG_BAND
    uRr = np.zeros((1, 128), np.float32); uRr[0, :64] = 1.0
    vRr = np.zeros((1, 128), np.float32); vRr[0, 64:] = NEG_BAND
    b = lambda a: np.ascontiguousarray(a.astype(BF))
    return {
        "cI": b(cI), "cShift": b(cShift), "cTri": b(cTri),
        "cuL": b(uLr), "cvL": b(vLr), "cuR": b(uRr), "cvR": b(vRr),
        "cOnes1": b(np.ones((1, 128), np.float32)),
        "cOnesC": np.ones((128, 1), np.float32),
        "cOnesRF": np.ones((1, 128), np.float32),
    }


def kernel(L0_attn_a_x, L0_attn_a_W_Q, L0_attn_a_W_K, L0_attn_a_W_V,
           L0_attn_a_W_O, L0_attn_a_b_Q, L0_attn_a_b_K, L0_attn_a_b_V,
           L0_attn_a_b_O, L0_attn_b_Wq, L0_attn_b_Wk, L0_attn_b_Wv,
           L0_attn_b_Wo, L0_ffn_w1, L0_ffn_w2, L0_ffn_w3):
    consts = make_consts()
    shared = {
        "wqA": np.ascontiguousarray(L0_attn_a_W_Q, np.float32),
        "wkA": np.ascontiguousarray(L0_attn_a_W_K, np.float32),
        "wvA": np.ascontiguousarray(L0_attn_a_W_V, np.float32),
        "woA": np.ascontiguousarray(L0_attn_a_W_O, np.float32),
        "bq": np.ascontiguousarray(L0_attn_a_b_Q, np.float32),
        "bk": np.ascontiguousarray(L0_attn_a_b_K, np.float32),
        "bv": np.ascontiguousarray(L0_attn_a_b_V, np.float32),
        "bo": np.ascontiguousarray(L0_attn_a_b_O, np.float32).reshape(1, Dm),
        "wqB": np.ascontiguousarray(L0_attn_b_Wq, np.float32),
        "wkB": np.ascontiguousarray(L0_attn_b_Wk, np.float32),
        "wvB": np.ascontiguousarray(L0_attn_b_Wv, np.float32),
        "woB": np.ascontiguousarray(L0_attn_b_Wo, np.float32),
        "w1": np.ascontiguousarray(L0_ffn_w1, np.float32),
        "w2": np.ascontiguousarray(L0_ffn_w2, np.float32),
        "w3": np.ascontiguousarray(L0_ffn_w3, np.float32),
        **consts,
    }
    x = np.ascontiguousarray(L0_attn_a_x, np.float32)
    in_maps = [dict(shared, x=x[c]) for c in range(8)]
    nc = build(STAGE)
    res = run_bass_kernel_spmd(nc, in_maps, core_ids=list(range(8)))
    return np.stack([np.asarray(res.results[c]["y"], np.float32)
                     for c in range(8)], axis=0)
